# version 02
# baseline (speedup 1.0000x reference)
import numpy as np
import ml_dtypes
from contextlib import ExitStack
import concourse.bass as bass
import concourse.mybir as mybir
from concourse.bass_utils import run_bass_kernel_spmd

F32 = mybir.dt.float32
BF16 = mybir.dt.bfloat16
U32 = mybir.dt.uint32
I32 = mybir.dt.int32
AF = mybir.ActivationFunctionType
ALU = mybir.AluOpType
AX = mybir.AxisListType
BF = ml_dtypes.bfloat16
ts = bass.ts
ds = bass.ds

T = 8192
D = 2048
HD = 128
SSEL = 1024
IMLP = 8192
NEG = -1.0e30
RTH = 10000.0
ISQ = float(1.0 / np.sqrt(np.float32(128.0)))
NCORE = 8
TWOPI = float(2.0 * np.pi)

_CACHE = {}


def _cached(key, fn):
    if key not in _CACHE:
        _CACHE[key] = fn()
    return _CACHE[key]


def build_ln(NT=1024):
    nc = bass.Bass("TRN2", target_bir_lowering=False)
    nt = NT // 128
    xs = nc.dram_tensor("xs", [NT, D], F32, kind="ExternalInput").ap()
    wr = nc.dram_tensor("wr", [1, D], F32, kind="ExternalInput").ap()
    onesf = nc.dram_tensor("onesf", [1, 128], F32, kind="ExternalInput").ap()
    identb = nc.dram_tensor("identb", [128, 128], BF16, kind="ExternalInput").ap()
    lntok = nc.dram_tensor("lntok", [NT, D], BF16, kind="ExternalOutput").ap()
    lnT = nc.dram_tensor("lnT", [16 * 128, NT], BF16, kind="ExternalOutput").ap()
    sco = nc.dram_tensor("sco", [128, nt], F32, kind="ExternalOutput").ap()
    es = ExitStack()
    sb = lambda n, shp, dt: es.enter_context(nc.sbuf_tensor(n, shp, dt))
    ps = lambda n, shp, dt=F32: es.enter_context(nc.psum_tensor(n, shp, dt))
    sem = lambda n: es.enter_context(nc.semaphore(n))
    with es:
        xt = sb("xt", [128, D], F32)
        xt2 = sb("xt2", [128, D], F32)
        tok = sb("tok", [128, D], BF16)
        tok2 = sb("tok2", [128, D], BF16)
        lnb = sb("lnb", [128, 2048], BF16)
        lnb2 = sb("lnb2", [128, 2048], BF16)
        sm = sb("sm", [128, 64], F32)
        wrbc = sb("wrbc", [128, D], F32)
        wrsum = sb("wrsum", [128, 1], F32)
        wrrow = sb("wrrow", [1, D], F32)
        onesfs = sb("onesfs", [1, 128], F32)
        idents = sb("idents", [128, 128], BF16)
        sct = sb("sct", [128, D], F32)
        psa = ps("psa", [128, 512])
        ptr = ps("ptr", [128, 1024], BF16)
        ptr2 = ps("ptr2", [128, 1024], BF16)
        slc = sem("slc"); sl0 = sem("sl0"); sl1 = sem("sl1")
        sv = sem("sv"); sa = sem("sa"); sp = sem("sp"); se = sem("se")
        st0 = sem("st0"); st1 = sem("st1"); stf = sem("stf")
        blk = es.enter_context(nc.Block())
        slp = [sl0, sl1]
        stp = [st0, st1]
        xb = [xt, xt2]
        tb = [tok, tok2]
        lb = [lnb, lnb2]
        NV = 9

        @blk.sync
        def _(s_):
            s_.dma_start(out=wrrow[:], in_=wr).then_inc(slc, 16)
            s_.dma_start(out=onesfs[:], in_=onesf).then_inc(slc, 16)
            s_.dma_start(out=idents[:], in_=identb).then_inc(slc, 16)
            for i in range(nt):
                if i >= 2:
                    s_.wait_ge(stp[i % 2], 32 * (i // 2))
                s_.dma_start(out=xb[i % 2][:], in_=xs[ts(i, 128), :]).then_inc(slp[i % 2], 16)

        @blk.tensor
        def _(t):
            t.wait_ge(slc, 48)
            for j in range(4):
                if j:
                    t.wait_ge(sv, j)
                t.matmul(psa[:, 0:512], onesfs[:, :], wrrow[:, j * 512:(j + 1) * 512],
                         start=True, stop=True).then_inc(sp, 1)
            for i in range(nt):
                t.wait_ge(sv, 6 + i * NV + 5)
                for g in range(2):
                    if i * 2 + g >= 2:
                        t.wait_ge(se, i * 2 + g - 1)
                    bank = ptr if g == 0 else ptr2
                    for k in range(8):
                        dt = g * 8 + k
                        mm = t.transpose(bank[:, k * 128:(k + 1) * 128],
                                         tb[i % 2][:, dt * 128:(dt + 1) * 128], idents[:])
                    mm.then_inc(sp, 1)

        @blk.vector
        def _(v):
            for j in range(4):
                v.wait_ge(sp, j + 1)
                v.tensor_copy(wrbc[:, j * 512:(j + 1) * 512], psa[:, 0:512]).then_inc(sv, 1)
            v.wait_ge(sv, 4)
            v.tensor_reduce(wrsum[:], wrbc[:], AX.X, ALU.add).then_inc(sv, 2)
            for i in range(nt):
                base = 6 + i * NV
                x = xb[i % 2]
                v.wait_ge(slp[i % 2], 16 * (i // 2 + 1))
                for q in range(4):
                    mm = v.bn_stats(sm[:, 6 * q:6 * q + 6], x[:, 512 * q:512 * (q + 1)])
                mm.then_inc(sv, 1)
                v.wait_ge(sv, base + 1)
                v.bn_aggr(sm[:, 24:26], sm[:, 0:24]).then_inc(sv, 1)
                v.wait_ge(sv, base + 2)
                v.tensor_scalar(sm[:, 26:27], sm[:, 25:26], 1e-5, None, ALU.add
                                ).then_inc(sv, 1)
                v.wait_ge(sv, base + 3)
                v.reciprocal(sm[:, 27:28], sm[:, 26:27]).then_inc(sv, 1)
                v.wait_ge(sa, i + 1)
                if i >= 2:
                    v.wait_ge(stp[i % 2], 32 * (i // 2))
                v.tensor_scalar(tb[i % 2][:], x[:], sm[:, 24:25], sm[:, 28:29],
                                ALU.subtract, ALU.mult)
                v.tensor_tensor(sct[:], x[:], wrbc[:], ALU.mult).then_inc(sv, 1)
                v.wait_ge(sv, base + 5)
                v.tensor_reduce(sm[:, 30:31], sct[:], AX.X, ALU.add).then_inc(sv, 1)
                v.wait_ge(sv, base + 6)
                v.tensor_tensor(sm[:, 32:33], sm[:, 24:25], wrsum[:], ALU.mult
                                ).then_inc(sv, 1)
                v.wait_ge(sv, base + 7)
                v.tensor_tensor(sm[:, 33:34], sm[:, 30:31], sm[:, 32:33], ALU.subtract
                                ).then_inc(sv, 1)
                v.wait_ge(sv, base + 8)
                v.tensor_tensor(sm[:, 40 + i:41 + i], sm[:, 33:34],
                                sm[:, 28:29], ALU.mult).then_inc(sv, 1)
                for g in range(2):
                    v.wait_ge(sp, 4 + i * 2 + g + 1)
                    bank = ptr if g == 0 else ptr2
                    v.tensor_copy(
                        lb[i % 2].rearrange("p (dt q) -> p dt q", q=128)[
                            :, g * 8:(g + 1) * 8, :],
                        bank[:].rearrange("p (k q) -> p k q", q=128)).then_inc(se, 1)

        @blk.scalar
        def _(sc):
            for i in range(nt):
                base = 6 + i * NV
                sc.wait_ge(sv, base + 4)
                sc.activation(sm[:, 28:29], sm[:, 27:28], AF.Sqrt).then_inc(sa, 1)
                sc.wait_ge(sv, base + 5)
                sc.dma_start(out=lntok[ts(i, 128), :], in_=tb[i % 2][:]
                             ).then_inc(stp[i % 2], 16)
                sc.wait_ge(se, (i + 1) * 2)
                sc.dma_start(out=lnT.rearrange("(dt p) n -> p dt n", p=128)[:, :, ts(i, 128)],
                             in_=lb[i % 2].rearrange("p (dt q) -> p dt q", q=128)
                             ).then_inc(stp[i % 2], 16)
            sc.wait_ge(sv, 6 + nt * NV)
            sc.dma_start(out=sco[:, :], in_=sm[:, 40:40 + nt]).then_inc(stf, 16)
            sc.wait_ge(st0, 32 * ((nt + 1) // 2))
            sc.wait_ge(st1, 32 * (nt // 2))
            sc.wait_ge(stf, 16)

    return nc


def build_att(TT, nq, nkv, hbias):
    nc = bass.Bass("TRN2", target_bir_lowering=False)
    nch = TT // 512
    ntt = TT // 128
    kvmap = [h * nkv // nq for h in range(nq)]
    nio = nq + 2 * nkv

    lnT = nc.dram_tensor("lnT", [16, 128, TT], BF16, kind="ExternalInput").ap()
    wqkv = nc.dram_tensor("wqkv", [D, nio * 128], F32, kind="ExternalInput").ap()
    if hbias:
        bqc = nc.dram_tensor("bqc", [128, nq], F32, kind="ExternalInput").ap()
        bkc = nc.dram_tensor("bkc", [128, nkv], F32, kind="ExternalInput").ap()
        bvc = nc.dram_tensor("bvc", [128, nkv], F32, kind="ExternalInput").ap()
    identb = nc.dram_tensor("identb", [128, 128], BF16, kind="ExternalInput").ap()
    onesb = nc.dram_tensor("onesb", [128, 1], BF16, kind="ExternalInput").ap()
    onesf = nc.dram_tensor("onesf", [1, 128], F32, kind="ExternalInput").ap()
    dmask = nc.dram_tensor("dmask", [4, 128, 512], F32, kind="ExternalInput").ap()
    ctxo = nc.dram_tensor("ctxo", [nq * 128, TT], BF16, kind="ExternalOutput").ap()

    wqkvb = nc.dram_tensor("wqkvb", [D, nio * 128], BF16).ap()

    def nkj(qc):
        return 4 * (qc + 1)

    def cumS(h, qc):
        tot = 0
        for hh in range(nq):
            for qq in range(nch):
                if (hh, qq) < (h, qc):
                    tot += nkj(qq)
        return tot

    es = ExitStack()
    sb = lambda n, shp, dt: es.enter_context(nc.sbuf_tensor(n, shp, dt))
    ps = lambda n, shp, dt=F32: es.enter_context(nc.psum_tensor(n, shp, dt))
    sem = lambda n: es.enter_context(nc.semaphore(n))
    with es:
        lnc = sb("lnc", [128, 16 * 512], BF16)
        lnc2 = sb("lnc2", [128, 16 * 512], BF16)
        wqs = sb("wqs", [128, 16 * nio * 128], BF16)
        qT = sb("qT", [128, nq * TT], BF16)
        kT = sb("kT", [128, nkv * TT], BF16)
        vT = sb("vT", [128, nkv * TT], BF16)
        vtok = sb("vtok", [128, nkv * TT], BF16)
        prb0 = sb("prb0", [128, 512], BF16)
        prb1 = sb("prb1", [128, 512], BF16)
        rrow = sb("rrow", [1, 512], F32)
        rbc = sb("rbc", [128, 512], F32)
        ctxc = sb("ctxc", [128, nq * 512], BF16)
        idents = sb("idents", [128, 128], BF16)
        onec = sb("onec", [128, 1], BF16)
        onesfs = sb("onesfs", [1, 128], F32)
        masks = sb("masks", [128, 2048], F32)
        bqs = sb("bqs", [128, max(nq, 1)], F32)
        bks = sb("bks", [128, max(nkv, 1)], F32)
        bvs = sb("bvs", [128, max(nkv, 1)], F32)
        psA = ps("psA", [128, 512])
        psB = ps("psB", [128, 512])
        psC = ps("psC", [128, 512])
        psD = ps("psD", [128, 512])
        psE = ps("psE", [128, 512])
        ptr = ps("ptr", [128, 1024], BF16)
        ptr2 = ps("ptr2", [128, 1024], BF16)
        sg = sem("sg"); slw = sem("slw"); sl0 = sem("sl0"); sl1 = sem("sl1")
        s_qp = sem("s_qp"); s_qv = sem("s_qv")
        s_tr = sem("s_tr"); s_te = sem("s_te"); s_s = sem("s_s"); s_d = sem("s_d")
        s_e = sem("s_e"); s_av = sem("s_av"); s_r = sem("s_r"); s_b = sem("s_b")
        s_v2 = sem("s_v2"); st0 = sem("st0"); st1 = sem("st1")
        blk = es.enter_context(nc.Block())
        slp = [sl0, sl1]
        stp = [st0, st1]
        lncb = [lnc, lnc2]
        prb = [prb0, prb1]
        NLD = 7 + (3 if hbias else 0) + 16

        @blk.gpsimd
        def _(g):
            g.dma_start(out=wqkvb, in_=wqkv).then_inc(sg, 16)

        @blk.sync
        def _(s_):
            s_.dma_start(out=idents[:], in_=identb).then_inc(slw, 16)
            s_.dma_start(out=onec[:], in_=onesb).then_inc(slw, 16)
            s_.dma_start(out=onesfs[:], in_=onesf).then_inc(slw, 16)
            for v_ in range(4):
                s_.dma_start(out=masks[:, v_ * 512:(v_ + 1) * 512], in_=dmask[v_, :, :]
                             ).then_inc(slw, 16)
            if hbias:
                s_.dma_start(out=bqs[:], in_=bqc).then_inc(slw, 16)
                s_.dma_start(out=bks[:], in_=bkc).then_inc(slw, 16)
                s_.dma_start(out=bvs[:], in_=bvc).then_inc(slw, 16)
            s_.wait_ge(sg, 16)
            for dm in range(16):
                s_.dma_start(out=wqs[:, dm * nio * 128:(dm + 1) * nio * 128],
                             in_=wqkvb[ts(dm, 128), :]).then_inc(slw, 16)
            for c in range(nch):
                if c >= 2:
                    s_.wait_ge(s_qp, (c - 1) * nio)
                s_.dma_start(out=lncb[c % 2].rearrange("p (dt q) -> p dt q", q=512),
                             in_=lnT[:, :, ts(c, 512)].rearrange("dt p q -> p dt q")
                             ).then_inc(slp[c % 2], 16)

        @blk.tensor
        def _(t):
            t.wait_ge(slw, NLD * 16)
            for c in range(nch):
                t.wait_ge(slp[c % 2], 16 * (c // 2 + 1))
                for io in range(nio):
                    u = c * nio + io
                    if u >= 2:
                        t.wait_ge(s_qv, u - 1)
                    bank = psA if io % 2 == 0 else psB
                    for dm in range(16):
                        mm = t.matmul(bank[:, 0:512],
                                      wqs[:, (dm * nio + io) * 128:(dm * nio + io + 1) * 128],
                                      lncb[c % 2][:, dm * 512:(dm + 1) * 512],
                                      start=(dm == 0), stop=(dm == 15))
                        if dm == 15:
                            mm.then_inc(s_qp, 1)
            t.wait_ge(s_qv, nch * nio)
            ngr = (nkv * ntt) // 8
            for g in range(ngr):
                if g >= 2:
                    t.wait_ge(s_te, g - 1)
                bank = ptr if g % 2 == 0 else ptr2
                for k in range(8):
                    i = g * 8 + k
                    mm = t.transpose(bank[:, k * 128:(k + 1) * 128],
                                     vT[:, i * 128:(i + 1) * 128], idents[:])
                mm.then_inc(s_tr, 1)
            t.wait_ge(s_te, ngr)
            for h in range(nq):
                kv = kvmap[h]
                n = h * nch
                for qc in range(nch):
                    npair = h * nch + qc
                    cs = cumS(h, qc)
                    nk = nkj(qc)
                    for kj in range(nk):
                        diag = kj >= 4 * qc
                        bank = psA if kj % 2 == 0 else psB
                        if cs + kj >= 2:
                            t.wait_ge(s_e, cs + kj - 1)
                        t.matmul(bank[:, 0:512],
                                 kT[:, kv * TT + kj * 128:kv * TT + (kj + 1) * 128],
                                 qT[:, h * TT + qc * 512:h * TT + (qc + 1) * 512],
                                 start=True, stop=True).then_inc(s_s, 1)
                        t.wait_ge(s_e, cs + kj + 1)
                        if kj == 0:
                            t.wait_ge(s_v2, 2 * npair)
                        t.matmul(psC[:, 0:512],
                                 vtok[:, kv * TT + kj * 128:kv * TT + (kj + 1) * 128],
                                 prb[kj % 2][:], start=(kj == 0), stop=(kj == nk - 1))
                        if kj == 0:
                            t.wait_ge(s_r, npair)
                        t.matmul(psD[0:1, 0:512], onec[:], prb[kj % 2][:],
                                 start=(kj == 0), stop=(kj == nk - 1)).then_inc(s_av, 1)
                    t.wait_ge(s_r, npair + 1)
                    if npair >= 1:
                        t.wait_ge(s_v2, 2 * npair - 1)
                    t.matmul(psE[:, 0:512], onesfs[:, :], rrow[:, 0:512],
                             start=True, stop=True).then_inc(s_b, 1)

        @blk.vector
        def _(v):
            for c in range(nch):
                for io in range(nio):
                    u = c * nio + io
                    v.wait_ge(s_qp, u + 1)
                    bank = psA if io % 2 == 0 else psB
                    if io < nq:
                        dst = qT[:, io * TT + c * 512:io * TT + (c + 1) * 512]
                        bias = bqs[:, io:io + 1] if hbias else None
                    elif io < nq + nkv:
                        kv = io - nq
                        dst = kT[:, kv * TT + c * 512:kv * TT + (c + 1) * 512]
                        bias = bks[:, kv:kv + 1] if hbias else None
                    else:
                        kv = io - nq - nkv
                        dst = vT[:, kv * TT + c * 512:kv * TT + (c + 1) * 512]
                        bias = bvs[:, kv:kv + 1] if hbias else None
                    if bias is not None:
                        v.tensor_scalar(dst, bank[:, 0:512], bias, None, ALU.add
                                        ).then_inc(s_qv, 1)
                    else:
                        v.tensor_copy(dst, bank[:, 0:512]).then_inc(s_qv, 1)
            for g in range((nkv * ntt) // 8):
                v.wait_ge(s_tr, g + 1)
                bank = ptr if g % 2 == 0 else ptr2
                v.tensor_copy(vtok[:, g * 1024:(g + 1) * 1024], bank[:]).then_inc(s_te, 1)
            for h in range(nq):
                for qc in range(nch):
                    npair = h * nch + qc
                    cs = cumS(h, qc)
                    nk = nkj(qc)
                    for kj in range(4 * qc, nk):
                        bank = psA if kj % 2 == 0 else psB
                        vm = kj - 4 * qc
                        v.wait_ge(s_s, cs + kj + 1)
                        v.tensor_tensor(bank[:, 0:512], bank[:, 0:512],
                                        masks[:, vm * 512:(vm + 1) * 512], ALU.add
                                        ).then_inc(s_d, 1)
                    v.wait_ge(s_av, cs + nk)
                    v.reciprocal(rrow[:, 0:512], psD[0:1, 0:512]).then_inc(s_r, 1)
                    v.wait_ge(s_b, npair + 1)
                    v.tensor_copy(rbc[:], psE[:, 0:512]).then_inc(s_v2, 1)
                    v.wait_ge(s_v2, 2 * npair + 1)
                    if npair >= 2:
                        v.wait_ge(stp[npair % 2], 16 * (npair // 2))
                    v.tensor_tensor(ctxc[:, (npair % 2) * 512:(npair % 2 + 1) * 512],
                                    psC[:, 0:512], rbc[:], ALU.mult).then_inc(s_v2, 1)

        @blk.scalar
        def _(sc):
            for h in range(nq):
                for qc in range(nch):
                    npair = h * nch + qc
                    cs = cumS(h, qc)
                    nk = nkj(qc)
                    for kj in range(nk):
                        diag = kj >= 4 * qc
                        bank = psA if kj % 2 == 0 else psB
                        if diag:
                            sc.wait_ge(s_d, 4 * npair + (kj - 4 * qc) + 1)
                        else:
                            sc.wait_ge(s_s, cs + kj + 1)
                        if cs + kj >= 2:
                            sc.wait_ge(s_av, cs + kj - 1)
                        sc.activation(prb[kj % 2][:], bank[:, 0:512], AF.Exp,
                                      scale=ISQ).then_inc(s_e, 1)
                    sc.wait_ge(s_v2, 2 * npair + 2)
                    sc.dma_start(out=ctxo[ts(h, 128), ts(qc, 512)],
                                 in_=ctxc[:, (npair % 2) * 512:(npair % 2 + 1) * 512]
                                 ).then_inc(stp[npair % 2], 16)
            npt = nq * nch
            sc.wait_ge(st0, 16 * ((npt + 1) // 2))
            sc.wait_ge(st1, 16 * (npt // 2))

    return nc


def _consts():
    C = {}
    C["identb"] = np.eye(128).astype(BF)
    C["identf"] = np.eye(128, dtype=np.float32)
    C["onesf"] = np.ones((1, 128), np.float32)
    C["onesb"] = np.ones((128, 1), BF)
    C["onescf"] = np.ones((128, 1), np.float32)
    dm = np.zeros((4, 128, 512), np.float32)
    for v in range(4):
        kj = v * 128 + np.arange(128)[:, None]
        qi = np.arange(512)[None, :]
        dm[v] = np.where(kj > qi, NEG, 0.0)
    C["dmask"] = dm
    rot = np.zeros((128, 128), np.float32)
    for i in range(64):
        rot[64 + i, i] = -1.0
        rot[i, 64 + i] = 1.0
    C["rotb"] = rot.astype(BF)
    C["iota1k"] = np.broadcast_to(np.arange(1024, dtype=np.float32), (128, 1024)).copy()
    C["trif"] = (np.arange(128)[:, None] <= np.arange(128)[None, :]).astype(np.float32)
    C["tcolf"] = np.arange(128, dtype=np.float32)[:, None]
    C["invfs"] = np.concatenate(
        [1.0 / RTH ** (np.arange(0, 128, 2) / 128.0)] * 2)[None, :].astype(np.float32)
    return C


def _run(nc, maps):
    return run_bass_kernel_spmd(nc, maps, core_ids=list(range(NCORE)))


def kernel(**inputs):
    f = lambda k: np.ascontiguousarray(np.asarray(inputs[k], np.float32))
    C = _consts()
    x0 = f("L0_attn_a_x").reshape(T, D)
    wrr = f("L0_ffn_w_router").reshape(1, D)
    p_ln = _cached("ln", lambda: build_ln(1024))
    p_atta = _cached("atta", lambda: build_att(T, 2, 1, False))
    p_attb = _cached("attb", lambda: build_att(T, 2, 2, True))
    p_proj = _cached("proj", lambda: build_proj(1024))
    p_fsel = _cached("fsel", lambda: build_fsel())
    p_fmain = _cached("fmain", lambda: build_fmain())
    p_scat = _cached("scat", lambda: build_scat(1024))

    def L_ln(x):
        maps = [{"xs": np.ascontiguousarray(x[c * 1024:(c + 1) * 1024]), "wr": wrr,
                 "onesf": C["onesf"], "identb": C["identb"]} for c in range(NCORE)]
        r = _run(p_ln, maps)
        lntok = np.concatenate([np.asarray(r.results[c]["lntok"]) for c in range(NCORE)], 0)
        lnT = np.concatenate([np.asarray(r.results[c]["lnT"]).reshape(16, 128, 1024)
                              for c in range(NCORE)], 2)
        sco = np.concatenate([np.asarray(r.results[c]["sco"]) for c in range(NCORE)], 1)
        return lntok, lnT, sco

    def L_att(prog, lnT, mk):
        maps = []
        for c in range(NCORE):
            m = mk(c)
            m.update({"lnT": lnT, "identb": C["identb"], "onesb": C["onesb"],
                      "onesf": C["onesf"], "dmask": C["dmask"]})
            maps.append(m)
        r = _run(prog, maps)
        return np.concatenate([np.asarray(r.results[c]["ctxo"]) for c in range(NCORE)], 0)

    def L_proj(ctxT, xprev, W, brow):
        onesbr = np.ones((1, 128), BF)
        maps = [{"cT": np.ascontiguousarray(ctxT[:, c * 1024:(c + 1) * 1024]
                                            ).reshape(16, 128, 1024),
                 "xprev": np.ascontiguousarray(xprev[c * 1024:(c + 1) * 1024]),
                 "W": W, "brow": brow, "onesbr": onesbr} for c in range(NCORE)]
        r = _run(p_proj, maps)
        return np.concatenate([np.asarray(r.results[c]["xnew"]) for c in range(NCORE)], 0)

    fwq, fwk, fwv = f("L0_ffn_wq"), f("L0_ffn_wk"), f("L0_ffn_wv")
    fw3 = np.ascontiguousarray(np.concatenate([fwq, fwk, fwv], 1))
    biasc = np.ascontiguousarray(np.concatenate(
        [f("L0_ffn_bq").ravel(), f("L0_ffn_bk").ravel(), f("L0_ffn_bv").ravel()]
        ).reshape(32, 128).T)
    ln1c = np.ascontiguousarray(f("L0_ffn_ln1_w").reshape(16, 128).T)
    ln2bc = np.broadcast_to(f("L0_ffn_ln2_w").reshape(1, D), (128, D)).copy()
    fwo, fwg = f("L0_ffn_wo"), f("L0_ffn_w_gate")
    fwu, fwd = f("L0_ffn_w_up"), f("L0_ffn_w_down")
    amasks = [np.where(np.arange(SSEL)[None, :] <= (128 * c + np.arange(128))[:, None],
                       0.0, NEG).astype(np.float32) for c in range(NCORE)]

    def ffn(xin, lntok, sco):
        maps = [{"sco": sco, "lnfull": lntok, "fw3": fw3, "ln1c": ln1c,
                 "biasc": biasc, "sbase": np.array([[c * 128]], np.uint32),
                 "invfs": C["invfs"], "identf": C["identf"], "identb": C["identb"],
                 "rotb": C["rotb"], "iota1k": C["iota1k"], "trif": C["trif"],
                 "tcolf": C["tcolf"], "onescf": C["onescf"], "onesf": C["onesf"]}
                for c in range(NCORE)]
        rs = _run(p_fsel, maps)
        kkTf = np.concatenate([np.asarray(rs.results[c]["kkTs"]) for c in range(NCORE)], 1)
        vfull = np.concatenate([np.asarray(rs.results[c]["vso"]) for c in range(NCORE)], 0)
        posco = np.asarray(rs.results[0]["posco"])
        acolo = np.asarray(rs.results[0]["acolo"])
        maps = [{"qTs": np.asarray(rs.results[c]["qTs"]), "kkT": kkTf, "vfull": vfull,
                 "seltok": np.asarray(rs.results[c]["selo"]),
                 "myscol": np.ascontiguousarray(posco[1, c * 128:(c + 1) * 128]
                                                ).reshape(128, 1).astype(np.float32),
                 "amask": amasks[c], "fwo": fwo, "fwg": fwg, "fwu": fwu, "fwd": fwd,
                 "ln2bc": ln2bc, "identb": C["identb"]} for c in range(NCORE)]
        rm = _run(p_fmain, maps)
        gated = np.concatenate([np.asarray(rm.results[c]["gato"]) for c in range(NCORE)], 0)
        maps = [{"xprev": np.ascontiguousarray(xin[c * 1024:(c + 1) * 1024]),
                 "lntk": np.ascontiguousarray(lntok[c * 1024:(c + 1) * 1024]),
                 "gated": gated,
                 "acol": np.ascontiguousarray(acolo[:, 8 * c:8 * c + 8])}
                for c in range(NCORE)]
        rr = _run(p_scat, maps)
        return np.concatenate([np.asarray(rr.results[c]["xnew"]) for c in range(NCORE)], 0)

    _, lnT0, _ = L_ln(x0)
    wq, wk, wv = f("L0_attn_a_Wq"), f("L0_attn_a_Wk"), f("L0_attn_a_Wv")
    ctxa = L_att(p_atta, lnT0, lambda c: {"wqkv": np.ascontiguousarray(np.concatenate(
        [wq[:, 256 * c:256 * (c + 1)], wk[:, 128 * (c // 2):128 * (c // 2 + 1)],
         wv[:, 128 * (c // 2):128 * (c // 2 + 1)]], 1))})
    x1 = L_proj(ctxa, x0, f("L0_attn_a_Wo"), f("L0_attn_a_bo").reshape(1, D))

    lntok1, _, sco1 = L_ln(x1)
    x2 = ffn(x1, lntok1, sco1)

    _, lnT2, _ = L_ln(x2)
    wab, bab = f("L0_attn_b_w_attn"), f("L0_attn_b_b_attn").ravel()

    def mk_b(c):
        sl = lambda o: wab[:, o + 256 * c:o + 256 * (c + 1)]
        bsl = lambda o: np.ascontiguousarray(
            bab[o + 256 * c:o + 256 * (c + 1)].reshape(2, 128).T)
        return {"wqkv": np.ascontiguousarray(np.concatenate(
                    [sl(0), sl(2048), sl(4096)], 1)),
                "bqc": bsl(0), "bkc": bsl(2048), "bvc": bsl(4096)}
    ctxb = L_att(p_attb, lnT2, mk_b)
    x3 = L_proj(ctxb, x2, f("L0_attn_b_w_proj"), f("L0_attn_b_b_proj").reshape(1, D))

    lntok3, _, sco3 = L_ln(x3)
    x4 = ffn(x3, lntok3, sco3)
    return np.ascontiguousarray(x4.reshape(1, T, D).astype(np.float32))


def build_proj(NT=1024):
    nc = bass.Bass("TRN2", target_bir_lowering=False)
    nt = NT // 128
    cT = nc.dram_tensor("cT", [16, 128, NT], BF16, kind="ExternalInput").ap()
    xprev = nc.dram_tensor("xprev", [NT, D], F32, kind="ExternalInput").ap()
    W = nc.dram_tensor("W", [D, D], F32, kind="ExternalInput").ap()
    brow = nc.dram_tensor("brow", [1, D], F32, kind="ExternalInput").ap()
    onesbr = nc.dram_tensor("onesbr", [1, 128], BF16, kind="ExternalInput").ap()
    xnew = nc.dram_tensor("xnew", [NT, D], F32, kind="ExternalOutput").ap()
    Wb = nc.dram_tensor("Wb", [D, D], BF16).ap()
    bb = nc.dram_tensor("bb", [1, D], BF16).ap()
    es = ExitStack()
    sb = lambda n, shp, dt: es.enter_context(nc.sbuf_tensor(n, shp, dt))
    ps = lambda n, shp, dt=F32: es.enter_context(nc.psum_tensor(n, shp, dt))
    sem = lambda n: es.enter_context(nc.semaphore(n))
    with es:
        cts = sb("cts", [128, 16 * NT], BF16)
        ws = sb("ws", [128, 16 * 2048], BF16)
        xt = sb("xt", [128, D], F32)
        xt2 = sb("xt2", [128, D], F32)
        xo = sb("xo", [128, D], F32)
        xo2 = sb("xo2", [128, D], F32)
        bbs = sb("bbs", [1, D], BF16)
        onbr = sb("onbr", [1, 128], BF16)
        psA = ps("psA", [128, 512])
        psB = ps("psB", [128, 512])
        sg = sem("sg"); slw = sem("slw"); sl0 = sem("sl0"); sl1 = sem("sl1")
        sp = sem("sp"); sv = sem("sv"); st0 = sem("st0"); st1 = sem("st1")
        blk = es.enter_context(nc.Block())
        slp = [sl0, sl1]
        stp = [st0, st1]
        xb = [xt, xt2]
        xob = [xo, xo2]

        @blk.gpsimd
        def _(g):
            g.dma_start(out=Wb, in_=W).then_inc(sg, 16)
            g.dma_start(out=bb, in_=brow).then_inc(sg, 16)

        @blk.sync
        def _(s_):
            s_.dma_start(out=onbr[:], in_=onesbr).then_inc(slw, 16)
            s_.dma_start(out=cts.rearrange("p (dt n) -> p dt n", n=NT),
                         in_=cT.rearrange("dt p n -> p dt n")).then_inc(slw, 16)
            s_.wait_ge(sg, 32)
            s_.dma_start(out=bbs[:], in_=bb).then_inc(slw, 16)
            for dm in range(16):
                s_.dma_start(out=ws[:, dm * 2048:(dm + 1) * 2048],
                             in_=Wb[ts(dm, 128), :]).then_inc(slw, 16)
            for i in range(nt):
                if i >= 2:
                    s_.wait_ge(stp[i % 2], 16 * (i // 2))
                s_.dma_start(out=xb[i % 2][:], in_=xprev[ts(i, 128), :]
                             ).then_inc(slp[i % 2], 16)

        @blk.tensor
        def _(t):
            t.wait_ge(slw, 19 * 16)
            for i in range(nt):
                for ch in range(4):
                    u = i * 4 + ch
                    if u >= 2:
                        t.wait_ge(sv, u - 1)
                    bank = psA if u % 2 == 0 else psB
                    for dm in range(16):
                        t.matmul(bank[:, 0:512],
                                 cts[:, dm * NT + i * 128:dm * NT + (i + 1) * 128],
                                 ws[:, dm * 2048 + ch * 512:dm * 2048 + (ch + 1) * 512],
                                 start=(dm == 0), stop=False)
                    t.matmul(bank[:, 0:512], onbr[:, :],
                             bbs[:, ch * 512:(ch + 1) * 512],
                             start=False, stop=True).then_inc(sp, 1)

        @blk.vector
        def _(v):
            for i in range(nt):
                v.wait_ge(slp[i % 2], 16 * (i // 2 + 1))
                if i >= 2:
                    v.wait_ge(stp[i % 2], 16 * (i // 2))
                for ch in range(4):
                    u = i * 4 + ch
                    v.wait_ge(sp, u + 1)
                    bank = psA if u % 2 == 0 else psB
                    v.scalar_tensor_tensor(
                        out=xob[i % 2][:, ch * 512:(ch + 1) * 512], in0=bank[:, 0:512],
                        scalar=1.0, in1=xb[i % 2][:, ch * 512:(ch + 1) * 512],
                        op0=ALU.mult, op1=ALU.add).then_inc(sv, 1)

        @blk.scalar
        def _(sc):
            for i in range(nt):
                sc.wait_ge(sv, i * 4 + 4)
                sc.dma_start(out=xnew[ts(i, 128), :], in_=xob[i % 2][:]
                             ).then_inc(stp[i % 2], 16)
            sc.wait_ge(st0, 16 * ((nt + 1) // 2))
            sc.wait_ge(st1, 16 * (nt // 2))

    return nc


def build_scat(NT=1024):
    nc = bass.Bass("TRN2", target_bir_lowering=False)
    nt = NT // 128
    xprev = nc.dram_tensor("xprev", [NT, D], F32, kind="ExternalInput").ap()
    lntk = nc.dram_tensor("lntk", [NT, D], BF16, kind="ExternalInput").ap()
    gated = nc.dram_tensor("gated", [SSEL, D], BF16, kind="ExternalInput").ap()
    acol = nc.dram_tensor("acol", [128, 8], U32, kind="ExternalInput").ap()
    xnew = nc.dram_tensor("xnew", [NT, D], F32, kind="ExternalOutput").ap()
    es = ExitStack()
    sb = lambda n, shp, dt: es.enter_context(nc.sbuf_tensor(n, shp, dt))
    sem = lambda n: es.enter_context(nc.semaphore(n))
    with es:
        xt = sb("xt", [128, D], F32)
        xt2 = sb("xt2", [128, D], F32)
        lt = sb("lt", [128, D], BF16)
        lt2 = sb("lt2", [128, D], BF16)
        gb = sb("gb", [128, D], BF16)
        gb2 = sb("gb2", [128, D], BF16)
        xo = sb("xo", [128, D], F32)
        xo2 = sb("xo2", [128, D], F32)
        acs = sb("acs", [128, 8], U32)
        af = sb("af", [128, 8], F32)
        mcol = sb("mcol", [128, 8], F32)
        sg0 = sem("sg0"); sg1 = sem("sg1")
        sl0 = sem("sl0"); sl1 = sem("sl1"); sla = sem("sla"); smz = sem("smz")
        sv = sem("sv"); sz = sem("sz")
        st0 = sem("st0"); st1 = sem("st1")
        blk = es.enter_context(nc.Block())
        sgp = [sg0, sg1]
        slp = [sl0, sl1]
        stp = [st0, st1]
        xb = [xt, xt2]
        lb = [lt, lt2]
        gbb = [gb, gb2]
        xob = [xo, xo2]

        @blk.gpsimd
        def _(g):
            g.dma_start(out=acs[:], in_=acol).then_inc(sla, 16)
            g.memset(gb[:], 0.0).then_inc(smz, 1)
            g.memset(gb2[:], 0.0).then_inc(smz, 1)
            g.wait_ge(sla, 16)
            g.wait_ge(smz, 2)
            for i in range(nt):
                if i >= 2:
                    g.wait_ge(sv, 3 * (i - 1))
                g.indirect_dma_start(
                    gbb[i % 2][:], None, gated,
                    bass.IndirectOffsetOnAxis(ap=acs[:, i:i + 1], axis=0),
                    bounds_check=SSEL - 1, oob_is_err=False).then_inc(sgp[i % 2], 16)

        @blk.sync
        def _(s_):
            for i in range(nt):
                if i >= 2:
                    s_.wait_ge(stp[i % 2], 16 * (i // 2))
                s_.dma_start(out=xb[i % 2][:], in_=xprev[ts(i, 128), :]
                             ).then_inc(slp[i % 2], 16)
                s_.dma_start(out=lb[i % 2][:], in_=lntk[ts(i, 128), :]
                             ).then_inc(slp[i % 2], 16)

        @blk.vector
        def _(v):
            v.wait_ge(sla, 16)
            v.tensor_copy(af[:], acs[:]).then_inc(sz, 1)
            v.wait_ge(sz, 1)
            v.tensor_scalar(mcol[:], af[:], float(SSEL) - 0.5, None, ALU.is_ge
                            ).then_inc(sz, 1)
            for i in range(nt):
                v.wait_ge(sz, 2)
                v.wait_ge(slp[i % 2], 32 * (i // 2 + 1))
                v.wait_ge(sgp[i % 2], 16 * (i // 2 + 1))
                if i >= 2:
                    v.wait_ge(stp[i % 2], 16 * (i // 2))
                v.scalar_tensor_tensor(
                    out=xob[i % 2][:], in0=lb[i % 2][:], scalar=mcol[:, i:i + 1],
                    in1=xb[i % 2][:], op0=ALU.mult, op1=ALU.add).then_inc(sv, 1)
                v.wait_ge(sv, 3 * i + 1)
                v.tensor_tensor(xob[i % 2][:], xob[i % 2][:], gbb[i % 2][:], ALU.add
                                ).then_inc(sv, 1)
                v.wait_ge(sv, 3 * i + 2)
                v.tensor_scalar(gbb[i % 2][:], gbb[i % 2][:], 0.0, None, ALU.mult
                                ).then_inc(sv, 1)

        @blk.scalar
        def _(sc):
            for i in range(nt):
                sc.wait_ge(sv, 3 * i + 2)
                sc.dma_start(out=xnew[ts(i, 128), :], in_=xob[i % 2][:]
                             ).then_inc(stp[i % 2], 16)
            sc.wait_ge(st0, 16 * ((nt + 1) // 2))
            sc.wait_ge(st1, 16 * (nt // 2))

    return nc


NBIS = 46


class _Sched:
    KINDS = {"vector": ("sv", 1), "tensor": ("sp", 1), "scalar": ("sa", 1),
             "gpsimd": ("sz", 1), "sync_dma": ("sl", 16),
             "scalar_dma": ("sa2", 16), "gpsimd_dma": ("sg", 16)}

    def __init__(self):
        self.steps = []
        self.cnt = {v[0]: 0 for v in self.KINDS.values()}

    def add(self, kind, fn):
        semn, inc = self.KINDS[kind]
        waits = dict((k, v) for k, v in self.cnt.items() if v > 0)
        self.steps.append((kind, fn, waits))
        self.cnt[semn] += inc

    def emit_one(self, eng, sems, kinds):
        lw = {}
        for kind, fn, waits in self.steps:
            if kind not in kinds:
                continue
            semn, inc = self.KINDS[kind]
            for k, v in waits.items():
                if lw.get(k, -1) < v:
                    eng.wait_ge(sems[k], v)
                    lw[k] = v
            inst = fn(eng)
            if inst is None:
                raise RuntimeError("step returned no instruction")
            inst.then_inc(sems[semn], inc)


def build_fsel():
    nc = bass.Bass("TRN2", target_bir_lowering=False)
    sco = nc.dram_tensor("sco", [128, 64], F32, kind="ExternalInput").ap()
    lnfull = nc.dram_tensor("lnfull", [T, D], BF16, kind="ExternalInput").ap()
    fw3 = nc.dram_tensor("fw3", [D, 4096], F32, kind="ExternalInput").ap()
    ln1c = nc.dram_tensor("ln1c", [128, 16], F32, kind="ExternalInput").ap()
    biasc = nc.dram_tensor("biasc", [128, 32], F32, kind="ExternalInput").ap()
    sbase = nc.dram_tensor("sbase", [1, 1], U32, kind="ExternalInput").ap()
    invfs = nc.dram_tensor("invfs", [1, 128], F32, kind="ExternalInput").ap()
    identf = nc.dram_tensor("identf", [128, 128], F32, kind="ExternalInput").ap()
    identb = nc.dram_tensor("identb", [128, 128], BF16, kind="ExternalInput").ap()
    rotb = nc.dram_tensor("rotb", [128, 128], BF16, kind="ExternalInput").ap()
    iota1k = nc.dram_tensor("iota1k", [128, 1024], F32, kind="ExternalInput").ap()
    trif = nc.dram_tensor("trif", [128, 128], F32, kind="ExternalInput").ap()
    tcolf = nc.dram_tensor("tcolf", [128, 1], F32, kind="ExternalInput").ap()
    onescf = nc.dram_tensor("onescf", [128, 1], F32, kind="ExternalInput").ap()
    onesf = nc.dram_tensor("onesf", [1, 128], F32, kind="ExternalInput").ap()

    qTs = nc.dram_tensor("qTs", [16 * 128, 128], BF16, kind="ExternalOutput").ap()
    kkTs = nc.dram_tensor("kkTs", [8 * 128, 128], BF16, kind="ExternalOutput").ap()
    vso = nc.dram_tensor("vso", [128, 1024], BF16, kind="ExternalOutput").ap()
    selo = nc.dram_tensor("selo", [128, D], BF16, kind="ExternalOutput").ap()
    posco = nc.dram_tensor("posco", [2, SSEL], F32, kind="ExternalOutput").ap()
    acolo = nc.dram_tensor("acolo", [128, 64], U32, kind="ExternalOutput").ap()

    fw3b = nc.dram_tensor("fw3b", [D, 4096], BF16).ap()
    posd = nc.dram_tensor("posd", [SSEL, 1], F32).ap()

    es = ExitStack()
    sb = lambda n, shp, dt: es.enter_context(nc.sbuf_tensor(n, shp, dt))
    ps = lambda n, shp, dt=F32: es.enter_context(nc.psum_tensor(n, shp, dt))
    sem = lambda n: es.enter_context(nc.semaphore(n))
    with es:
        scsb = sb("scsb", [128, 64], F32)
        selw = sb("selw", [128, 64], F32)
        mge = sb("mge", [128, 64], F32)
        mgt = sb("mgt", [128, 64], F32)
        meq = sb("meq", [128, 64], F32)
        Ceq = sb("Ceq", [128, 64], F32)
        Csel = sb("Csel", [128, 64], F32)
        afb = sb("afb", [128, 64], F32)
        acu = sb("acu", [128, 64], U32)
        srow = sb("srow", [1, 16], F32)
        crow1 = sb("crow1", [1, 64], F32)
        crow2 = sb("crow2", [1, 64], F32)
        exbc = sb("exbc", [128, 64], F32)
        midc = sb("midc", [128, 1], F32)
        cntp = sb("cntp", [128, 1], F32)
        needc = sb("needc", [128, 1], F32)
        iot = sb("iot", [128, 1024], F32)
        trs = sb("trs", [128, 128], F32)
        tcs = sb("tcs", [128, 1], F32)
        onc = sb("onc", [128, 1], F32)
        onr = sb("onr", [1, 128], F32)
        idf = sb("idf", [128, 128], F32)
        idb = sb("idb", [128, 128], BF16)
        rob = sb("rob", [128, 128], BF16)
        Pf_ = [sb("Pf0", [128, 1024], F32), sb("Pf1", [128, 1024], F32)]
        stf_ = [sb("stf0", [128, 2], F32), sb("stf1", [128, 2], F32)]
        poscosb = sb("poscosb", [2, 1024], F32)
        sbs = sb("sbs", [1, 1], U32)
        sbf1 = sb("sbf1", [1, 1], F32)
        slotf = sb("slotf", [128, 1], F32)
        slotu = sb("slotu", [128, 1], U32)
        posg = sb("posg", [128, 1], F32)
        myidx = sb("myidx", [128, 1], U32)
        posrow = sb("posrow", [1, 128], F32)
        selrow = sb("selrow", [128, D], BF16)
        sq = sb("sq", [128, D], F32)
        rms = sb("rms", [128, 8], F32)
        hrow = sb("hrow", [128, D], BF16)
        hT = sb("hT", [128, D], BF16)
        fws = sb("fws", [128, 65536], BF16)
        lnc1 = sb("lnc1", [128, 16], F32)
        bcs = sb("bcs", [128, 32], F32)
        qko = sb("qko", [128, 4096], BF16)
        invr = sb("invr", [1, 128], F32)
        angs = sb("angs", [128, 128], F32)
        u1 = sb("u1", [128, 128], F32)
        ri = sb("ri", [128, 128], I32)
        rf = sb("rf", [128, 128], F32)
        sini = sb("sini", [128, 128], F32)
        sinv = sb("sinv", [128, 128], F32)
        cosv = sb("cosv", [128, 128], F32)
        rt1 = sb("rt1", [128, 128], F32)
        rt2 = sb("rt2", [128, 128], F32)
        vstok = sb("vstok", [128, 1024], BF16)
        psF = ps("psF", [128, 512])
        psG = ps("psG", [128, 512])
        psH1 = ps("psH1", [128, 512])
        psH2 = ps("psH2", [128, 512])
        psR = ps("psR", [128, 512])
        ptr = ps("ptr", [128, 1024], BF16)
        SEMS = {n: sem(n) for n in ["sv", "sp", "sa", "sz", "sl", "sa2", "sg"]}
        blk = es.enter_context(nc.Block())

        S = _Sched()

        S.add("gpsimd_dma", lambda g: g.dma_start(out=fw3b, in_=fw3))
        S.add("gpsimd_dma", lambda g: g.dma_start(out=sbs[:], in_=sbase))
        jobs = [(scsb, sco), (iot, iota1k), (trs, trif), (tcs, tcolf),
                (onc, onescf), (onr, onesf), (idf, identf), (idb, identb),
                (rob, rotb), (lnc1, ln1c), (bcs, biasc), (invr, invfs)]
        for dst, src in jobs:
            S.add("sync_dma", lambda s_, dst=dst, src=src: s_.dma_start(out=dst[:], in_=src))
        for dm in range(16):
            S.add("sync_dma", lambda s_, dm=dm: s_.dma_start(
                out=fws[:, dm * 4096:(dm + 1) * 4096], in_=fw3b[ts(dm, 128), :]))

        S.add("vector", lambda v: v.tensor_reduce(midc[:], scsb[:], AX.X, ALU.max,
                                                  apply_absolute_value=True))
        S.add("tensor", lambda t: t.transpose(psG[0:1, 0:128], midc[:], idf[:]))
        S.add("vector", lambda v: v.tensor_reduce(srow[:, 4:5], psG[0:1, 0:128],
                                                  AX.X, ALU.max))
        S.add("vector", lambda v: v.tensor_scalar(srow[:, 1:2], srow[:, 4:5], 1.0,
                                                  None, ALU.add))
        S.add("vector", lambda v: v.tensor_scalar(srow[:, 0:1], srow[:, 1:2], -1.0,
                                                  None, ALU.mult))
        for it in range(NBIS):
            S.add("vector", lambda v: v.tensor_tensor(srow[:, 4:5], srow[:, 0:1],
                                                      srow[:, 1:2], ALU.add))
            S.add("vector", lambda v: v.tensor_scalar(srow[:, 2:3], srow[:, 4:5], 0.5,
                                                      None, ALU.mult))
            S.add("tensor", lambda t: t.matmul(psF[:, 0:1], onr[:, :], srow[:, 2:3],
                                               start=True, stop=True))
            S.add("vector", lambda v: v.tensor_copy(midc[:], psF[:, 0:1]))
            S.add("vector", lambda v: v.tensor_scalar(selw[:], scsb[:], midc[:], 0.0,
                                                      ALU.is_ge, ALU.add,
                                                      accum_out=cntp[:]))
            S.add("tensor", lambda t: t.matmul(psG[0:1, 0:1], onc[:], cntp[:],
                                               start=True, stop=True))
            S.add("vector", lambda v: v.tensor_scalar(srow[:, 3:4], psG[0:1, 0:1],
                                                      1023.5, None, ALU.is_ge))
            S.add("vector", lambda v: v.tensor_tensor(srow[:, 5:6], srow[:, 2:3],
                                                      srow[:, 0:1], ALU.subtract))
            S.add("vector", lambda v: v.tensor_tensor(srow[:, 6:7], srow[:, 5:6],
                                                      srow[:, 3:4], ALU.mult))
            S.add("vector", lambda v: v.tensor_tensor(srow[:, 0:1], srow[:, 0:1],
                                                      srow[:, 6:7], ALU.add))
            S.add("vector", lambda v: v.tensor_tensor(srow[:, 7:8], srow[:, 1:2],
                                                      srow[:, 2:3], ALU.subtract))
            S.add("vector", lambda v: v.tensor_tensor(srow[:, 8:9], srow[:, 7:8],
                                                      srow[:, 3:4], ALU.mult))
            S.add("vector", lambda v: v.tensor_tensor(srow[:, 9:10], srow[:, 2:3],
                                                      srow[:, 8:9], ALU.add))
            S.add("vector", lambda v: v.tensor_copy(srow[:, 1:2], srow[:, 9:10]))

        S.add("tensor", lambda t: t.matmul(psF[:, 0:1], onr[:, :], srow[:, 0:1],
                                           start=True, stop=True))
        S.add("vector", lambda v: v.tensor_copy(midc[:], psF[:, 0:1]))
        S.add("vector", lambda v: v.tensor_scalar(mge[:], scsb[:], midc[:], None,
                                                  ALU.is_ge))
        S.add("tensor", lambda t: t.matmul(psF[:, 0:1], onr[:, :], srow[:, 1:2],
                                           start=True, stop=True))
        S.add("vector", lambda v: v.tensor_copy(midc[:], psF[:, 0:1]))
        S.add("vector", lambda v: v.tensor_scalar(mgt[:], scsb[:], midc[:], 0.0,
                                                  ALU.is_ge, ALU.add,
                                                  accum_out=cntp[:]))
        S.add("vector", lambda v: v.tensor_tensor(meq[:], mge[:], mgt[:], ALU.subtract))
        S.add("tensor", lambda t: t.matmul(psG[0:1, 0:1], onc[:], cntp[:],
                                           start=True, stop=True))
        S.add("vector", lambda v: v.tensor_scalar(srow[:, 10:11], psG[0:1, 0:1],
                                                  -1.0, 1024.0, ALU.mult, ALU.add))
        S.add("tensor", lambda t: t.matmul(psF[:, 0:1], onr[:, :], srow[:, 10:11],
                                           start=True, stop=True))
        S.add("vector", lambda v: v.tensor_copy(needc[:], psF[:, 0:1]))

        def cumsum_steps(src, dst):
            S.add("tensor", lambda t: t.matmul(psG[:, 0:64], trs[:], src[:],
                                               start=True, stop=True))
            S.add("tensor", lambda t: t.matmul(psF[0:1, 0:64], onc[:], src[:],
                                               start=True, stop=True))
            S.add("vector", lambda v: v.tensor_copy(crow1[:], psF[0:1, 0:64]))
            for k in [1, 2, 4, 8, 16, 32]:
                S.add("vector", lambda v, k=k: v.tensor_copy(crow2[:, 0:k], crow1[:, 0:k]))
                S.add("vector", lambda v, k=k: v.tensor_tensor(
                    crow2[:, k:64], crow1[:, k:64], crow1[:, 0:64 - k], ALU.add))
                S.add("vector", lambda v: v.tensor_copy(crow1[:], crow2[:]))
            S.add("vector", lambda v: v.tensor_tensor(crow2[:], crow1[:],
                                                      psF[0:1, 0:64], ALU.subtract))
            S.add("tensor", lambda t: t.matmul(psF[:, 0:64], onr[:, :], crow2[:],
                                               start=True, stop=True))
            S.add("vector", lambda v: v.tensor_copy(exbc[:], psF[:, 0:64]))
            S.add("vector", lambda v: v.tensor_tensor(dst[:], psG[:, 0:64], exbc[:],
                                                      ALU.add))

        cumsum_steps(meq, Ceq)
        S.add("vector", lambda v: v.tensor_scalar(Ceq[:], Ceq[:], needc[:], None,
                                                  ALU.is_le))
        S.add("vector", lambda v: v.tensor_tensor(Ceq[:], Ceq[:], meq[:], ALU.mult))
        S.add("vector", lambda v: v.tensor_tensor(selw[:], mgt[:], Ceq[:], ALU.add))
        cumsum_steps(selw, Csel)
        S.add("vector", lambda v: v.tensor_scalar(afb[:], Csel[:], -1.0, None, ALU.add))
        S.add("vector", lambda v: v.tensor_tensor(afb[:], afb[:], selw[:], ALU.mult))
        S.add("vector", lambda v: v.tensor_scalar(Csel[:], selw[:], -1e6, 1e6,
                                                  ALU.mult, ALU.add))
        S.add("vector", lambda v: v.tensor_tensor(afb[:], afb[:], Csel[:], ALU.add))
        S.add("vector", lambda v: v.tensor_copy(acu[:], afb[:]))
        S.add("scalar_dma", lambda sc: sc.dma_start(out=acolo, in_=acu[:]))

        for f in range(64):
            pf = Pf_[f % 2]
            stf = stf_[f % 2]
            S.add("vector", lambda v, f=f, stf=stf: v.tensor_scalar(
                stf[:, 0:1], tcs[:], float(f * 128), None, ALU.add))
            S.add("vector", lambda v, f=f, stf=stf: v.tensor_copy(
                stf[:, 1:2], scsb[:, f:f + 1]))
            S.add("vector", lambda v, f=f, pf=pf: v.tensor_scalar(
                pf[:], iot[:], afb[:, f:f + 1], 0.0, ALU.subtract, ALU.is_equal))
            S.add("tensor", lambda t, f=f, pf=pf, stf=stf: t.matmul(
                psH1[0:2, 0:512], stf[:], pf[:, 0:512],
                start=(f == 0), stop=(f == 63)))
            S.add("tensor", lambda t, f=f, pf=pf, stf=stf: t.matmul(
                psH2[0:2, 0:512], stf[:], pf[:, 512:1024],
                start=(f == 0), stop=(f == 63)))
        S.add("vector", lambda v: v.tensor_copy(poscosb[:, 0:512], psH1[0:2, 0:512]))
        S.add("vector", lambda v: v.tensor_copy(poscosb[:, 512:1024], psH2[0:2, 0:512]))
        S.add("scalar_dma", lambda sc: sc.dma_start(out=posco, in_=poscosb[:]))
        S.add("scalar_dma", lambda sc: sc.dma_start(
            out=posd.rearrange("a b -> b a"), in_=poscosb[0:1, :]))

        S.add("vector", lambda v: v.tensor_copy(sbf1[:], sbs[:]))
        S.add("tensor", lambda t: t.matmul(psF[:, 0:1], onr[:, :], sbf1[:],
                                           start=True, stop=True))
        S.add("vector", lambda v: v.tensor_copy(slotf[:], psF[:, 0:1]))
        S.add("vector", lambda v: v.tensor_tensor(slotf[:], slotf[:], tcs[:], ALU.add))
        S.add("vector", lambda v: v.tensor_copy(slotu[:], slotf[:]))
        S.add("gpsimd_dma", lambda g: g.indirect_dma_start(
            posg[:], None, posd, bass.IndirectOffsetOnAxis(ap=slotu[:], axis=0)))
        S.add("vector", lambda v: v.tensor_copy(myidx[:], posg[:]))
        S.add("gpsimd_dma", lambda g: g.indirect_dma_start(
            selrow[:], None, lnfull, bass.IndirectOffsetOnAxis(ap=myidx[:], axis=0)))
        S.add("tensor", lambda t: t.transpose(psF[0:1, 0:128], posg[:], idf[:]))
        S.add("vector", lambda v: v.tensor_copy(posrow[:], psF[0:1, 0:128]))
        S.add("scalar_dma", lambda sc: sc.dma_start(out=selo, in_=selrow[:]))

        S.add("vector", lambda v: v.tensor_tensor(sq[:], selrow[:], selrow[:], ALU.mult))
        S.add("vector", lambda v: v.tensor_reduce(rms[:, 0:1], sq[:], AX.X, ALU.add))
        S.add("vector", lambda v: v.tensor_scalar(rms[:, 1:2], rms[:, 0:1],
                                                  1.0 / 2048.0, 1e-6, ALU.mult, ALU.add))
        S.add("vector", lambda v: v.reciprocal(rms[:, 2:3], rms[:, 1:2]))
        S.add("scalar", lambda sc: sc.activation(rms[:, 3:4], rms[:, 2:3], AF.Sqrt))
        S.add("vector", lambda v: v.tensor_scalar(hrow[:], selrow[:], rms[:, 3:4],
                                                  None, ALU.mult))
        for g_ in range(2):
            bank = ptr
            def trg(t, g_=g_):
                for k in range(8):
                    dt = g_ * 8 + k
                    mm = t.transpose(ptr[:, k * 128:(k + 1) * 128],
                                     hrow[:, dt * 128:(dt + 1) * 128], idb[:])
                return mm
            S.add("tensor", trg)
            for k in range(8):
                dt = g_ * 8 + k
                S.add("vector", lambda v, dt=dt, k=k: v.tensor_scalar(
                    hT[:, dt * 128:(dt + 1) * 128], ptr[:, k * 128:(k + 1) * 128],
                    lnc1[:, dt:dt + 1], None, ALU.mult))

        for io in range(32):
            bank = psH1 if io % 2 == 0 else psH2
            def qmm(t, io=io, bank=bank):
                for dm in range(16):
                    mm = t.matmul(bank[:, 0:128],
                                  fws[:, (dm * 32 + io) * 128:(dm * 32 + io + 1) * 128],
                                  hT[:, dm * 128:(dm + 1) * 128],
                                  start=(dm == 0), stop=(dm == 15))
                return mm
            S.add("tensor", qmm)
            S.add("vector", lambda v, io=io, bank=bank: v.tensor_scalar(
                qko[:, io * 128:(io + 1) * 128], bank[:, 0:128],
                bcs[:, io:io + 1], None, ALU.add))

        S.add("tensor", lambda t: t.matmul(psR[:, 0:128], invr[:, :], posrow[:],
                                           start=True, stop=True))
        S.add("vector", lambda v: v.tensor_copy(angs[:], psR[:, 0:128]))
        for nm, off in [("sin", 0.0), ("cos", 0.25)]:
            dstv = sinv if nm == "sin" else cosv
            S.add("vector", lambda v, off=off: v.tensor_scalar(
                u1[:], angs[:], 1.0 / TWOPI, off, ALU.mult, ALU.add))
            S.add("vector", lambda v: v.tensor_copy(ri[:], u1[:]))
            S.add("vector", lambda v: v.tensor_copy(rf[:], ri[:]))
            S.add("vector", lambda v: v.tensor_tensor(sini[:], u1[:], rf[:],
                                                      ALU.subtract))
            S.add("vector", lambda v: v.tensor_scalar(rf[:], sini[:], 0.5, None,
                                                      ALU.is_gt))
            S.add("vector", lambda v: v.tensor_tensor(sini[:], sini[:], rf[:],
                                                      ALU.subtract))
            S.add("vector", lambda v: v.tensor_scalar(rf[:], sini[:], -0.5, None,
                                                      ALU.is_lt))
            S.add("vector", lambda v: v.tensor_tensor(sini[:], sini[:], rf[:],
                                                      ALU.add))
            S.add("vector", lambda v: v.tensor_scalar(sini[:], sini[:], TWOPI,
                                                      None, ALU.mult))
            S.add("scalar", lambda sc, dstv=dstv: sc.activation(dstv[:], sini[:],
                                                                AF.Sin))
        for ti in range(24):
            S.add("tensor", lambda t, ti=ti: t.matmul(
                psR[:, 0:128], rob[:], qko[:, ti * 128:(ti + 1) * 128],
                start=True, stop=True))
            S.add("vector", lambda v, ti=ti: v.tensor_tensor(
                rt1[:], qko[:, ti * 128:(ti + 1) * 128], cosv[:], ALU.mult))
            S.add("vector", lambda v: v.tensor_tensor(rt2[:], psR[:, 0:128], sinv[:],
                                                      ALU.mult))
            S.add("vector", lambda v, ti=ti: v.tensor_tensor(
                qko[:, ti * 128:(ti + 1) * 128], rt1[:], rt2[:], ALU.add))

        def vtr(t):
            for k in range(8):
                mm = t.transpose(ptr[:, k * 128:(k + 1) * 128],
                                 qko[:, (24 + k) * 128:(25 + k) * 128], idb[:])
            return mm
        S.add("tensor", vtr)
        S.add("vector", lambda v: v.tensor_copy(vstok[:], ptr[:]))

        S.add("scalar_dma", lambda sc: sc.dma_start(
            out=qTs.rearrange("(dt p) s -> p dt s", p=128),
            in_=qko[:, 0:2048].rearrange("p (dt s) -> p dt s", s=128)))
        S.add("scalar_dma", lambda sc: sc.dma_start(
            out=kkTs.rearrange("(dt p) s -> p dt s", p=128),
            in_=qko[:, 2048:3072].rearrange("p (dt s) -> p dt s", s=128)))
        S.add("scalar_dma", lambda sc: sc.dma_start(out=vso, in_=vstok[:]))

        @blk.sync
        def _(s_):
            S.emit_one(s_, SEMS, ["sync_dma"])

        @blk.gpsimd
        def _(g):
            S.emit_one(g, SEMS, ["gpsimd", "gpsimd_dma"])

        @blk.vector
        def _(v):
            S.emit_one(v, SEMS, ["vector"])

        @blk.tensor
        def _(t):
            S.emit_one(t, SEMS, ["tensor"])

        @blk.scalar
        def _(sc):
            S.emit_one(sc, SEMS, ["scalar", "scalar_dma"])
            sc.wait_ge(SEMS["sa2"], S.cnt["sa2"])
            sc.wait_ge(SEMS["sg"], S.cnt["sg"])
            sc.wait_ge(SEMS["sv"], S.cnt["sv"])

    return nc

def build_fmain():
    nc = bass.Bass("TRN2", target_bir_lowering=False)
    qTs = nc.dram_tensor("qTs", [16 * 128, 128], BF16, kind="ExternalInput").ap()
    kkT = nc.dram_tensor("kkT", [8 * 128, SSEL], BF16, kind="ExternalInput").ap()
    vfull = nc.dram_tensor("vfull", [SSEL, 1024], BF16, kind="ExternalInput").ap()
    seltok = nc.dram_tensor("seltok", [128, D], BF16, kind="ExternalInput").ap()
    myscol = nc.dram_tensor("myscol", [128, 1], F32, kind="ExternalInput").ap()
    amask = nc.dram_tensor("amask", [128, SSEL], F32, kind="ExternalInput").ap()
    fwo = nc.dram_tensor("fwo", [D, D], F32, kind="ExternalInput").ap()
    fwg = nc.dram_tensor("fwg", [D, IMLP], F32, kind="ExternalInput").ap()
    fwu = nc.dram_tensor("fwu", [D, IMLP], F32, kind="ExternalInput").ap()
    fwd = nc.dram_tensor("fwd", [IMLP, D], F32, kind="ExternalInput").ap()
    ln2bc = nc.dram_tensor("ln2bc", [128, D], F32, kind="ExternalInput").ap()
    identb = nc.dram_tensor("identb", [128, 128], BF16, kind="ExternalInput").ap()
    gato = nc.dram_tensor("gato", [128, D], BF16, kind="ExternalOutput").ap()

    fwob = nc.dram_tensor("fwob", [D, D], BF16).ap()
    fwgb = nc.dram_tensor("fwgb", [D, IMLP], BF16).ap()
    fwub = nc.dram_tensor("fwub", [D, IMLP], BF16).ap()
    fwdb = nc.dram_tensor("fwdb", [IMLP, D], BF16).ap()

    es = ExitStack()
    sb = lambda n, shp, dt: es.enter_context(nc.sbuf_tensor(n, shp, dt))
    ps = lambda n, shp, dt=F32: es.enter_context(nc.psum_tensor(n, shp, dt))
    sem = lambda n: es.enter_context(nc.semaphore(n))
    with es:
        qs = sb("qs", [128, 2048], BF16)
        ks = sb("ks", [128, 8192], BF16)
        vss = sb("vss", [128, 8192], BF16)
        Pb = sb("Pb", [128, 1024], BF16)
        Pe = sb("Pe", [128, 1024], F32)
        PT = sb("PT", [128, 1024], BF16)
        ctxT = sb("ctxT", [128, 2048], BF16)
        sel = sb("sel", [128, D], BF16)
        self32 = sb("self32", [128, D], F32)
        x1tok = sb("x1tok", [128, D], F32)
        sqtok = sb("sqtok", [128, D], F32)
        h2k = sb("h2k", [128, D], BF16)
        h2T = sb("h2T", [128, D], BF16)
        gatok = sb("gatok", [128, D], BF16)
        msk = sb("msk", [128, SSEL], F32)
        lnbc = sb("lnbc", [128, D], F32)
        wsl = [sb(f"wsl{i}", [128, 2048], BF16) for i in range(2)]
        wgl = [sb(f"wgl{i}", [128, 2048], BF16) for i in range(2)]
        wul = [sb(f"wul{i}", [128, 2048], BF16) for i in range(2)]
        wdl = [sb(f"wdl{i}", [128, 2048], BF16) for i in range(2)]
        g1 = sb("g1", [128, 128], F32)
        mt = sb("mt", [128, 128], BF16)
        sm2 = sb("sm2", [128, 16], F32)
        rstdc = sb("rstdc", [128, 1], F32)
        gcol = sb("gcol", [128, 1], F32)
        msc = sb("msc", [128, 1], F32)
        idb = sb("idb", [128, 128], BF16)
        psA = ps("psA", [128, 512])
        psB = ps("psB", [128, 512])
        psC = ps("psC", [128, 512])
        psD = ps("psD", [128, 512])
        psE = ps("psE", [128, 512])
        psFb = ps("psFb", [128, 512])
        ptr = ps("ptr", [128, 1024], BF16)
        BANKS = [psC, psD, psE, psFb]
        SEMS = {n: sem(n) for n in ["sv", "sp", "sa", "sz", "sl", "sa2", "sg"]}
        swp = [sem("sw0"), sem("sw1")]
        smp = sem("smp")
        smv = sem("smv")
        sma = sem("sma")
        sst = sem("sst")
        blk = es.enter_context(nc.Block())

        S = _Sched()

        S.add("gpsimd_dma", lambda g: g.dma_start(out=fwob, in_=fwo))
        S.add("gpsimd_dma", lambda g: g.dma_start(out=fwgb, in_=fwg))
        S.add("gpsimd_dma", lambda g: g.dma_start(out=fwub, in_=fwu))
        S.add("gpsimd_dma", lambda g: g.dma_start(out=fwdb, in_=fwd))
        jobs = [(idb, identb), (msk, amask), (sel, seltok), (msc, myscol),
                (lnbc, ln2bc)]
        for dst, src in jobs:
            S.add("sync_dma", lambda s_, dst=dst, src=src: s_.dma_start(out=dst[:], in_=src))
        S.add("sync_dma", lambda s_: s_.dma_start(
            out=qs.rearrange("p (dt s) -> p dt s", s=128),
            in_=qTs.rearrange("(dt p) s -> p dt s", p=128)))
        S.add("sync_dma", lambda s_: s_.dma_start(
            out=ks.rearrange("p (dt s) -> p dt s", s=SSEL),
            in_=kkT.rearrange("(dt p) s -> p dt s", p=128)))
        S.add("sync_dma", lambda s_: s_.dma_start(
            out=vss.rearrange("p (st dv) -> p st dv", dv=1024),
            in_=vfull.rearrange("(st p) dv -> p st dv", p=128)))

        for h in range(16):
            kv = h // 2
            S.add("tensor", lambda t, h=h, kv=kv: t.matmul(
                psA[:, 0:512], qs[:, h * 128:(h + 1) * 128],
                ks[:, kv * 1024:kv * 1024 + 512], start=True, stop=True))
            S.add("tensor", lambda t, h=h, kv=kv: t.matmul(
                psB[:, 0:512], qs[:, h * 128:(h + 1) * 128],
                ks[:, kv * 1024 + 512:(kv + 1) * 1024], start=True, stop=True))
            S.add("vector", lambda v: v.tensor_tensor(psA[:, 0:512], psA[:, 0:512],
                                                      msk[:, 0:512], ALU.add))
            S.add("vector", lambda v: v.tensor_tensor(psB[:, 0:512], psB[:, 0:512],
                                                      msk[:, 512:1024], ALU.add))
            S.add("vector", lambda v: v.reduce_max(sm2[:, 0:1], psA[:, 0:512], AX.X))
            S.add("vector", lambda v: v.reduce_max(sm2[:, 1:2], psB[:, 0:512], AX.X))
            S.add("vector", lambda v: v.tensor_tensor(sm2[:, 2:3], sm2[:, 0:1],
                                                      sm2[:, 1:2], ALU.max))
            S.add("vector", lambda v: v.tensor_scalar(sm2[:, 3:4], sm2[:, 2:3],
                                                      -ISQ, None, ALU.mult))
            S.add("scalar", lambda sc: sc.activation(Pe[:, 0:512], psA[:, 0:512],
                                                     AF.Exp, bias=sm2[:, 3:4], scale=ISQ))
            S.add("scalar", lambda sc: sc.activation(Pe[:, 512:1024], psB[:, 0:512],
                                                     AF.Exp, bias=sm2[:, 3:4], scale=ISQ))
            S.add("vector", lambda v: v.tensor_reduce(sm2[:, 4:5], Pe[:], AX.X, ALU.add))
            S.add("vector", lambda v: v.reciprocal(sm2[:, 5:6], sm2[:, 4:5]))
            S.add("vector", lambda v: v.tensor_scalar(Pb[:], Pe[:], sm2[:, 5:6],
                                                      None, ALU.mult))
            def trg(t):
                for k in range(8):
                    mm = t.transpose(ptr[:, k * 128:(k + 1) * 128],
                                     Pb[:, k * 128:(k + 1) * 128], idb[:])
                return mm
            S.add("tensor", trg)
            S.add("vector", lambda v: v.tensor_copy(PT[:], ptr[:]))
            def avg(t, kv=kv):
                for st_ in range(8):
                    mm = t.matmul(psC[:, 0:128],
                                  vss[:, st_ * 1024 + kv * 128:st_ * 1024 + (kv + 1) * 128],
                                  PT[:, st_ * 128:(st_ + 1) * 128],
                                  start=(st_ == 0), stop=(st_ == 7))
                return mm
            S.add("tensor", avg)
            S.add("vector", lambda v, h=h: v.tensor_copy(ctxT[:, h * 128:(h + 1) * 128],
                                                         psC[:, 0:128]))

        S.add("vector", lambda v: v.tensor_copy(self32[:], sel[:]))
        for dc in range(16):
            S.add("sync_dma", lambda s_, dc=dc: s_.dma_start(
                out=wsl[dc % 2][:], in_=fwob[ts(dc, 128), :]))
            def womm(t, dc=dc):
                for ch in range(4):
                    mm = t.matmul(BANKS[ch][:, 0:512],
                                  ctxT[:, dc * 128:(dc + 1) * 128],
                                  wsl[dc % 2][:, ch * 512:(ch + 1) * 512],
                                  start=(dc == 0), stop=(dc == 15))
                return mm
            S.add("tensor", womm)
        for ch in range(4):
            S.add("vector", lambda v, ch=ch: v.scalar_tensor_tensor(
                out=x1tok[:, ch * 512:(ch + 1) * 512],
                in0=BANKS[ch][:, 0:512], scalar=1.0,
                in1=self32[:, ch * 512:(ch + 1) * 512], op0=ALU.mult, op1=ALU.add))

        S.add("vector", lambda v: v.tensor_tensor(sqtok[:], x1tok[:], x1tok[:], ALU.mult))
        S.add("vector", lambda v: v.tensor_reduce(sm2[:, 6:7], sqtok[:], AX.X, ALU.add))
        S.add("vector", lambda v: v.tensor_scalar(sm2[:, 7:8], sm2[:, 6:7],
                                                  1.0 / 2048.0, 1e-6, ALU.mult, ALU.add))
        S.add("vector", lambda v: v.reciprocal(sm2[:, 8:9], sm2[:, 7:8]))
        S.add("scalar", lambda sc: sc.activation(rstdc[:], sm2[:, 8:9], AF.Sqrt))
        S.add("vector", lambda v: v.tensor_scalar(sqtok[:], x1tok[:], rstdc[:, 0:1],
                                                  None, ALU.mult))
        S.add("vector", lambda v: v.tensor_tensor(h2k[:], sqtok[:], lnbc[:], ALU.mult))
        for g_ in range(2):
            def trg2(t, g_=g_):
                for k in range(8):
                    dt = g_ * 8 + k
                    mm = t.transpose(ptr[:, k * 128:(k + 1) * 128],
                                     h2k[:, dt * 128:(dt + 1) * 128], idb[:])
                return mm
            S.add("tensor", trg2)
            S.add("vector", lambda v, g_=g_: v.tensor_copy(
                h2T[:, g_ * 1024:(g_ + 1) * 1024], ptr[:]))
        S.add("scalar", lambda sc: sc.activation(gcol[:], msc[:], AF.Sigmoid))

        TOT = dict(S.cnt)

        @blk.sync
        def _(s_):
            S.emit_one(s_, SEMS, ["sync_dma"])
            for I in range(64):
                if I >= 2:
                    s_.wait_ge(smp, 3 * (I - 1))
                s_.dma_start(
                    out=wgl[I % 2].rearrange("p (dm q) -> p dm q", q=128),
                    in_=fwgb.rearrange("(dm p) q -> p dm q", p=128)
                    [:, :, I * 128:(I + 1) * 128]).then_inc(swp[I % 2], 16)
                s_.dma_start(
                    out=wul[I % 2].rearrange("p (dm q) -> p dm q", q=128),
                    in_=fwub.rearrange("(dm p) q -> p dm q", p=128)
                    [:, :, I * 128:(I + 1) * 128]).then_inc(swp[I % 2], 16)
                s_.dma_start(out=wdl[I % 2][:], in_=fwdb[ts(I, 128), :]
                             ).then_inc(swp[I % 2], 16)

        @blk.gpsimd
        def _(g):
            S.emit_one(g, SEMS, ["gpsimd", "gpsimd_dma"])

        @blk.tensor
        def _(t):
            S.emit_one(t, SEMS, ["tensor"])
            for k in ["sv", "sa"]:
                t.wait_ge(SEMS[k], TOT[k])
            for I in range(64):
                t.wait_ge(swp[I % 2], 48 * (I // 2 + 1))
                if I >= 1:
                    t.wait_ge(smv, 2 * I)
                for dm in range(16):
                    mm = t.matmul(psA[:, 0:128], wgl[I % 2][:, dm * 128:(dm + 1) * 128],
                                  h2T[:, dm * 128:(dm + 1) * 128],
                                  start=(dm == 0), stop=(dm == 15))
                mm.then_inc(smp, 1)
                for dm in range(16):
                    mm = t.matmul(psB[:, 0:128], wul[I % 2][:, dm * 128:(dm + 1) * 128],
                                  h2T[:, dm * 128:(dm + 1) * 128],
                                  start=(dm == 0), stop=(dm == 15))
                mm.then_inc(smp, 1)
                t.wait_ge(smv, 2 * I + 2)
                for ch in range(4):
                    mm = t.matmul(BANKS[ch][:, 0:512], mt[:],
                                  wdl[I % 2][:, ch * 512:(ch + 1) * 512],
                                  start=(I == 0), stop=(I == 63))
                mm.then_inc(smp, 1)

        @blk.vector
        def _(v):
            S.emit_one(v, SEMS, ["vector"])
            for k in ["sa", "sp"]:
                v.wait_ge(SEMS[k], TOT[k])
            for I in range(64):
                v.wait_ge(sma, I + 1)
                v.wait_ge(smp, 3 * I + 2)
                v.tensor_tensor(mt[:], psB[:, 0:128], g1[:], ALU.mult).then_inc(smv, 2)
            v.wait_ge(smp, 192)
            for ch in range(4):
                v.scalar_tensor_tensor(
                    out=sqtok[:, ch * 512:(ch + 1) * 512],
                    in0=BANKS[ch][:, 0:512], scalar=1.0,
                    in1=x1tok[:, ch * 512:(ch + 1) * 512], op0=ALU.mult, op1=ALU.add
                    ).then_inc(smv, 1)
            v.wait_ge(smv, 132)
            v.tensor_tensor(sqtok[:], sqtok[:], self32[:], ALU.subtract).then_inc(smv, 1)
            v.wait_ge(smv, 133)
            v.tensor_scalar(sqtok[:], sqtok[:], gcol[:, 0:1], None, ALU.mult
                            ).then_inc(smv, 1)
            v.wait_ge(smv, 134)
            v.tensor_tensor(gatok[:], sqtok[:], self32[:], ALU.add).then_inc(smv, 1)

        @blk.scalar
        def _(sc):
            S.emit_one(sc, SEMS, ["scalar", "scalar_dma"])
            sc.wait_ge(SEMS["sp"], TOT["sp"])
            for I in range(64):
                sc.wait_ge(smp, 3 * I + 1)
                sc.activation(g1[:], psA[:, 0:128], AF.Silu).then_inc(sma, 1)
            sc.wait_ge(smv, 135)
            sc.dma_start(out=gato, in_=gatok[:]).then_inc(sst, 16)
            sc.wait_ge(sst, 16)
            sc.wait_ge(SEMS["sg"], S.cnt["sg"])
            sc.wait_ge(SEMS["sa2"], S.cnt["sa2"])

    return nc


# version 03
# speedup vs baseline: 1.1575x; 1.1575x over previous
import numpy as np
import ml_dtypes
from contextlib import ExitStack
import concourse.bass as bass
import concourse.mybir as mybir
from concourse.bass_utils import run_bass_kernel_spmd

F32 = mybir.dt.float32
BF16 = mybir.dt.bfloat16
U32 = mybir.dt.uint32
I32 = mybir.dt.int32
AF = mybir.ActivationFunctionType
ALU = mybir.AluOpType
AX = mybir.AxisListType
BF = ml_dtypes.bfloat16
ts = bass.ts
ds = bass.ds

T = 8192
D = 2048
HD = 128
SSEL = 1024
IMLP = 8192
NEG = -1.0e30
RTH = 10000.0
ISQ = float(1.0 / np.sqrt(np.float32(128.0)))
NCORE = 8
TWOPI = float(2.0 * np.pi)

_CACHE = {}


def _cached(key, fn):
    if key not in _CACHE:
        _CACHE[key] = fn()
    return _CACHE[key]


def build_ln(NT=1024):
    nc = bass.Bass("TRN2", target_bir_lowering=False)
    nt = NT // 128
    xs = nc.dram_tensor("xs", [NT, D], F32, kind="ExternalInput").ap()
    wr = nc.dram_tensor("wr", [1, D], F32, kind="ExternalInput").ap()
    onesf = nc.dram_tensor("onesf", [1, 128], F32, kind="ExternalInput").ap()
    identb = nc.dram_tensor("identb", [128, 128], BF16, kind="ExternalInput").ap()
    lntok = nc.dram_tensor("lntok", [NT, D], BF16, kind="ExternalOutput").ap()
    lnT = nc.dram_tensor("lnT", [16 * 128, NT], BF16, kind="ExternalOutput").ap()
    sco = nc.dram_tensor("sco", [128, nt], F32, kind="ExternalOutput").ap()
    es = ExitStack()
    sb = lambda n, shp, dt: es.enter_context(nc.sbuf_tensor(n, shp, dt))
    ps = lambda n, shp, dt=F32: es.enter_context(nc.psum_tensor(n, shp, dt))
    sem = lambda n: es.enter_context(nc.semaphore(n))
    with es:
        xt = sb("xt", [128, D], F32)
        xt2 = sb("xt2", [128, D], F32)
        tok = sb("tok", [128, D], BF16)
        tok2 = sb("tok2", [128, D], BF16)
        lnb = sb("lnb", [128, 2048], BF16)
        lnb2 = sb("lnb2", [128, 2048], BF16)
        sm = sb("sm", [128, 64], F32)
        wrbc = sb("wrbc", [128, D], F32)
        wrsum = sb("wrsum", [128, 1], F32)
        wrrow = sb("wrrow", [1, D], F32)
        onesfs = sb("onesfs", [1, 128], F32)
        idents = sb("idents", [128, 128], BF16)
        sct = sb("sct", [128, D], F32)
        psa = ps("psa", [128, 512])
        ptr = ps("ptr", [128, 1024], BF16)
        ptr2 = ps("ptr2", [128, 1024], BF16)
        slc = sem("slc"); sl0 = sem("sl0"); sl1 = sem("sl1")
        sv = sem("sv"); sa = sem("sa"); sp = sem("sp"); se = sem("se")
        st0 = sem("st0"); st1 = sem("st1"); stf = sem("stf")
        blk = es.enter_context(nc.Block())
        slp = [sl0, sl1]
        stp = [st0, st1]
        xb = [xt, xt2]
        tb = [tok, tok2]
        lb = [lnb, lnb2]
        NV = 9

        @blk.sync
        def _(s_):
            s_.dma_start(out=wrrow[:], in_=wr).then_inc(slc, 16)
            s_.dma_start(out=onesfs[:], in_=onesf).then_inc(slc, 16)
            s_.dma_start(out=idents[:], in_=identb).then_inc(slc, 16)
            for i in range(nt):
                if i >= 2:
                    s_.wait_ge(stp[i % 2], 32 * (i // 2))
                s_.dma_start(out=xb[i % 2][:], in_=xs[ts(i, 128), :]).then_inc(slp[i % 2], 16)

        @blk.tensor
        def _(t):
            t.wait_ge(slc, 48)
            for j in range(4):
                if j:
                    t.wait_ge(sv, j)
                t.matmul(psa[:, 0:512], onesfs[:, :], wrrow[:, j * 512:(j + 1) * 512],
                         start=True, stop=True).then_inc(sp, 1)
            for i in range(nt):
                t.wait_ge(sv, 6 + i * NV + 5)
                for g in range(2):
                    if i * 2 + g >= 2:
                        t.wait_ge(se, i * 2 + g - 1)
                    bank = ptr if g == 0 else ptr2
                    for k in range(8):
                        dt = g * 8 + k
                        mm = t.transpose(bank[:, k * 128:(k + 1) * 128],
                                         tb[i % 2][:, dt * 128:(dt + 1) * 128], idents[:])
                    mm.then_inc(sp, 1)

        @blk.vector
        def _(v):
            for j in range(4):
                v.wait_ge(sp, j + 1)
                v.tensor_copy(wrbc[:, j * 512:(j + 1) * 512], psa[:, 0:512]).then_inc(sv, 1)
            v.wait_ge(sv, 4)
            v.tensor_reduce(wrsum[:], wrbc[:], AX.X, ALU.add).then_inc(sv, 2)
            for i in range(nt):
                base = 6 + i * NV
                x = xb[i % 2]
                v.wait_ge(slp[i % 2], 16 * (i // 2 + 1))
                for q in range(4):
                    mm = v.bn_stats(sm[:, 6 * q:6 * q + 6], x[:, 512 * q:512 * (q + 1)])
                mm.then_inc(sv, 1)
                v.wait_ge(sv, base + 1)
                v.bn_aggr(sm[:, 24:26], sm[:, 0:24]).then_inc(sv, 1)
                v.wait_ge(sv, base + 2)
                v.tensor_scalar(sm[:, 26:27], sm[:, 25:26], 1e-5, None, ALU.add
                                ).then_inc(sv, 1)
                v.wait_ge(sv, base + 3)
                v.reciprocal(sm[:, 27:28], sm[:, 26:27]).then_inc(sv, 1)
                v.wait_ge(sa, i + 1)
                if i >= 2:
                    v.wait_ge(stp[i % 2], 32 * (i // 2))
                v.tensor_scalar(tb[i % 2][:], x[:], sm[:, 24:25], sm[:, 28:29],
                                ALU.subtract, ALU.mult)
                v.tensor_tensor(sct[:], x[:], wrbc[:], ALU.mult).then_inc(sv, 1)
                v.wait_ge(sv, base + 5)
                v.tensor_reduce(sm[:, 30:31], sct[:], AX.X, ALU.add).then_inc(sv, 1)
                v.wait_ge(sv, base + 6)
                v.tensor_tensor(sm[:, 32:33], sm[:, 24:25], wrsum[:], ALU.mult
                                ).then_inc(sv, 1)
                v.wait_ge(sv, base + 7)
                v.tensor_tensor(sm[:, 33:34], sm[:, 30:31], sm[:, 32:33], ALU.subtract
                                ).then_inc(sv, 1)
                v.wait_ge(sv, base + 8)
                v.tensor_tensor(sm[:, 40 + i:41 + i], sm[:, 33:34],
                                sm[:, 28:29], ALU.mult).then_inc(sv, 1)
                for g in range(2):
                    v.wait_ge(sp, 4 + i * 2 + g + 1)
                    bank = ptr if g == 0 else ptr2
                    v.tensor_copy(
                        lb[i % 2].rearrange("p (dt q) -> p dt q", q=128)[
                            :, g * 8:(g + 1) * 8, :],
                        bank[:].rearrange("p (k q) -> p k q", q=128)).then_inc(se, 1)

        @blk.scalar
        def _(sc):
            for i in range(nt):
                base = 6 + i * NV
                sc.wait_ge(sv, base + 4)
                sc.activation(sm[:, 28:29], sm[:, 27:28], AF.Sqrt).then_inc(sa, 1)
                sc.wait_ge(sv, base + 5)
                sc.dma_start(out=lntok[ts(i, 128), :], in_=tb[i % 2][:]
                             ).then_inc(stp[i % 2], 16)
                sc.wait_ge(se, (i + 1) * 2)
                sc.dma_start(out=lnT.rearrange("(dt p) n -> p dt n", p=128)[:, :, ts(i, 128)],
                             in_=lb[i % 2].rearrange("p (dt q) -> p dt q", q=128)
                             ).then_inc(stp[i % 2], 16)
            sc.wait_ge(sv, 6 + nt * NV)
            sc.dma_start(out=sco[:, :], in_=sm[:, 40:40 + nt]).then_inc(stf, 16)
            sc.wait_ge(st0, 32 * ((nt + 1) // 2))
            sc.wait_ge(st1, 32 * (nt // 2))
            sc.wait_ge(stf, 16)

    return nc


def build_att(TT, nq, nkv, hbias):
    nc = bass.Bass("TRN2", target_bir_lowering=False)
    nch = TT // 512
    ntt = TT // 128
    kvmap = [h * nkv // nq for h in range(nq)]
    nio = nq + 2 * nkv

    lnT = nc.dram_tensor("lnT", [16, 128, TT], BF16, kind="ExternalInput").ap()
    wqkv = nc.dram_tensor("wqkv", [D, nio * 128], F32, kind="ExternalInput").ap()
    if hbias:
        bqc = nc.dram_tensor("bqc", [128, nq], F32, kind="ExternalInput").ap()
        bkc = nc.dram_tensor("bkc", [128, nkv], F32, kind="ExternalInput").ap()
        bvc = nc.dram_tensor("bvc", [128, nkv], F32, kind="ExternalInput").ap()
    identb = nc.dram_tensor("identb", [128, 128], BF16, kind="ExternalInput").ap()
    onesb = nc.dram_tensor("onesb", [128, 1], BF16, kind="ExternalInput").ap()
    onesf = nc.dram_tensor("onesf", [1, 128], F32, kind="ExternalInput").ap()
    dmask = nc.dram_tensor("dmask", [4, 128, 512], F32, kind="ExternalInput").ap()
    ctxo = nc.dram_tensor("ctxo", [nq * 128, TT], BF16, kind="ExternalOutput").ap()

    wqkvb = nc.dram_tensor("wqkvb", [D, nio * 128], BF16).ap()

    def nkj(qc):
        return 4 * (qc + 1)

    def cumS(h, qc):
        tot = 0
        for hh in range(nq):
            for qq in range(nch):
                if (hh, qq) < (h, qc):
                    tot += nkj(qq)
        return tot

    es = ExitStack()
    sb = lambda n, shp, dt: es.enter_context(nc.sbuf_tensor(n, shp, dt))
    ps = lambda n, shp, dt=F32: es.enter_context(nc.psum_tensor(n, shp, dt))
    sem = lambda n: es.enter_context(nc.semaphore(n))
    with es:
        lnc = sb("lnc", [128, 16 * 512], BF16)
        lnc2 = sb("lnc2", [128, 16 * 512], BF16)
        wqs = sb("wqs", [128, 16 * nio * 128], BF16)
        qT = sb("qT", [128, nq * TT], BF16)
        kT = sb("kT", [128, nkv * TT], BF16)
        vT = sb("vT", [128, nkv * TT], BF16)
        vtok = sb("vtok", [128, nkv * TT], BF16)
        prb0 = sb("prb0", [128, 512], BF16)
        prb1 = sb("prb1", [128, 512], BF16)
        rrow = sb("rrow", [1, 512], F32)
        rbc = sb("rbc", [128, 512], F32)
        ctxc = sb("ctxc", [128, nq * 512], BF16)
        idents = sb("idents", [128, 128], BF16)
        onec = sb("onec", [128, 1], BF16)
        onesfs = sb("onesfs", [1, 128], F32)
        masks = sb("masks", [128, 2048], F32)
        bqs = sb("bqs", [128, max(nq, 1)], F32)
        bks = sb("bks", [128, max(nkv, 1)], F32)
        bvs = sb("bvs", [128, max(nkv, 1)], F32)
        psA = ps("psA", [128, 512])
        psB = ps("psB", [128, 512])
        psC = ps("psC", [128, 512])
        psD = ps("psD", [128, 512])
        psE = ps("psE", [128, 512])
        ptr = ps("ptr", [128, 1024], BF16)
        ptr2 = ps("ptr2", [128, 1024], BF16)
        sg = sem("sg"); slw = sem("slw"); sl0 = sem("sl0"); sl1 = sem("sl1")
        s_qp = sem("s_qp"); s_qv = sem("s_qv")
        s_tr = sem("s_tr"); s_te = sem("s_te"); s_s = sem("s_s"); s_d = sem("s_d")
        s_e = sem("s_e"); s_av = sem("s_av"); s_r = sem("s_r"); s_b = sem("s_b")
        s_v2 = sem("s_v2"); st0 = sem("st0"); st1 = sem("st1")
        blk = es.enter_context(nc.Block())
        slp = [sl0, sl1]
        stp = [st0, st1]
        lncb = [lnc, lnc2]
        prb = [prb0, prb1]
        NLD = 7 + (3 if hbias else 0) + 16

        @blk.gpsimd
        def _(g):
            g.dma_start(out=wqkvb, in_=wqkv).then_inc(sg, 16)

        @blk.sync
        def _(s_):
            s_.dma_start(out=idents[:], in_=identb).then_inc(slw, 16)
            s_.dma_start(out=onec[:], in_=onesb).then_inc(slw, 16)
            s_.dma_start(out=onesfs[:], in_=onesf).then_inc(slw, 16)
            for v_ in range(4):
                s_.dma_start(out=masks[:, v_ * 512:(v_ + 1) * 512], in_=dmask[v_, :, :]
                             ).then_inc(slw, 16)
            if hbias:
                s_.dma_start(out=bqs[:], in_=bqc).then_inc(slw, 16)
                s_.dma_start(out=bks[:], in_=bkc).then_inc(slw, 16)
                s_.dma_start(out=bvs[:], in_=bvc).then_inc(slw, 16)
            s_.wait_ge(sg, 16)
            for dm in range(16):
                s_.dma_start(out=wqs[:, dm * nio * 128:(dm + 1) * nio * 128],
                             in_=wqkvb[ts(dm, 128), :]).then_inc(slw, 16)
            for c in range(nch):
                if c >= 2:
                    s_.wait_ge(s_qp, (c - 1) * nio)
                s_.dma_start(out=lncb[c % 2].rearrange("p (dt q) -> p dt q", q=512),
                             in_=lnT[:, :, ts(c, 512)].rearrange("dt p q -> p dt q")
                             ).then_inc(slp[c % 2], 16)

        @blk.tensor
        def _(t):
            t.wait_ge(slw, NLD * 16)
            for c in range(nch):
                t.wait_ge(slp[c % 2], 16 * (c // 2 + 1))
                for io in range(nio):
                    u = c * nio + io
                    if u >= 2:
                        t.wait_ge(s_qv, u - 1)
                    bank = psA if io % 2 == 0 else psB
                    for dm in range(16):
                        mm = t.matmul(bank[:, 0:512],
                                      wqs[:, (dm * nio + io) * 128:(dm * nio + io + 1) * 128],
                                      lncb[c % 2][:, dm * 512:(dm + 1) * 512],
                                      start=(dm == 0), stop=(dm == 15))
                        if dm == 15:
                            mm.then_inc(s_qp, 1)
            t.wait_ge(s_qv, nch * nio)
            ngr = (nkv * ntt) // 8
            for g in range(ngr):
                if g >= 2:
                    t.wait_ge(s_te, g - 1)
                bank = ptr if g % 2 == 0 else ptr2
                for k in range(8):
                    i = g * 8 + k
                    mm = t.transpose(bank[:, k * 128:(k + 1) * 128],
                                     vT[:, i * 128:(i + 1) * 128], idents[:])
                mm.then_inc(s_tr, 1)
            t.wait_ge(s_te, ngr)
            for h in range(nq):
                kv = kvmap[h]
                n = h * nch
                for qc in range(nch):
                    npair = h * nch + qc
                    cs = cumS(h, qc)
                    nk = nkj(qc)
                    for kj in range(nk):
                        diag = kj >= 4 * qc
                        bank = psA if kj % 2 == 0 else psB
                        if cs + kj >= 2:
                            t.wait_ge(s_e, cs + kj - 1)
                        t.matmul(bank[:, 0:512],
                                 kT[:, kv * TT + kj * 128:kv * TT + (kj + 1) * 128],
                                 qT[:, h * TT + qc * 512:h * TT + (qc + 1) * 512],
                                 start=True, stop=True).then_inc(s_s, 1)
                        t.wait_ge(s_e, cs + kj + 1)
                        if kj == 0:
                            t.wait_ge(s_v2, 2 * npair)
                        t.matmul(psC[:, 0:512],
                                 vtok[:, kv * TT + kj * 128:kv * TT + (kj + 1) * 128],
                                 prb[kj % 2][:], start=(kj == 0), stop=(kj == nk - 1))
                        if kj == 0:
                            t.wait_ge(s_r, npair)
                        t.matmul(psD[0:1, 0:512], onec[:], prb[kj % 2][:],
                                 start=(kj == 0), stop=(kj == nk - 1)).then_inc(s_av, 1)
                    t.wait_ge(s_r, npair + 1)
                    if npair >= 1:
                        t.wait_ge(s_v2, 2 * npair - 1)
                    t.matmul(psE[:, 0:512], onesfs[:, :], rrow[:, 0:512],
                             start=True, stop=True).then_inc(s_b, 1)

        @blk.vector
        def _(v):
            for c in range(nch):
                for io in range(nio):
                    u = c * nio + io
                    v.wait_ge(s_qp, u + 1)
                    bank = psA if io % 2 == 0 else psB
                    if io < nq:
                        dst = qT[:, io * TT + c * 512:io * TT + (c + 1) * 512]
                        bias = bqs[:, io:io + 1] if hbias else None
                    elif io < nq + nkv:
                        kv = io - nq
                        dst = kT[:, kv * TT + c * 512:kv * TT + (c + 1) * 512]
                        bias = bks[:, kv:kv + 1] if hbias else None
                    else:
                        kv = io - nq - nkv
                        dst = vT[:, kv * TT + c * 512:kv * TT + (c + 1) * 512]
                        bias = bvs[:, kv:kv + 1] if hbias else None
                    if bias is not None:
                        v.tensor_scalar(dst, bank[:, 0:512], bias, None, ALU.add
                                        ).then_inc(s_qv, 1)
                    else:
                        v.tensor_copy(dst, bank[:, 0:512]).then_inc(s_qv, 1)
            for g in range((nkv * ntt) // 8):
                v.wait_ge(s_tr, g + 1)
                bank = ptr if g % 2 == 0 else ptr2
                v.tensor_copy(vtok[:, g * 1024:(g + 1) * 1024], bank[:]).then_inc(s_te, 1)
            for h in range(nq):
                for qc in range(nch):
                    npair = h * nch + qc
                    cs = cumS(h, qc)
                    nk = nkj(qc)
                    for kj in range(4 * qc, nk):
                        bank = psA if kj % 2 == 0 else psB
                        vm = kj - 4 * qc
                        v.wait_ge(s_s, cs + kj + 1)
                        v.tensor_tensor(bank[:, 0:512], bank[:, 0:512],
                                        masks[:, vm * 512:(vm + 1) * 512], ALU.add
                                        ).then_inc(s_d, 1)
                    v.wait_ge(s_av, cs + nk)
                    v.reciprocal(rrow[:, 0:512], psD[0:1, 0:512]).then_inc(s_r, 1)
                    v.wait_ge(s_b, npair + 1)
                    v.tensor_copy(rbc[:], psE[:, 0:512]).then_inc(s_v2, 1)
                    v.wait_ge(s_v2, 2 * npair + 1)
                    if npair >= 2:
                        v.wait_ge(stp[npair % 2], 16 * (npair // 2))
                    v.tensor_tensor(ctxc[:, (npair % 2) * 512:(npair % 2 + 1) * 512],
                                    psC[:, 0:512], rbc[:], ALU.mult).then_inc(s_v2, 1)

        @blk.scalar
        def _(sc):
            for h in range(nq):
                for qc in range(nch):
                    npair = h * nch + qc
                    cs = cumS(h, qc)
                    nk = nkj(qc)
                    for kj in range(nk):
                        diag = kj >= 4 * qc
                        bank = psA if kj % 2 == 0 else psB
                        if diag:
                            sc.wait_ge(s_d, 4 * npair + (kj - 4 * qc) + 1)
                        else:
                            sc.wait_ge(s_s, cs + kj + 1)
                        if cs + kj >= 2:
                            sc.wait_ge(s_av, cs + kj - 1)
                        sc.activation(prb[kj % 2][:], bank[:, 0:512], AF.Exp,
                                      scale=ISQ).then_inc(s_e, 1)
                    sc.wait_ge(s_v2, 2 * npair + 2)
                    sc.dma_start(out=ctxo[ts(h, 128), ts(qc, 512)],
                                 in_=ctxc[:, (npair % 2) * 512:(npair % 2 + 1) * 512]
                                 ).then_inc(stp[npair % 2], 16)
            npt = nq * nch
            sc.wait_ge(st0, 16 * ((npt + 1) // 2))
            sc.wait_ge(st1, 16 * (npt // 2))

    return nc


def _consts():
    C = {}
    C["identb"] = np.eye(128).astype(BF)
    C["identf"] = np.eye(128, dtype=np.float32)
    C["onesf"] = np.ones((1, 128), np.float32)
    C["onesb"] = np.ones((128, 1), BF)
    C["onescf"] = np.ones((128, 1), np.float32)
    dm = np.zeros((4, 128, 512), np.float32)
    for v in range(4):
        kj = v * 128 + np.arange(128)[:, None]
        qi = np.arange(512)[None, :]
        dm[v] = np.where(kj > qi, NEG, 0.0)
    C["dmask"] = dm
    rot = np.zeros((128, 128), np.float32)
    for i in range(64):
        rot[64 + i, i] = -1.0
        rot[i, 64 + i] = 1.0
    C["rotb"] = rot.astype(BF)
    C["iota1k"] = np.broadcast_to(np.arange(1024, dtype=np.float32), (128, 1024)).copy()
    C["trif"] = (np.arange(128)[:, None] <= np.arange(128)[None, :]).astype(np.float32)
    C["tcolf"] = np.arange(128, dtype=np.float32)[:, None]
    C["invfs"] = np.concatenate(
        [1.0 / RTH ** (np.arange(0, 128, 2) / 128.0)] * 2)[None, :].astype(np.float32)
    return C


def _run(nc, maps):
    return run_bass_kernel_spmd(nc, maps, core_ids=list(range(NCORE)))


def kernel(**inputs):
    f = lambda k: np.ascontiguousarray(np.asarray(inputs[k], np.float32))
    C = _consts()
    x0 = f("L0_attn_a_x").reshape(T, D)
    wrr = f("L0_ffn_w_router").reshape(1, D)
    p_ln = _cached("ln", lambda: build_ln(1024))
    p_atta = _cached("atta", lambda: build_att(T, 2, 1, False))
    p_attb = _cached("attb", lambda: build_att(T, 2, 2, True))
    p_proj = _cached("proj", lambda: build_proj(1024))
    p_fsel = _cached("fsel", lambda: build_fsel())
    p_fmain = _cached("fmain", lambda: build_fmain())
    p_scat = _cached("scat", lambda: build_scat(1024))
    p_cast = _cached("cast", lambda: build_cast())

    def L_ln(x):
        maps = [{"xs": np.ascontiguousarray(x[c * 1024:(c + 1) * 1024]), "wr": wrr,
                 "onesf": C["onesf"], "identb": C["identb"]} for c in range(NCORE)]
        r = _run(p_ln, maps)
        lntok = np.concatenate([np.asarray(r.results[c]["lntok"]) for c in range(NCORE)], 0)
        lnT = np.concatenate([np.asarray(r.results[c]["lnT"]).reshape(16, 128, 1024)
                              for c in range(NCORE)], 2)
        sco = np.concatenate([np.asarray(r.results[c]["sco"]) for c in range(NCORE)], 1)
        return lntok, lnT, sco

    def L_att(prog, lnT, mk):
        maps = []
        for c in range(NCORE):
            m = mk(c)
            m.update({"lnT": lnT, "identb": C["identb"], "onesb": C["onesb"],
                      "onesf": C["onesf"], "dmask": C["dmask"]})
            maps.append(m)
        r = _run(prog, maps)
        return np.concatenate([np.asarray(r.results[c]["ctxo"]) for c in range(NCORE)], 0)

    def L_proj(ctxT, xprev, W, brow):
        onesbr = np.ones((1, 128), BF)
        maps = [{"cT": np.ascontiguousarray(ctxT[:, c * 1024:(c + 1) * 1024]
                                            ).reshape(16, 128, 1024),
                 "xprev": np.ascontiguousarray(xprev[c * 1024:(c + 1) * 1024]),
                 "Wb": W, "brow": brow, "onesbr": onesbr} for c in range(NCORE)]
        r = _run(p_proj, maps)
        return np.concatenate([np.asarray(r.results[c]["xnew"]) for c in range(NCORE)], 0)

    fwq, fwk, fwv = f("L0_ffn_wq"), f("L0_ffn_wk"), f("L0_ffn_wv")
    fw3 = np.ascontiguousarray(np.concatenate([fwq, fwk, fwv], 1))
    biasc = np.ascontiguousarray(np.concatenate(
        [f("L0_ffn_bq").ravel(), f("L0_ffn_bk").ravel(), f("L0_ffn_bv").ravel()]
        ).reshape(32, 128).T)
    ln1c = np.ascontiguousarray(f("L0_ffn_ln1_w").reshape(16, 128).T)
    ln2bc = np.broadcast_to(f("L0_ffn_ln2_w").reshape(1, D), (128, D)).copy()
    fwg, fwu = f("L0_ffn_w_gate"), f("L0_ffn_w_up")
    fwd, fwo = f("L0_ffn_w_down"), f("L0_ffn_wo")
    woa, wpb = f("L0_attn_a_Wo"), f("L0_attn_b_w_proj")
    allf = np.concatenate([fwg.ravel(), fwu.ravel(), fwd.ravel(), fw3.ravel(),
                           fwo.ravel(), woa.ravel(), wpb.ravel()]).reshape(8, 1088, 8192)
    rc = _run(p_cast, [{"cin": np.ascontiguousarray(allf[c])} for c in range(NCORE)])
    allb = np.concatenate([np.asarray(rc.results[c]["cout"]).ravel()
                           for c in range(NCORE)])
    E = 2048 * 8192
    fwgb, fwub = allb[0:E].reshape(D, IMLP), allb[E:2 * E].reshape(D, IMLP)
    fwdb = allb[2 * E:3 * E].reshape(IMLP, D)
    fw3b = allb[3 * E:3 * E + D * 4096].reshape(D, 4096)
    o = 3 * E + D * 4096
    fwob = allb[o:o + D * D].reshape(D, D)
    woab = allb[o + D * D:o + 2 * D * D].reshape(D, D)
    wpbb = allb[o + 2 * D * D:o + 3 * D * D].reshape(D, D)
    amasks = [np.where(np.arange(SSEL)[None, :] <= (128 * c + np.arange(128))[:, None],
                       0.0, NEG).astype(np.float32) for c in range(NCORE)]

    def ffn(xin, lntok, sco):
        maps = [{"sco": sco, "lnfull": lntok, "fw3b": fw3b, "ln1c": ln1c,
                 "biasc": biasc, "sbase": np.array([[c * 128]], np.uint32),
                 "invfs": C["invfs"], "identf": C["identf"], "identb": C["identb"],
                 "rotb": C["rotb"], "iota1k": C["iota1k"], "trif": C["trif"],
                 "tcolf": C["tcolf"], "onescf": C["onescf"], "onesf": C["onesf"]}
                for c in range(NCORE)]
        rs = _run(p_fsel, maps)
        kkTf = np.concatenate([np.asarray(rs.results[c]["kkTs"]) for c in range(NCORE)], 1)
        vfull = np.concatenate([np.asarray(rs.results[c]["vso"]) for c in range(NCORE)], 0)
        posco = np.asarray(rs.results[0]["posco"])
        acolo = np.asarray(rs.results[0]["acolo"])
        maps = [{"qTs": np.asarray(rs.results[c]["qTs"]), "kkT": kkTf, "vfull": vfull,
                 "seltok": np.asarray(rs.results[c]["selo"]),
                 "myscol": np.ascontiguousarray(posco[1, c * 128:(c + 1) * 128]
                                                ).reshape(128, 1).astype(np.float32),
                 "amask": amasks[c], "fwob": fwob, "fwgb": fwgb, "fwub": fwub,
                 "fwdb": fwdb, "ln2bc": ln2bc, "identb": C["identb"]}
                for c in range(NCORE)]
        rm = _run(p_fmain, maps)
        gated = np.concatenate([np.asarray(rm.results[c]["gato"]) for c in range(NCORE)], 0)
        maps = [{"xprev": np.ascontiguousarray(xin[c * 1024:(c + 1) * 1024]),
                 "lntk": np.ascontiguousarray(lntok[c * 1024:(c + 1) * 1024]),
                 "gated": gated,
                 "acol": np.ascontiguousarray(acolo[:, 8 * c:8 * c + 8])}
                for c in range(NCORE)]
        rr = _run(p_scat, maps)
        return np.concatenate([np.asarray(rr.results[c]["xnew"]) for c in range(NCORE)], 0)

    _, lnT0, _ = L_ln(x0)
    wq, wk, wv = f("L0_attn_a_Wq"), f("L0_attn_a_Wk"), f("L0_attn_a_Wv")
    ctxa = L_att(p_atta, lnT0, lambda c: {"wqkv": np.ascontiguousarray(np.concatenate(
        [wq[:, 256 * c:256 * (c + 1)], wk[:, 128 * (c // 2):128 * (c // 2 + 1)],
         wv[:, 128 * (c // 2):128 * (c // 2 + 1)]], 1))})
    x1 = L_proj(ctxa, x0, woab, f("L0_attn_a_bo").reshape(1, D))

    lntok1, _, sco1 = L_ln(x1)
    x2 = ffn(x1, lntok1, sco1)

    _, lnT2, _ = L_ln(x2)
    wab, bab = f("L0_attn_b_w_attn"), f("L0_attn_b_b_attn").ravel()

    def mk_b(c):
        sl = lambda o: wab[:, o + 256 * c:o + 256 * (c + 1)]
        bsl = lambda o: np.ascontiguousarray(
            bab[o + 256 * c:o + 256 * (c + 1)].reshape(2, 128).T)
        return {"wqkv": np.ascontiguousarray(np.concatenate(
                    [sl(0), sl(2048), sl(4096)], 1)),
                "bqc": bsl(0), "bkc": bsl(2048), "bvc": bsl(4096)}
    ctxb = L_att(p_attb, lnT2, mk_b)
    x3 = L_proj(ctxb, x2, wpbb, f("L0_attn_b_b_proj").reshape(1, D))

    lntok3, _, sco3 = L_ln(x3)
    x4 = ffn(x3, lntok3, sco3)
    return np.ascontiguousarray(x4.reshape(1, T, D).astype(np.float32))


def build_proj(NT=1024):
    nc = bass.Bass("TRN2", target_bir_lowering=False)
    nt = NT // 128
    cT = nc.dram_tensor("cT", [16, 128, NT], BF16, kind="ExternalInput").ap()
    xprev = nc.dram_tensor("xprev", [NT, D], F32, kind="ExternalInput").ap()
    brow = nc.dram_tensor("brow", [1, D], F32, kind="ExternalInput").ap()
    onesbr = nc.dram_tensor("onesbr", [1, 128], BF16, kind="ExternalInput").ap()
    xnew = nc.dram_tensor("xnew", [NT, D], F32, kind="ExternalOutput").ap()
    Wb = nc.dram_tensor("Wb", [D, D], BF16, kind="ExternalInput").ap()
    bb = nc.dram_tensor("bb", [1, D], BF16).ap()
    es = ExitStack()
    sb = lambda n, shp, dt: es.enter_context(nc.sbuf_tensor(n, shp, dt))
    ps = lambda n, shp, dt=F32: es.enter_context(nc.psum_tensor(n, shp, dt))
    sem = lambda n: es.enter_context(nc.semaphore(n))
    with es:
        cts = sb("cts", [128, 16 * NT], BF16)
        ws = sb("ws", [128, 16 * 2048], BF16)
        xt = sb("xt", [128, D], F32)
        xt2 = sb("xt2", [128, D], F32)
        xo = sb("xo", [128, D], F32)
        xo2 = sb("xo2", [128, D], F32)
        bbs = sb("bbs", [1, D], BF16)
        onbr = sb("onbr", [1, 128], BF16)
        psA = ps("psA", [128, 512])
        psB = ps("psB", [128, 512])
        sg = sem("sg"); slw = sem("slw"); sl0 = sem("sl0"); sl1 = sem("sl1")
        sp = sem("sp"); sv = sem("sv"); st0 = sem("st0"); st1 = sem("st1")
        sw2 = sem("sw2")
        blk = es.enter_context(nc.Block())
        slp = [sl0, sl1]
        stp = [st0, st1]
        xb = [xt, xt2]
        xob = [xo, xo2]

        @blk.gpsimd
        def _(g):
            g.dma_start(out=bb, in_=brow).then_inc(sg, 16)

        @blk.sync
        def _(s_):
            s_.dma_start(out=onbr[:], in_=onesbr).then_inc(slw, 16)
            s_.dma_start(out=cts.rearrange("p (dt n) -> p dt n", n=NT),
                         in_=cT.rearrange("dt p n -> p dt n")).then_inc(slw, 16)
            s_.wait_ge(sg, 16)
            s_.dma_start(out=bbs[:], in_=bb).then_inc(slw, 16)
            for dm in range(0, 16, 2):
                s_.dma_start(out=ws[:, dm * 2048:(dm + 1) * 2048],
                             in_=Wb[ts(dm, 128), :]).then_inc(slw, 16)
            for i in range(nt):
                if i >= 2:
                    s_.wait_ge(stp[i % 2], 16 * (i // 2))
                s_.dma_start(out=xb[i % 2][:], in_=xprev[ts(i, 128), :]
                             ).then_inc(slp[i % 2], 16)

        @blk.tensor
        def _(t):
            t.wait_ge(slw, 11 * 16)
            t.wait_ge(sw2, 8 * 16)
            for i in range(nt):
                for ch in range(4):
                    u = i * 4 + ch
                    if u >= 2:
                        t.wait_ge(sv, u - 1)
                    bank = psA if u % 2 == 0 else psB
                    for dm in range(16):
                        t.matmul(bank[:, 0:512],
                                 cts[:, dm * NT + i * 128:dm * NT + (i + 1) * 128],
                                 ws[:, dm * 2048 + ch * 512:dm * 2048 + (ch + 1) * 512],
                                 start=(dm == 0), stop=False)
                    t.matmul(bank[:, 0:512], onbr[:, :],
                             bbs[:, ch * 512:(ch + 1) * 512],
                             start=False, stop=True).then_inc(sp, 1)

        @blk.vector
        def _(v):
            for i in range(nt):
                v.wait_ge(slp[i % 2], 16 * (i // 2 + 1))
                if i >= 2:
                    v.wait_ge(stp[i % 2], 16 * (i // 2))
                for ch in range(4):
                    u = i * 4 + ch
                    v.wait_ge(sp, u + 1)
                    bank = psA if u % 2 == 0 else psB
                    v.scalar_tensor_tensor(
                        out=xob[i % 2][:, ch * 512:(ch + 1) * 512], in0=bank[:, 0:512],
                        scalar=1.0, in1=xb[i % 2][:, ch * 512:(ch + 1) * 512],
                        op0=ALU.mult, op1=ALU.add).then_inc(sv, 1)

        @blk.scalar
        def _(sc):
            for dm in range(1, 16, 2):
                sc.dma_start(out=ws[:, dm * 2048:(dm + 1) * 2048],
                             in_=Wb[ts(dm, 128), :]).then_inc(sw2, 16)
            for i in range(nt):
                sc.wait_ge(sv, i * 4 + 4)
                sc.dma_start(out=xnew[ts(i, 128), :], in_=xob[i % 2][:]
                             ).then_inc(stp[i % 2], 16)
            sc.wait_ge(st0, 16 * ((nt + 1) // 2))
            sc.wait_ge(st1, 16 * (nt // 2))

    return nc


def build_scat(NT=1024):
    nc = bass.Bass("TRN2", target_bir_lowering=False)
    nt = NT // 128
    xprev = nc.dram_tensor("xprev", [NT, D], F32, kind="ExternalInput").ap()
    lntk = nc.dram_tensor("lntk", [NT, D], BF16, kind="ExternalInput").ap()
    gated = nc.dram_tensor("gated", [SSEL, D], BF16, kind="ExternalInput").ap()
    acol = nc.dram_tensor("acol", [128, 8], U32, kind="ExternalInput").ap()
    xnew = nc.dram_tensor("xnew", [NT, D], F32, kind="ExternalOutput").ap()
    es = ExitStack()
    sb = lambda n, shp, dt: es.enter_context(nc.sbuf_tensor(n, shp, dt))
    sem = lambda n: es.enter_context(nc.semaphore(n))
    with es:
        xt = sb("xt", [128, D], F32)
        xt2 = sb("xt2", [128, D], F32)
        lt = sb("lt", [128, D], BF16)
        lt2 = sb("lt2", [128, D], BF16)
        gb = sb("gb", [128, D], BF16)
        gb2 = sb("gb2", [128, D], BF16)
        xo = sb("xo", [128, D], F32)
        xo2 = sb("xo2", [128, D], F32)
        acs = sb("acs", [128, 8], U32)
        af = sb("af", [128, 8], F32)
        mcol = sb("mcol", [128, 8], F32)
        sg0 = sem("sg0"); sg1 = sem("sg1")
        sl0 = sem("sl0"); sl1 = sem("sl1"); sla = sem("sla"); smz = sem("smz")
        sv = sem("sv"); sz = sem("sz")
        st0 = sem("st0"); st1 = sem("st1")
        blk = es.enter_context(nc.Block())
        sgp = [sg0, sg1]
        slp = [sl0, sl1]
        stp = [st0, st1]
        xb = [xt, xt2]
        lb = [lt, lt2]
        gbb = [gb, gb2]
        xob = [xo, xo2]

        @blk.gpsimd
        def _(g):
            g.dma_start(out=acs[:], in_=acol).then_inc(sla, 16)
            g.memset(gb[:], 0.0).then_inc(smz, 1)
            g.memset(gb2[:], 0.0).then_inc(smz, 1)
            g.wait_ge(sla, 16)
            g.wait_ge(smz, 2)
            for i in range(nt):
                if i >= 2:
                    g.wait_ge(sv, 3 * (i - 1))
                g.indirect_dma_start(
                    gbb[i % 2][:], None, gated,
                    bass.IndirectOffsetOnAxis(ap=acs[:, i:i + 1], axis=0),
                    bounds_check=SSEL - 1, oob_is_err=False).then_inc(sgp[i % 2], 16)

        @blk.sync
        def _(s_):
            for i in range(nt):
                if i >= 2:
                    s_.wait_ge(stp[i % 2], 16 * (i // 2))
                s_.dma_start(out=xb[i % 2][:], in_=xprev[ts(i, 128), :]
                             ).then_inc(slp[i % 2], 16)
                s_.dma_start(out=lb[i % 2][:], in_=lntk[ts(i, 128), :]
                             ).then_inc(slp[i % 2], 16)

        @blk.vector
        def _(v):
            v.wait_ge(sla, 16)
            v.tensor_copy(af[:], acs[:]).then_inc(sz, 1)
            v.wait_ge(sz, 1)
            v.tensor_scalar(mcol[:], af[:], float(SSEL) - 0.5, None, ALU.is_ge
                            ).then_inc(sz, 1)
            for i in range(nt):
                v.wait_ge(sz, 2)
                v.wait_ge(slp[i % 2], 32 * (i // 2 + 1))
                v.wait_ge(sgp[i % 2], 16 * (i // 2 + 1))
                if i >= 2:
                    v.wait_ge(stp[i % 2], 16 * (i // 2))
                v.scalar_tensor_tensor(
                    out=xob[i % 2][:], in0=lb[i % 2][:], scalar=mcol[:, i:i + 1],
                    in1=xb[i % 2][:], op0=ALU.mult, op1=ALU.add).then_inc(sv, 1)
                v.wait_ge(sv, 3 * i + 1)
                v.tensor_tensor(xob[i % 2][:], xob[i % 2][:], gbb[i % 2][:], ALU.add
                                ).then_inc(sv, 1)
                v.wait_ge(sv, 3 * i + 2)
                v.tensor_scalar(gbb[i % 2][:], gbb[i % 2][:], 0.0, None, ALU.mult
                                ).then_inc(sv, 1)

        @blk.scalar
        def _(sc):
            for i in range(nt):
                sc.wait_ge(sv, 3 * i + 2)
                sc.dma_start(out=xnew[ts(i, 128), :], in_=xob[i % 2][:]
                             ).then_inc(stp[i % 2], 16)
            sc.wait_ge(st0, 16 * ((nt + 1) // 2))
            sc.wait_ge(st1, 16 * (nt // 2))

    return nc


NBIS = 46


class _Sched:
    KINDS = {"vector": ("sv", 1), "tensor": ("sp", 1), "scalar": ("sa", 1),
             "gpsimd": ("sz", 1), "sync_dma": ("sl", 16),
             "scalar_dma": ("sa2", 16), "gpsimd_dma": ("sg", 16)}

    def __init__(self):
        self.steps = []
        self.cnt = {v[0]: 0 for v in self.KINDS.values()}

    def add(self, kind, fn, lag=None):
        semn, inc = self.KINDS[kind]
        waits = {}
        for k, v in self.cnt.items():
            if lag:
                for lk, ln in lag.items():
                    if self.KINDS[lk][0] == k:
                        v = v - ln * self.KINDS[lk][1]
            if v > 0:
                waits[k] = v
        self.steps.append((kind, fn, waits))
        self.cnt[semn] += inc

    def emit_one(self, eng, sems, kinds):
        lw = {}
        for kind, fn, waits in self.steps:
            if kind not in kinds:
                continue
            semn, inc = self.KINDS[kind]
            for k, v in waits.items():
                if lw.get(k, -1) < v:
                    eng.wait_ge(sems[k], v)
                    lw[k] = v
            inst = fn(eng)
            if inst is None:
                raise RuntimeError("step returned no instruction")
            inst.then_inc(sems[semn], inc)


def build_fsel():
    nc = bass.Bass("TRN2", target_bir_lowering=False)
    sco = nc.dram_tensor("sco", [128, 64], F32, kind="ExternalInput").ap()
    lnfull = nc.dram_tensor("lnfull", [T, D], BF16, kind="ExternalInput").ap()
    ln1c = nc.dram_tensor("ln1c", [128, 16], F32, kind="ExternalInput").ap()
    biasc = nc.dram_tensor("biasc", [128, 32], F32, kind="ExternalInput").ap()
    sbase = nc.dram_tensor("sbase", [1, 1], U32, kind="ExternalInput").ap()
    invfs = nc.dram_tensor("invfs", [1, 128], F32, kind="ExternalInput").ap()
    identf = nc.dram_tensor("identf", [128, 128], F32, kind="ExternalInput").ap()
    identb = nc.dram_tensor("identb", [128, 128], BF16, kind="ExternalInput").ap()
    rotb = nc.dram_tensor("rotb", [128, 128], BF16, kind="ExternalInput").ap()
    iota1k = nc.dram_tensor("iota1k", [128, 1024], F32, kind="ExternalInput").ap()
    trif = nc.dram_tensor("trif", [128, 128], F32, kind="ExternalInput").ap()
    tcolf = nc.dram_tensor("tcolf", [128, 1], F32, kind="ExternalInput").ap()
    onescf = nc.dram_tensor("onescf", [128, 1], F32, kind="ExternalInput").ap()
    onesf = nc.dram_tensor("onesf", [1, 128], F32, kind="ExternalInput").ap()

    qTs = nc.dram_tensor("qTs", [16 * 128, 128], BF16, kind="ExternalOutput").ap()
    kkTs = nc.dram_tensor("kkTs", [8 * 128, 128], BF16, kind="ExternalOutput").ap()
    vso = nc.dram_tensor("vso", [128, 1024], BF16, kind="ExternalOutput").ap()
    selo = nc.dram_tensor("selo", [128, D], BF16, kind="ExternalOutput").ap()
    posco = nc.dram_tensor("posco", [2, SSEL], F32, kind="ExternalOutput").ap()
    acolo = nc.dram_tensor("acolo", [128, 64], U32, kind="ExternalOutput").ap()

    fw3b = nc.dram_tensor("fw3b", [D, 4096], BF16, kind="ExternalInput").ap()
    posd = nc.dram_tensor("posd", [SSEL, 1], F32).ap()

    es = ExitStack()
    sb = lambda n, shp, dt: es.enter_context(nc.sbuf_tensor(n, shp, dt))
    ps = lambda n, shp, dt=F32: es.enter_context(nc.psum_tensor(n, shp, dt))
    sem = lambda n: es.enter_context(nc.semaphore(n))
    with es:
        scsb = sb("scsb", [128, 64], F32)
        selw = sb("selw", [128, 64], F32)
        mge = sb("mge", [128, 64], F32)
        mgt = sb("mgt", [128, 64], F32)
        meq = sb("meq", [128, 64], F32)
        Ceq = sb("Ceq", [128, 64], F32)
        Csel = sb("Csel", [128, 64], F32)
        afb = sb("afb", [128, 64], F32)
        acu = sb("acu", [128, 64], U32)
        srow = sb("srow", [1, 16], F32)
        crow1 = sb("crow1", [1, 64], F32)
        crow2 = sb("crow2", [1, 64], F32)
        exbc = sb("exbc", [128, 64], F32)
        midc = sb("midc", [128, 1], F32)
        cntp = sb("cntp", [128, 1], F32)
        needc = sb("needc", [128, 1], F32)
        iot = sb("iot", [128, 1024], F32)
        trs = sb("trs", [128, 128], F32)
        tcs = sb("tcs", [128, 1], F32)
        onc = sb("onc", [128, 1], F32)
        onr = sb("onr", [1, 128], F32)
        idf = sb("idf", [128, 128], F32)
        idb = sb("idb", [128, 128], BF16)
        rob = sb("rob", [128, 128], BF16)
        Pf_ = [sb("Pf0", [128, 1024], F32), sb("Pf1", [128, 1024], F32)]
        stf_ = [sb("stf0", [128, 2], F32), sb("stf1", [128, 2], F32)]
        poscosb = sb("poscosb", [2, 1024], F32)
        sbs = sb("sbs", [1, 1], U32)
        sbf1 = sb("sbf1", [1, 1], F32)
        slotf = sb("slotf", [128, 1], F32)
        slotu = sb("slotu", [128, 1], U32)
        posg = sb("posg", [128, 1], F32)
        myidx = sb("myidx", [128, 1], U32)
        posrow = sb("posrow", [1, 128], F32)
        selrow = sb("selrow", [128, D], BF16)
        sq = sb("sq", [128, D], F32)
        rms = sb("rms", [128, 8], F32)
        hrow = sb("hrow", [128, D], BF16)
        hT = sb("hT", [128, D], BF16)
        fws = sb("fws", [128, 65536], BF16)
        lnc1 = sb("lnc1", [128, 16], F32)
        bcs = sb("bcs", [128, 32], F32)
        qko = sb("qko", [128, 4096], BF16)
        invr = sb("invr", [1, 128], F32)
        angs = sb("angs", [128, 128], F32)
        u1 = sb("u1", [128, 128], F32)
        ri = sb("ri", [128, 128], I32)
        rf = sb("rf", [128, 128], F32)
        sini = sb("sini", [128, 128], F32)
        sinv = sb("sinv", [128, 128], F32)
        cosv = sb("cosv", [128, 128], F32)
        rt1 = sb("rt1", [128, 128], F32)
        rt2 = sb("rt2", [128, 128], F32)
        vstok = sb("vstok", [128, 1024], BF16)
        psF = ps("psF", [128, 512])
        psG = ps("psG", [128, 512])
        psH1 = ps("psH1", [128, 512])
        psH2 = ps("psH2", [128, 512])
        psR = ps("psR", [128, 512])
        ptr = ps("ptr", [128, 1024], BF16)
        SEMS = {n: sem(n) for n in ["sv", "sp", "sa", "sz", "sl", "sa2", "sg"]}
        blk = es.enter_context(nc.Block())

        S = _Sched()

        S.add("gpsimd_dma", lambda g: g.dma_start(out=sbs[:], in_=sbase))
        jobs = [(scsb, sco), (iot, iota1k), (trs, trif), (tcs, tcolf),
                (onc, onescf), (onr, onesf), (idf, identf), (idb, identb),
                (rob, rotb), (lnc1, ln1c), (bcs, biasc), (invr, invfs)]
        for dst, src in jobs:
            S.add("sync_dma", lambda s_, dst=dst, src=src: s_.dma_start(out=dst[:], in_=src))
        BIG = {k: 10 ** 6 for k in _Sched.KINDS}
        for dm in range(16):
            q = "sync_dma" if dm % 2 == 0 else "scalar_dma"
            S.add(q, lambda e, dm=dm: e.dma_start(
                out=fws[:, dm * 4096:(dm + 1) * 4096], in_=fw3b[ts(dm, 128), :]),
                lag=BIG)

        S.add("vector", lambda v: v.tensor_reduce(midc[:], scsb[:], AX.X, ALU.max,
                                                  apply_absolute_value=True))
        S.add("tensor", lambda t: t.transpose(psG[0:1, 0:128], midc[:], idf[:]))
        S.add("vector", lambda v: v.tensor_reduce(srow[:, 4:5], psG[0:1, 0:128],
                                                  AX.X, ALU.max))
        S.add("vector", lambda v: v.tensor_scalar(srow[:, 1:2], srow[:, 4:5], 1.0,
                                                  None, ALU.add))
        S.add("vector", lambda v: v.tensor_scalar(srow[:, 0:1], srow[:, 1:2], -1.0,
                                                  None, ALU.mult))
        for it in range(NBIS):
            S.add("vector", lambda v: v.tensor_tensor(srow[:, 4:5], srow[:, 0:1],
                                                      srow[:, 1:2], ALU.add))
            S.add("vector", lambda v: v.tensor_scalar(srow[:, 2:3], srow[:, 4:5], 0.5,
                                                      None, ALU.mult))
            S.add("tensor", lambda t: t.matmul(psF[:, 0:1], onr[:, :], srow[:, 2:3],
                                               start=True, stop=True))
            S.add("vector", lambda v: v.tensor_copy(midc[:], psF[:, 0:1]))
            S.add("vector", lambda v: v.tensor_scalar(selw[:], scsb[:], midc[:], 0.0,
                                                      ALU.is_ge, ALU.add,
                                                      accum_out=cntp[:]))
            S.add("tensor", lambda t: t.matmul(psG[0:1, 0:1], onc[:], cntp[:],
                                               start=True, stop=True))
            S.add("vector", lambda v: v.tensor_scalar(srow[:, 3:4], psG[0:1, 0:1],
                                                      1023.5, None, ALU.is_ge))
            S.add("vector", lambda v: v.tensor_tensor(srow[:, 5:6], srow[:, 2:3],
                                                      srow[:, 0:1], ALU.subtract))
            S.add("vector", lambda v: v.tensor_tensor(srow[:, 6:7], srow[:, 5:6],
                                                      srow[:, 3:4], ALU.mult))
            S.add("vector", lambda v: v.tensor_tensor(srow[:, 0:1], srow[:, 0:1],
                                                      srow[:, 6:7], ALU.add))
            S.add("vector", lambda v: v.tensor_tensor(srow[:, 7:8], srow[:, 1:2],
                                                      srow[:, 2:3], ALU.subtract))
            S.add("vector", lambda v: v.tensor_tensor(srow[:, 8:9], srow[:, 7:8],
                                                      srow[:, 3:4], ALU.mult))
            S.add("vector", lambda v: v.tensor_tensor(srow[:, 9:10], srow[:, 2:3],
                                                      srow[:, 8:9], ALU.add))
            S.add("vector", lambda v: v.tensor_copy(srow[:, 1:2], srow[:, 9:10]))

        S.add("tensor", lambda t: t.matmul(psF[:, 0:1], onr[:, :], srow[:, 0:1],
                                           start=True, stop=True))
        S.add("vector", lambda v: v.tensor_copy(midc[:], psF[:, 0:1]))
        S.add("vector", lambda v: v.tensor_scalar(mge[:], scsb[:], midc[:], None,
                                                  ALU.is_ge))
        S.add("tensor", lambda t: t.matmul(psF[:, 0:1], onr[:, :], srow[:, 1:2],
                                           start=True, stop=True))
        S.add("vector", lambda v: v.tensor_copy(midc[:], psF[:, 0:1]))
        S.add("vector", lambda v: v.tensor_scalar(mgt[:], scsb[:], midc[:], 0.0,
                                                  ALU.is_ge, ALU.add,
                                                  accum_out=cntp[:]))
        S.add("vector", lambda v: v.tensor_tensor(meq[:], mge[:], mgt[:], ALU.subtract))
        S.add("tensor", lambda t: t.matmul(psG[0:1, 0:1], onc[:], cntp[:],
                                           start=True, stop=True))
        S.add("vector", lambda v: v.tensor_scalar(srow[:, 10:11], psG[0:1, 0:1],
                                                  -1.0, 1024.0, ALU.mult, ALU.add))
        S.add("tensor", lambda t: t.matmul(psF[:, 0:1], onr[:, :], srow[:, 10:11],
                                           start=True, stop=True))
        S.add("vector", lambda v: v.tensor_copy(needc[:], psF[:, 0:1]))

        def cumsum_steps(src, dst):
            S.add("tensor", lambda t: t.matmul(psG[:, 0:64], trs[:], src[:],
                                               start=True, stop=True))
            S.add("tensor", lambda t: t.matmul(psF[0:1, 0:64], onc[:], src[:],
                                               start=True, stop=True))
            S.add("vector", lambda v: v.tensor_copy(crow1[:], psF[0:1, 0:64]))
            for k in [1, 2, 4, 8, 16, 32]:
                S.add("vector", lambda v, k=k: v.tensor_copy(crow2[:, 0:k], crow1[:, 0:k]))
                S.add("vector", lambda v, k=k: v.tensor_tensor(
                    crow2[:, k:64], crow1[:, k:64], crow1[:, 0:64 - k], ALU.add))
                S.add("vector", lambda v: v.tensor_copy(crow1[:], crow2[:]))
            S.add("vector", lambda v: v.tensor_tensor(crow2[:], crow1[:],
                                                      psF[0:1, 0:64], ALU.subtract))
            S.add("tensor", lambda t: t.matmul(psF[:, 0:64], onr[:, :], crow2[:],
                                               start=True, stop=True))
            S.add("vector", lambda v: v.tensor_copy(exbc[:], psF[:, 0:64]))
            S.add("vector", lambda v: v.tensor_tensor(dst[:], psG[:, 0:64], exbc[:],
                                                      ALU.add))

        cumsum_steps(meq, Ceq)
        S.add("vector", lambda v: v.tensor_scalar(Ceq[:], Ceq[:], needc[:], None,
                                                  ALU.is_le))
        S.add("vector", lambda v: v.tensor_tensor(Ceq[:], Ceq[:], meq[:], ALU.mult))
        S.add("vector", lambda v: v.tensor_tensor(selw[:], mgt[:], Ceq[:], ALU.add))
        cumsum_steps(selw, Csel)
        S.add("vector", lambda v: v.tensor_scalar(afb[:], Csel[:], -1.0, None, ALU.add))
        S.add("vector", lambda v: v.tensor_tensor(afb[:], afb[:], selw[:], ALU.mult))
        S.add("vector", lambda v: v.tensor_scalar(Csel[:], selw[:], -1e6, 1e6,
                                                  ALU.mult, ALU.add))
        S.add("vector", lambda v: v.tensor_tensor(afb[:], afb[:], Csel[:], ALU.add))
        S.add("vector", lambda v: v.tensor_copy(acu[:], afb[:]))
        S.add("scalar_dma", lambda sc: sc.dma_start(out=acolo, in_=acu[:]))

        for f in range(64):
            pf = Pf_[f % 2]
            stf = stf_[f % 2]
            S.add("vector", lambda v, f=f, stf=stf: v.tensor_scalar(
                stf[:, 0:1], tcs[:], float(f * 128), None, ALU.add))
            S.add("vector", lambda v, f=f, stf=stf: v.tensor_copy(
                stf[:, 1:2], scsb[:, f:f + 1]))
            S.add("vector", lambda v, f=f, pf=pf: v.tensor_scalar(
                pf[:], iot[:], afb[:, f:f + 1], 0.0, ALU.subtract, ALU.is_equal))
            S.add("tensor", lambda t, f=f, pf=pf, stf=stf: t.matmul(
                psH1[0:2, 0:512], stf[:], pf[:, 0:512],
                start=(f == 0), stop=(f == 63)))
            S.add("tensor", lambda t, f=f, pf=pf, stf=stf: t.matmul(
                psH2[0:2, 0:512], stf[:], pf[:, 512:1024],
                start=(f == 0), stop=(f == 63)))
        S.add("vector", lambda v: v.tensor_copy(poscosb[:, 0:512], psH1[0:2, 0:512]))
        S.add("vector", lambda v: v.tensor_copy(poscosb[:, 512:1024], psH2[0:2, 0:512]))
        S.add("scalar_dma", lambda sc: sc.dma_start(out=posco, in_=poscosb[:]))
        S.add("scalar_dma", lambda sc: sc.dma_start(
            out=posd.rearrange("a b -> b a"), in_=poscosb[0:1, :]))

        S.add("vector", lambda v: v.tensor_copy(sbf1[:], sbs[:]))
        S.add("tensor", lambda t: t.matmul(psF[:, 0:1], onr[:, :], sbf1[:],
                                           start=True, stop=True))
        S.add("vector", lambda v: v.tensor_copy(slotf[:], psF[:, 0:1]))
        S.add("vector", lambda v: v.tensor_tensor(slotf[:], slotf[:], tcs[:], ALU.add))
        S.add("vector", lambda v: v.tensor_copy(slotu[:], slotf[:]))
        S.add("gpsimd_dma", lambda g: g.indirect_dma_start(
            posg[:], None, posd, bass.IndirectOffsetOnAxis(ap=slotu[:], axis=0)))
        S.add("vector", lambda v: v.tensor_copy(myidx[:], posg[:]))
        S.add("gpsimd_dma", lambda g: g.indirect_dma_start(
            selrow[:], None, lnfull, bass.IndirectOffsetOnAxis(ap=myidx[:], axis=0)))
        S.add("tensor", lambda t: t.transpose(psF[0:1, 0:128], posg[:], idf[:]))
        S.add("vector", lambda v: v.tensor_copy(posrow[:], psF[0:1, 0:128]))
        S.add("scalar_dma", lambda sc: sc.dma_start(out=selo, in_=selrow[:]))

        S.add("vector", lambda v: v.tensor_tensor(sq[:], selrow[:], selrow[:], ALU.mult))
        S.add("vector", lambda v: v.tensor_reduce(rms[:, 0:1], sq[:], AX.X, ALU.add))
        S.add("vector", lambda v: v.tensor_scalar(rms[:, 1:2], rms[:, 0:1],
                                                  1.0 / 2048.0, 1e-6, ALU.mult, ALU.add))
        S.add("vector", lambda v: v.reciprocal(rms[:, 2:3], rms[:, 1:2]))
        S.add("scalar", lambda sc: sc.activation(rms[:, 3:4], rms[:, 2:3], AF.Sqrt))
        S.add("vector", lambda v: v.tensor_scalar(hrow[:], selrow[:], rms[:, 3:4],
                                                  None, ALU.mult))
        for g_ in range(2):
            bank = ptr
            def trg(t, g_=g_):
                for k in range(8):
                    dt = g_ * 8 + k
                    mm = t.transpose(ptr[:, k * 128:(k + 1) * 128],
                                     hrow[:, dt * 128:(dt + 1) * 128], idb[:])
                return mm
            S.add("tensor", trg)
            for k in range(8):
                dt = g_ * 8 + k
                S.add("vector", lambda v, dt=dt, k=k: v.tensor_scalar(
                    hT[:, dt * 128:(dt + 1) * 128], ptr[:, k * 128:(k + 1) * 128],
                    lnc1[:, dt:dt + 1], None, ALU.mult))

        for io in range(32):
            bank = psH1 if io % 2 == 0 else psH2
            def qmm(t, io=io, bank=bank):
                for dm in range(16):
                    mm = t.matmul(bank[:, 0:128],
                                  fws[:, (dm * 32 + io) * 128:(dm * 32 + io + 1) * 128],
                                  hT[:, dm * 128:(dm + 1) * 128],
                                  start=(dm == 0), stop=(dm == 15))
                return mm
            S.add("tensor", qmm)
            S.add("vector", lambda v, io=io, bank=bank: v.tensor_scalar(
                qko[:, io * 128:(io + 1) * 128], bank[:, 0:128],
                bcs[:, io:io + 1], None, ALU.add))

        S.add("tensor", lambda t: t.matmul(psR[:, 0:128], invr[:, :], posrow[:],
                                           start=True, stop=True))
        S.add("vector", lambda v: v.tensor_copy(angs[:], psR[:, 0:128]))
        for nm, off in [("sin", 0.0), ("cos", 0.25)]:
            dstv = sinv if nm == "sin" else cosv
            S.add("vector", lambda v, off=off: v.tensor_scalar(
                u1[:], angs[:], 1.0 / TWOPI, off, ALU.mult, ALU.add))
            S.add("vector", lambda v: v.tensor_copy(ri[:], u1[:]))
            S.add("vector", lambda v: v.tensor_copy(rf[:], ri[:]))
            S.add("vector", lambda v: v.tensor_tensor(sini[:], u1[:], rf[:],
                                                      ALU.subtract))
            S.add("vector", lambda v: v.tensor_scalar(rf[:], sini[:], 0.5, None,
                                                      ALU.is_gt))
            S.add("vector", lambda v: v.tensor_tensor(sini[:], sini[:], rf[:],
                                                      ALU.subtract))
            S.add("vector", lambda v: v.tensor_scalar(rf[:], sini[:], -0.5, None,
                                                      ALU.is_lt))
            S.add("vector", lambda v: v.tensor_tensor(sini[:], sini[:], rf[:],
                                                      ALU.add))
            S.add("vector", lambda v: v.tensor_scalar(sini[:], sini[:], TWOPI,
                                                      None, ALU.mult))
            S.add("scalar", lambda sc, dstv=dstv: sc.activation(dstv[:], sini[:],
                                                                AF.Sin))
        for ti in range(24):
            S.add("tensor", lambda t, ti=ti: t.matmul(
                psR[:, 0:128], rob[:], qko[:, ti * 128:(ti + 1) * 128],
                start=True, stop=True))
            S.add("vector", lambda v, ti=ti: v.tensor_tensor(
                rt1[:], qko[:, ti * 128:(ti + 1) * 128], cosv[:], ALU.mult))
            S.add("vector", lambda v: v.tensor_tensor(rt2[:], psR[:, 0:128], sinv[:],
                                                      ALU.mult))
            S.add("vector", lambda v, ti=ti: v.tensor_tensor(
                qko[:, ti * 128:(ti + 1) * 128], rt1[:], rt2[:], ALU.add))

        def vtr(t):
            for k in range(8):
                mm = t.transpose(ptr[:, k * 128:(k + 1) * 128],
                                 qko[:, (24 + k) * 128:(25 + k) * 128], idb[:])
            return mm
        S.add("tensor", vtr)
        S.add("vector", lambda v: v.tensor_copy(vstok[:], ptr[:]))

        S.add("scalar_dma", lambda sc: sc.dma_start(
            out=qTs.rearrange("(dt p) s -> p dt s", p=128),
            in_=qko[:, 0:2048].rearrange("p (dt s) -> p dt s", s=128)))
        S.add("scalar_dma", lambda sc: sc.dma_start(
            out=kkTs.rearrange("(dt p) s -> p dt s", p=128),
            in_=qko[:, 2048:3072].rearrange("p (dt s) -> p dt s", s=128)))
        S.add("scalar_dma", lambda sc: sc.dma_start(out=vso, in_=vstok[:]))

        @blk.sync
        def _(s_):
            S.emit_one(s_, SEMS, ["sync_dma"])

        @blk.gpsimd
        def _(g):
            S.emit_one(g, SEMS, ["gpsimd", "gpsimd_dma"])

        @blk.vector
        def _(v):
            S.emit_one(v, SEMS, ["vector"])

        @blk.tensor
        def _(t):
            S.emit_one(t, SEMS, ["tensor"])

        @blk.scalar
        def _(sc):
            S.emit_one(sc, SEMS, ["scalar", "scalar_dma"])
            sc.wait_ge(SEMS["sa2"], S.cnt["sa2"])
            sc.wait_ge(SEMS["sg"], S.cnt["sg"])
            sc.wait_ge(SEMS["sv"], S.cnt["sv"])

    return nc

def build_fmain():
    nc = bass.Bass("TRN2", target_bir_lowering=False)
    qTs = nc.dram_tensor("qTs", [16 * 128, 128], BF16, kind="ExternalInput").ap()
    kkT = nc.dram_tensor("kkT", [8 * 128, SSEL], BF16, kind="ExternalInput").ap()
    vfull = nc.dram_tensor("vfull", [SSEL, 1024], BF16, kind="ExternalInput").ap()
    seltok = nc.dram_tensor("seltok", [128, D], BF16, kind="ExternalInput").ap()
    myscol = nc.dram_tensor("myscol", [128, 1], F32, kind="ExternalInput").ap()
    amask = nc.dram_tensor("amask", [128, SSEL], F32, kind="ExternalInput").ap()
    ln2bc = nc.dram_tensor("ln2bc", [128, D], F32, kind="ExternalInput").ap()
    identb = nc.dram_tensor("identb", [128, 128], BF16, kind="ExternalInput").ap()
    gato = nc.dram_tensor("gato", [128, D], BF16, kind="ExternalOutput").ap()

    fwob = nc.dram_tensor("fwob", [D, D], BF16, kind="ExternalInput").ap()
    fwgb = nc.dram_tensor("fwgb", [D, IMLP], BF16, kind="ExternalInput").ap()
    fwub = nc.dram_tensor("fwub", [D, IMLP], BF16, kind="ExternalInput").ap()
    fwdb = nc.dram_tensor("fwdb", [IMLP, D], BF16, kind="ExternalInput").ap()

    es = ExitStack()
    sb = lambda n, shp, dt: es.enter_context(nc.sbuf_tensor(n, shp, dt))
    ps = lambda n, shp, dt=F32: es.enter_context(nc.psum_tensor(n, shp, dt))
    sem = lambda n: es.enter_context(nc.semaphore(n))
    with es:
        qs = sb("qs", [128, 2048], BF16)
        ks = sb("ks", [128, 8192], BF16)
        vss = sb("vss", [128, 8192], BF16)
        Pb = sb("Pb", [128, 1024], BF16)
        Pe = sb("Pe", [128, 1024], F32)
        PT = sb("PT", [128, 1024], BF16)
        ctxT = sb("ctxT", [128, 2048], BF16)
        sel = sb("sel", [128, D], BF16)
        self32 = sb("self32", [128, D], F32)
        x1tok = sb("x1tok", [128, D], F32)
        sqtok = sb("sqtok", [128, D], F32)
        h2k = sb("h2k", [128, D], BF16)
        h2T = sb("h2T", [128, D], BF16)
        gatok = sb("gatok", [128, D], BF16)
        msk = sb("msk", [128, SSEL], F32)
        lnbc = sb("lnbc", [128, D], F32)
        wsl = [sb(f"wsl{i}", [128, 2048], BF16) for i in range(2)]
        wgl = [sb(f"wgl{i}", [128, 2048], BF16) for i in range(2)]
        wul = [sb(f"wul{i}", [128, 2048], BF16) for i in range(2)]
        wdl = [sb(f"wdl{i}", [128, 2048], BF16) for i in range(2)]
        g1 = sb("g1", [128, 128], F32)
        mt = sb("mt", [128, 128], BF16)
        sm2 = sb("sm2", [128, 16], F32)
        rstdc = sb("rstdc", [128, 1], F32)
        gcol = sb("gcol", [128, 1], F32)
        msc = sb("msc", [128, 1], F32)
        idb = sb("idb", [128, 128], BF16)
        psA = ps("psA", [128, 512])
        psB = ps("psB", [128, 512])
        psC = ps("psC", [128, 512])
        psD = ps("psD", [128, 512])
        psE = ps("psE", [128, 512])
        psFb = ps("psFb", [128, 512])
        ptr = ps("ptr", [128, 1024], BF16)
        BANKS = [psC, psD, psE, psFb]
        SEMS = {n: sem(n) for n in ["sv", "sp", "sa", "sz", "sl", "sa2", "sg"]}
        swg = [sem("swg0"), sem("swg1")]
        swu = [sem("swu0"), sem("swu1")]
        swd = [sem("swd0"), sem("swd1")]
        smp = sem("smp")
        smv = sem("smv")
        sma = sem("sma")
        sst = sem("sst")
        blk = es.enter_context(nc.Block())

        S = _Sched()

        jobs = [(idb, identb), (msk, amask), (sel, seltok), (msc, myscol),
                (lnbc, ln2bc)]
        for dst, src in jobs:
            S.add("sync_dma", lambda s_, dst=dst, src=src: s_.dma_start(out=dst[:], in_=src))
        S.add("sync_dma", lambda s_: s_.dma_start(
            out=qs.rearrange("p (dt s) -> p dt s", s=128),
            in_=qTs.rearrange("(dt p) s -> p dt s", p=128)))
        S.add("sync_dma", lambda s_: s_.dma_start(
            out=ks.rearrange("p (dt s) -> p dt s", s=SSEL),
            in_=kkT.rearrange("(dt p) s -> p dt s", p=128)))
        S.add("sync_dma", lambda s_: s_.dma_start(
            out=vss.rearrange("p (st dv) -> p st dv", dv=1024),
            in_=vfull.rearrange("(st p) dv -> p st dv", p=128)))

        for h in range(16):
            kv = h // 2
            S.add("tensor", lambda t, h=h, kv=kv: t.matmul(
                psA[:, 0:512], qs[:, h * 128:(h + 1) * 128],
                ks[:, kv * 1024:kv * 1024 + 512], start=True, stop=True))
            S.add("tensor", lambda t, h=h, kv=kv: t.matmul(
                psB[:, 0:512], qs[:, h * 128:(h + 1) * 128],
                ks[:, kv * 1024 + 512:(kv + 1) * 1024], start=True, stop=True))
            S.add("vector", lambda v: v.tensor_tensor(psA[:, 0:512], psA[:, 0:512],
                                                      msk[:, 0:512], ALU.add))
            S.add("vector", lambda v: v.tensor_tensor(psB[:, 0:512], psB[:, 0:512],
                                                      msk[:, 512:1024], ALU.add))
            S.add("vector", lambda v: v.reduce_max(sm2[:, 0:1], psA[:, 0:512], AX.X))
            S.add("vector", lambda v: v.reduce_max(sm2[:, 1:2], psB[:, 0:512], AX.X))
            S.add("vector", lambda v: v.tensor_tensor(sm2[:, 2:3], sm2[:, 0:1],
                                                      sm2[:, 1:2], ALU.max))
            S.add("vector", lambda v: v.tensor_scalar(sm2[:, 3:4], sm2[:, 2:3],
                                                      -ISQ, None, ALU.mult))
            S.add("scalar", lambda sc: sc.activation(Pe[:, 0:512], psA[:, 0:512],
                                                     AF.Exp, bias=sm2[:, 3:4], scale=ISQ))
            S.add("scalar", lambda sc: sc.activation(Pe[:, 512:1024], psB[:, 0:512],
                                                     AF.Exp, bias=sm2[:, 3:4], scale=ISQ))
            S.add("vector", lambda v: v.tensor_reduce(sm2[:, 4:5], Pe[:], AX.X, ALU.add))
            S.add("vector", lambda v: v.reciprocal(sm2[:, 5:6], sm2[:, 4:5]))
            S.add("vector", lambda v: v.tensor_scalar(Pb[:], Pe[:], sm2[:, 5:6],
                                                      None, ALU.mult))
            def trg(t):
                for k in range(8):
                    mm = t.transpose(ptr[:, k * 128:(k + 1) * 128],
                                     Pb[:, k * 128:(k + 1) * 128], idb[:])
                return mm
            S.add("tensor", trg)
            S.add("vector", lambda v: v.tensor_copy(PT[:], ptr[:]))
            def avg(t, kv=kv):
                for st_ in range(8):
                    mm = t.matmul(psC[:, 0:128],
                                  vss[:, st_ * 1024 + kv * 128:st_ * 1024 + (kv + 1) * 128],
                                  PT[:, st_ * 128:(st_ + 1) * 128],
                                  start=(st_ == 0), stop=(st_ == 7))
                return mm
            S.add("tensor", avg)
            S.add("vector", lambda v, h=h: v.tensor_copy(ctxT[:, h * 128:(h + 1) * 128],
                                                         psC[:, 0:128]))

        S.add("vector", lambda v: v.tensor_copy(self32[:], sel[:]))
        for dc in range(16):
            S.add("sync_dma", lambda s_, dc=dc: s_.dma_start(
                out=wsl[dc % 2][:], in_=fwob[ts(dc, 128), :]), lag={"tensor": 1})
            def womm(t, dc=dc):
                for ch in range(4):
                    mm = t.matmul(BANKS[ch][:, 0:512],
                                  ctxT[:, dc * 128:(dc + 1) * 128],
                                  wsl[dc % 2][:, ch * 512:(ch + 1) * 512],
                                  start=(dc == 0), stop=(dc == 15))
                return mm
            S.add("tensor", womm)
        for ch in range(4):
            S.add("vector", lambda v, ch=ch: v.scalar_tensor_tensor(
                out=x1tok[:, ch * 512:(ch + 1) * 512],
                in0=BANKS[ch][:, 0:512], scalar=1.0,
                in1=self32[:, ch * 512:(ch + 1) * 512], op0=ALU.mult, op1=ALU.add))

        S.add("vector", lambda v: v.tensor_tensor(sqtok[:], x1tok[:], x1tok[:], ALU.mult))
        S.add("vector", lambda v: v.tensor_reduce(sm2[:, 6:7], sqtok[:], AX.X, ALU.add))
        S.add("vector", lambda v: v.tensor_scalar(sm2[:, 7:8], sm2[:, 6:7],
                                                  1.0 / 2048.0, 1e-6, ALU.mult, ALU.add))
        S.add("vector", lambda v: v.reciprocal(sm2[:, 8:9], sm2[:, 7:8]))
        S.add("scalar", lambda sc: sc.activation(rstdc[:], sm2[:, 8:9], AF.Sqrt))
        S.add("vector", lambda v: v.tensor_scalar(sqtok[:], x1tok[:], rstdc[:, 0:1],
                                                  None, ALU.mult))
        S.add("vector", lambda v: v.tensor_tensor(h2k[:], sqtok[:], lnbc[:], ALU.mult))
        for g_ in range(2):
            def trg2(t, g_=g_):
                for k in range(8):
                    dt = g_ * 8 + k
                    mm = t.transpose(ptr[:, k * 128:(k + 1) * 128],
                                     h2k[:, dt * 128:(dt + 1) * 128], idb[:])
                return mm
            S.add("tensor", trg2)
            S.add("vector", lambda v, g_=g_: v.tensor_copy(
                h2T[:, g_ * 1024:(g_ + 1) * 1024], ptr[:]))
        S.add("scalar", lambda sc: sc.activation(gcol[:], msc[:], AF.Sigmoid))

        TOT = dict(S.cnt)

        def ld_gu(e, buf, wsrc, I, sems_):
            return e.dma_start(
                out=buf[I % 2].rearrange("p (dm q) -> p dm q", q=128),
                in_=wsrc.rearrange("(dm p) q -> p dm q", p=128)
                [:, :, I * 128:(I + 1) * 128]).then_inc(sems_[I % 2], 16)

        @blk.sync
        def _(s_):
            S.emit_one(s_, SEMS, ["sync_dma"])
            for I in range(64):
                if I >= 2:
                    s_.wait_ge(smp, 3 * (I - 1))
                ld_gu(s_, wgl, fwgb, I, swg)

        @blk.gpsimd
        def _(g):
            S.emit_one(g, SEMS, ["gpsimd", "gpsimd_dma"])
            for I in range(64):
                if I >= 2:
                    g.wait_ge(smp, 3 * (I - 1))
                g.dma_start(out=wdl[I % 2][:], in_=fwdb[ts(I, 128), :]
                            ).then_inc(swd[I % 2], 16)

        @blk.tensor
        def _(t):
            S.emit_one(t, SEMS, ["tensor"])
            for k in ["sv", "sa"]:
                t.wait_ge(SEMS[k], TOT[k])
            for I in range(64):
                for sw in (swg, swu, swd):
                    t.wait_ge(sw[I % 2], 16 * (I // 2 + 1))
                if I >= 1:
                    t.wait_ge(smv, 2 * I)
                for dm in range(16):
                    mm = t.matmul(psA[:, 0:128], wgl[I % 2][:, dm * 128:(dm + 1) * 128],
                                  h2T[:, dm * 128:(dm + 1) * 128],
                                  start=(dm == 0), stop=(dm == 15))
                mm.then_inc(smp, 1)
                for dm in range(16):
                    mm = t.matmul(psB[:, 0:128], wul[I % 2][:, dm * 128:(dm + 1) * 128],
                                  h2T[:, dm * 128:(dm + 1) * 128],
                                  start=(dm == 0), stop=(dm == 15))
                mm.then_inc(smp, 1)
                t.wait_ge(smv, 2 * I + 2)
                for ch in range(4):
                    mm = t.matmul(BANKS[ch][:, 0:512], mt[:],
                                  wdl[I % 2][:, ch * 512:(ch + 1) * 512],
                                  start=(I == 0), stop=(I == 63))
                mm.then_inc(smp, 1)

        @blk.vector
        def _(v):
            S.emit_one(v, SEMS, ["vector"])
            for k in ["sa", "sp"]:
                v.wait_ge(SEMS[k], TOT[k])
            for I in range(64):
                v.wait_ge(sma, I + 1)
                v.wait_ge(smp, 3 * I + 2)
                v.tensor_tensor(mt[:], psB[:, 0:128], g1[:], ALU.mult).then_inc(smv, 2)
            v.wait_ge(smp, 192)
            for ch in range(4):
                v.scalar_tensor_tensor(
                    out=sqtok[:, ch * 512:(ch + 1) * 512],
                    in0=BANKS[ch][:, 0:512], scalar=1.0,
                    in1=x1tok[:, ch * 512:(ch + 1) * 512], op0=ALU.mult, op1=ALU.add
                    ).then_inc(smv, 1)
            v.wait_ge(smv, 132)
            v.tensor_tensor(sqtok[:], sqtok[:], self32[:], ALU.subtract).then_inc(smv, 1)
            v.wait_ge(smv, 133)
            v.tensor_scalar(sqtok[:], sqtok[:], gcol[:, 0:1], None, ALU.mult
                            ).then_inc(smv, 1)
            v.wait_ge(smv, 134)
            v.tensor_tensor(gatok[:], sqtok[:], self32[:], ALU.add).then_inc(smv, 1)

        @blk.scalar
        def _(sc):
            S.emit_one(sc, SEMS, ["scalar", "scalar_dma"])
            sc.wait_ge(SEMS["sp"], TOT["sp"])
            ld_gu(sc, wul, fwub, 0, swu)
            ld_gu(sc, wul, fwub, 1, swu)
            for I in range(64):
                sc.wait_ge(smp, 3 * I + 1)
                sc.activation(g1[:], psA[:, 0:128], AF.Silu).then_inc(sma, 1)
                if I + 2 < 64:
                    sc.wait_ge(smp, 3 * I + 2)
                    ld_gu(sc, wul, fwub, I + 2, swu)
            sc.wait_ge(smv, 135)
            sc.dma_start(out=gato, in_=gatok[:]).then_inc(sst, 16)
            sc.wait_ge(sst, 16)
            sc.wait_ge(SEMS["sg"], S.cnt["sg"])
            sc.wait_ge(SEMS["sa2"], S.cnt["sa2"])

    return nc


def build_cast():
    nc = bass.Bass("TRN2", target_bir_lowering=False)
    cin = nc.dram_tensor("cin", [1088, 8192], F32, kind="ExternalInput").ap()
    cout = nc.dram_tensor("cout", [1088, 8192], BF16, kind="ExternalOutput").ap()
    es = ExitStack()
    with es:
        scp = es.enter_context(nc.semaphore("scp"))
        blk = es.enter_context(nc.Block())

        @blk.gpsimd
        def _(g):
            g.dma_start(out=cout, in_=cin).then_inc(scp, 16)
            g.wait_ge(scp, 16)
    return nc


# version 04
# speedup vs baseline: 1.7426x; 1.5055x over previous
import numpy as np
import ml_dtypes
from contextlib import ExitStack
import concourse.bass as bass
import concourse.mybir as mybir
from concourse.bass_utils import run_bass_kernel_spmd

F32 = mybir.dt.float32
BF16 = mybir.dt.bfloat16
U32 = mybir.dt.uint32
I32 = mybir.dt.int32
AF = mybir.ActivationFunctionType
ALU = mybir.AluOpType
AX = mybir.AxisListType
BF = ml_dtypes.bfloat16
ts = bass.ts
ds = bass.ds

T = 8192
D = 2048
HD = 128
SSEL = 1024
IMLP = 8192
NEG = -1.0e30
RTH = 10000.0
ISQ = float(1.0 / np.sqrt(np.float32(128.0)))
NCORE = 8
TWOPI = float(2.0 * np.pi)

_CACHE = {}


def _cached(key, fn):
    if key not in _CACHE:
        _CACHE[key] = fn()
    return _CACHE[key]


def build_ln(NT=1024):
    nc = bass.Bass("TRN2", target_bir_lowering=False)
    nt = NT // 128
    xs = nc.dram_tensor("xs", [NT, D], F32, kind="ExternalInput").ap()
    wr = nc.dram_tensor("wr", [1, D], F32, kind="ExternalInput").ap()
    onesf = nc.dram_tensor("onesf", [1, 128], F32, kind="ExternalInput").ap()
    identb = nc.dram_tensor("identb", [128, 128], BF16, kind="ExternalInput").ap()
    lntok = nc.dram_tensor("lntok", [NT, D], BF16, kind="ExternalOutput").ap()
    lnT = nc.dram_tensor("lnT", [16 * 128, NT], BF16, kind="ExternalOutput").ap()
    sco = nc.dram_tensor("sco", [128, nt], F32, kind="ExternalOutput").ap()
    es = ExitStack()
    sb = lambda n, shp, dt: es.enter_context(nc.sbuf_tensor(n, shp, dt))
    ps = lambda n, shp, dt=F32: es.enter_context(nc.psum_tensor(n, shp, dt))
    sem = lambda n: es.enter_context(nc.semaphore(n))
    with es:
        xt = sb("xt", [128, D], F32)
        xt2 = sb("xt2", [128, D], F32)
        tok = sb("tok", [128, D], BF16)
        tok2 = sb("tok2", [128, D], BF16)
        lnb = sb("lnb", [128, 2048], BF16)
        lnb2 = sb("lnb2", [128, 2048], BF16)
        sm = sb("sm", [128, 64], F32)
        wrbc = sb("wrbc", [128, D], F32)
        wrsum = sb("wrsum", [128, 1], F32)
        wrrow = sb("wrrow", [1, D], F32)
        onesfs = sb("onesfs", [1, 128], F32)
        idents = sb("idents", [128, 128], BF16)
        sct = sb("sct", [128, D], F32)
        psa = ps("psa", [128, 512])
        ptr = ps("ptr", [128, 1024], BF16)
        ptr2 = ps("ptr2", [128, 1024], BF16)
        slc = sem("slc"); sl0 = sem("sl0"); sl1 = sem("sl1")
        sv = sem("sv"); sa = sem("sa"); sp = sem("sp"); se = sem("se")
        st0 = sem("st0"); st1 = sem("st1"); stf = sem("stf")
        blk = es.enter_context(nc.Block())
        slp = [sl0, sl1]
        stp = [st0, st1]
        xb = [xt, xt2]
        tb = [tok, tok2]
        lb = [lnb, lnb2]
        NV = 9

        @blk.sync
        def _(s_):
            s_.dma_start(out=wrrow[:], in_=wr).then_inc(slc, 16)
            s_.dma_start(out=onesfs[:], in_=onesf).then_inc(slc, 16)
            s_.dma_start(out=idents[:], in_=identb).then_inc(slc, 16)
            for i in range(nt):
                if i >= 2:
                    s_.wait_ge(stp[i % 2], 32 * (i // 2))
                s_.dma_start(out=xb[i % 2][:], in_=xs[ts(i, 128), :]).then_inc(slp[i % 2], 16)

        @blk.tensor
        def _(t):
            t.wait_ge(slc, 48)
            for j in range(4):
                if j:
                    t.wait_ge(sv, j)
                t.matmul(psa[:, 0:512], onesfs[:, :], wrrow[:, j * 512:(j + 1) * 512],
                         start=True, stop=True).then_inc(sp, 1)
            for i in range(nt):
                t.wait_ge(sv, 6 + i * NV + 5)
                for g in range(2):
                    if i * 2 + g >= 2:
                        t.wait_ge(se, i * 2 + g - 1)
                    bank = ptr if g == 0 else ptr2
                    for k in range(8):
                        dt = g * 8 + k
                        mm = t.transpose(bank[:, k * 128:(k + 1) * 128],
                                         tb[i % 2][:, dt * 128:(dt + 1) * 128], idents[:])
                    mm.then_inc(sp, 1)

        @blk.vector
        def _(v):
            for j in range(4):
                v.wait_ge(sp, j + 1)
                v.tensor_copy(wrbc[:, j * 512:(j + 1) * 512], psa[:, 0:512]).then_inc(sv, 1)
            v.wait_ge(sv, 4)
            v.tensor_reduce(wrsum[:], wrbc[:], AX.X, ALU.add).then_inc(sv, 2)
            for i in range(nt):
                base = 6 + i * NV
                x = xb[i % 2]
                v.wait_ge(slp[i % 2], 16 * (i // 2 + 1))
                for q in range(4):
                    mm = v.bn_stats(sm[:, 6 * q:6 * q + 6], x[:, 512 * q:512 * (q + 1)])
                mm.then_inc(sv, 1)
                v.wait_ge(sv, base + 1)
                v.bn_aggr(sm[:, 24:26], sm[:, 0:24]).then_inc(sv, 1)
                v.wait_ge(sv, base + 2)
                v.tensor_scalar(sm[:, 26:27], sm[:, 25:26], 1e-5, None, ALU.add
                                ).then_inc(sv, 1)
                v.wait_ge(sv, base + 3)
                v.reciprocal(sm[:, 27:28], sm[:, 26:27]).then_inc(sv, 1)
                v.wait_ge(sa, i + 1)
                if i >= 2:
                    v.wait_ge(stp[i % 2], 32 * (i // 2))
                v.tensor_scalar(tb[i % 2][:], x[:], sm[:, 24:25], sm[:, 28:29],
                                ALU.subtract, ALU.mult)
                v.tensor_tensor(sct[:], x[:], wrbc[:], ALU.mult).then_inc(sv, 1)
                v.wait_ge(sv, base + 5)
                v.tensor_reduce(sm[:, 30:31], sct[:], AX.X, ALU.add).then_inc(sv, 1)
                v.wait_ge(sv, base + 6)
                v.tensor_tensor(sm[:, 32:33], sm[:, 24:25], wrsum[:], ALU.mult
                                ).then_inc(sv, 1)
                v.wait_ge(sv, base + 7)
                v.tensor_tensor(sm[:, 33:34], sm[:, 30:31], sm[:, 32:33], ALU.subtract
                                ).then_inc(sv, 1)
                v.wait_ge(sv, base + 8)
                v.tensor_tensor(sm[:, 40 + i:41 + i], sm[:, 33:34],
                                sm[:, 28:29], ALU.mult).then_inc(sv, 1)
                for g in range(2):
                    v.wait_ge(sp, 4 + i * 2 + g + 1)
                    bank = ptr if g == 0 else ptr2
                    v.tensor_copy(
                        lb[i % 2].rearrange("p (dt q) -> p dt q", q=128)[
                            :, g * 8:(g + 1) * 8, :],
                        bank[:].rearrange("p (k q) -> p k q", q=128)).then_inc(se, 1)

        @blk.scalar
        def _(sc):
            for i in range(nt):
                base = 6 + i * NV
                sc.wait_ge(sv, base + 4)
                sc.activation(sm[:, 28:29], sm[:, 27:28], AF.Sqrt).then_inc(sa, 1)
                sc.wait_ge(sv, base + 5)
                sc.dma_start(out=lntok[ts(i, 128), :], in_=tb[i % 2][:]
                             ).then_inc(stp[i % 2], 16)
                sc.wait_ge(se, (i + 1) * 2)
                sc.dma_start(out=lnT.rearrange("(dt p) n -> p dt n", p=128)[:, :, ts(i, 128)],
                             in_=lb[i % 2].rearrange("p (dt q) -> p dt q", q=128)
                             ).then_inc(stp[i % 2], 16)
            sc.wait_ge(sv, 6 + nt * NV)
            sc.dma_start(out=sco[:, :], in_=sm[:, 40:40 + nt]).then_inc(stf, 16)
            sc.wait_ge(st0, 32 * ((nt + 1) // 2))
            sc.wait_ge(st1, 32 * (nt // 2))
            sc.wait_ge(stf, 16)

    return nc


def build_att(TT, nq, nkv, hbias):
    nc = bass.Bass("TRN2", target_bir_lowering=False)
    nch = TT // 512
    ntt = TT // 128
    kvmap = [h * nkv // nq for h in range(nq)]
    nio = nq + 2 * nkv

    lnT = nc.dram_tensor("lnT", [16, 128, TT], BF16, kind="ExternalInput").ap()
    wqkv = nc.dram_tensor("wqkv", [D, nio * 128], F32, kind="ExternalInput").ap()
    if hbias:
        bqc = nc.dram_tensor("bqc", [128, nq], F32, kind="ExternalInput").ap()
        bkc = nc.dram_tensor("bkc", [128, nkv], F32, kind="ExternalInput").ap()
        bvc = nc.dram_tensor("bvc", [128, nkv], F32, kind="ExternalInput").ap()
    identb = nc.dram_tensor("identb", [128, 128], BF16, kind="ExternalInput").ap()
    onesb = nc.dram_tensor("onesb", [128, 1], BF16, kind="ExternalInput").ap()
    onesf = nc.dram_tensor("onesf", [1, 128], F32, kind="ExternalInput").ap()
    dmask = nc.dram_tensor("dmask", [4, 128, 512], F32, kind="ExternalInput").ap()
    ctxo = nc.dram_tensor("ctxo", [nq * 128, TT], BF16, kind="ExternalOutput").ap()

    wqkvb = nc.dram_tensor("wqkvb", [D, nio * 128], BF16).ap()

    def nkj(qc):
        return 4 * (qc + 1)

    def cumS(h, qc):
        tot = 0
        for hh in range(nq):
            for qq in range(nch):
                if (hh, qq) < (h, qc):
                    tot += nkj(qq)
        return tot

    es = ExitStack()
    sb = lambda n, shp, dt: es.enter_context(nc.sbuf_tensor(n, shp, dt))
    ps = lambda n, shp, dt=F32: es.enter_context(nc.psum_tensor(n, shp, dt))
    sem = lambda n: es.enter_context(nc.semaphore(n))
    with es:
        lnc = sb("lnc", [128, 16 * 512], BF16)
        lnc2 = sb("lnc2", [128, 16 * 512], BF16)
        wqs = sb("wqs", [128, 16 * nio * 128], BF16)
        qT = sb("qT", [128, nq * TT], BF16)
        kT = sb("kT", [128, nkv * TT], BF16)
        vT = sb("vT", [128, nkv * TT], BF16)
        vtok = sb("vtok", [128, nkv * TT], BF16)
        prb0 = sb("prb0", [128, 512], BF16)
        prb1 = sb("prb1", [128, 512], BF16)
        rrow = sb("rrow", [1, 512], F32)
        rbc = sb("rbc", [128, 512], F32)
        ctxc = sb("ctxc", [128, nq * 512], BF16)
        idents = sb("idents", [128, 128], BF16)
        onec = sb("onec", [128, 1], BF16)
        onesfs = sb("onesfs", [1, 128], F32)
        masks = sb("masks", [128, 2048], F32)
        bqs = sb("bqs", [128, max(nq, 1)], F32)
        bks = sb("bks", [128, max(nkv, 1)], F32)
        bvs = sb("bvs", [128, max(nkv, 1)], F32)
        psA = ps("psA", [128, 512])
        psB = ps("psB", [128, 512])
        psCp = [ps("psC0", [128, 512]), ps("psC1", [128, 512])]
        psDp = [ps("psD0", [128, 512]), ps("psD1", [128, 512])]
        psE = ps("psE", [128, 512])
        ptr = ps("ptr", [128, 1024], BF16)
        sg = sem("sg"); slw = sem("slw"); sl0 = sem("sl0"); sl1 = sem("sl1")
        s_qp = sem("s_qp"); s_qv = sem("s_qv")
        s_tr = sem("s_tr"); s_te = sem("s_te"); s_s = sem("s_s"); s_d = sem("s_d")
        s_e = sem("s_e"); s_av = sem("s_av"); s_r = sem("s_r"); s_b = sem("s_b")
        s_v2 = sem("s_v2"); st0 = sem("st0"); st1 = sem("st1")
        blk = es.enter_context(nc.Block())
        slp = [sl0, sl1]
        stp = [st0, st1]
        lncb = [lnc, lnc2]
        prb = [prb0, prb1]
        NLD = 7 + (3 if hbias else 0) + 16

        @blk.gpsimd
        def _(g):
            g.dma_start(out=wqkvb, in_=wqkv).then_inc(sg, 16)

        @blk.sync
        def _(s_):
            s_.dma_start(out=idents[:], in_=identb).then_inc(slw, 16)
            s_.dma_start(out=onec[:], in_=onesb).then_inc(slw, 16)
            s_.dma_start(out=onesfs[:], in_=onesf).then_inc(slw, 16)
            for v_ in range(4):
                s_.dma_start(out=masks[:, v_ * 512:(v_ + 1) * 512], in_=dmask[v_, :, :]
                             ).then_inc(slw, 16)
            if hbias:
                s_.dma_start(out=bqs[:], in_=bqc).then_inc(slw, 16)
                s_.dma_start(out=bks[:], in_=bkc).then_inc(slw, 16)
                s_.dma_start(out=bvs[:], in_=bvc).then_inc(slw, 16)
            s_.wait_ge(sg, 16)
            for dm in range(16):
                s_.dma_start(out=wqs[:, dm * nio * 128:(dm + 1) * nio * 128],
                             in_=wqkvb[ts(dm, 128), :]).then_inc(slw, 16)
            for c in range(nch):
                if c >= 2:
                    s_.wait_ge(s_qp, (c - 1) * nio)
                s_.dma_start(out=lncb[c % 2].rearrange("p (dt q) -> p dt q", q=512),
                             in_=lnT[:, :, ts(c, 512)].rearrange("dt p q -> p dt q")
                             ).then_inc(slp[c % 2], 16)

        @blk.tensor
        def _(t):
            def warmup(nmm):
                for _ in range(nmm):
                    t.matmul(psE[:, 0:512], idents[:, :], qT[:, 0:512],
                             start=True, stop=True)
            t.wait_ge(slw, NLD * 16)
            for c in range(nch):
                t.wait_ge(slp[c % 2], 16 * (c // 2 + 1))
                for io in range(nio):
                    u = c * nio + io
                    if u >= 2:
                        t.wait_ge(s_qv, u - 1)
                    bank = psA if io % 2 == 0 else psB
                    for dm in range(16):
                        mm = t.matmul(bank[:, 0:512],
                                      wqs[:, (dm * nio + io) * 128:(dm * nio + io + 1) * 128],
                                      lncb[c % 2][:, dm * 512:(dm + 1) * 512],
                                      start=(dm == 0), stop=(dm == 15))
                        if dm == 15:
                            mm.then_inc(s_qp, 1)
            t.wait_ge(s_qv, nch * nio)
            ngr = (nkv * ntt) // 8
            for g in range(ngr):
                if g >= 1:
                    t.wait_ge(s_te, g)
                for k in range(8):
                    i = g * 8 + k
                    mm = t.transpose(ptr[:, k * 128:(k + 1) * 128],
                                     vT[:, i * 128:(i + 1) * 128], idents[:])
                mm.then_inc(s_tr, 1)
            t.wait_ge(s_te, ngr)
            warmup(14)

            def bcast(np_):
                if np_ >= 1:
                    t.wait_ge(s_v2, 2 * np_ - 1)
                t.wait_ge(s_r, np_ + 1)
                t.matmul(psE[:, 0:512], onesfs[:, :], rrow[:, 0:512],
                         start=True, stop=True).then_inc(s_b, 1)
            units = []
            for h in range(nq):
                for qc in range(nch):
                    for kj in range(nkj(qc)):
                        units.append((h, qc, kj, h * nch + qc, nkj(qc)))

            def emitS(j):
                h, qc, kj, npair, nk = units[j]
                kv = kvmap[h]
                bank = psA if kj % 2 == 0 else psB
                if j >= 2:
                    t.wait_ge(s_e, j - 1)
                t.matmul(bank[:, 0:512],
                         kT[:, kv * TT + kj * 128:kv * TT + (kj + 1) * 128],
                         qT[:, h * TT + qc * 512:h * TT + (qc + 1) * 512],
                         start=True, stop=True).then_inc(s_s, 1)

            emitS(0)
            for i, (h, qc, kj, npair, nk) in enumerate(units):
                if i + 1 < len(units):
                    emitS(i + 1)
                kv = kvmap[h]
                t.wait_ge(s_e, i + 1)
                if kj == 0 and npair >= 2:
                    t.wait_ge(s_v2, 2 * npair - 2)
                t.matmul(psCp[npair % 2][:, 0:512],
                         vtok[:, kv * TT + kj * 128:kv * TT + (kj + 1) * 128],
                         prb[kj % 2][:], start=(kj == 0), stop=(kj == nk - 1))
                if kj == 0 and npair >= 2:
                    t.wait_ge(s_r, npair - 1)
                t.matmul(psDp[npair % 2][0:1, 0:512], onec[:], prb[kj % 2][:],
                         start=(kj == 0), stop=(kj == nk - 1)).then_inc(s_av, 1)
                if kj == 2 and npair >= 1:
                    bcast(npair - 1)

            bcast(nq * nch - 1)

        @blk.vector
        def _(v):
            for c in range(nch):
                for io in range(nio):
                    u = c * nio + io
                    v.wait_ge(s_qp, u + 1)
                    bank = psA if io % 2 == 0 else psB
                    if io < nq:
                        dst = qT[:, io * TT + c * 512:io * TT + (c + 1) * 512]
                        bias = bqs[:, io:io + 1] if hbias else None
                    elif io < nq + nkv:
                        kv = io - nq
                        dst = kT[:, kv * TT + c * 512:kv * TT + (c + 1) * 512]
                        bias = bks[:, kv:kv + 1] if hbias else None
                    else:
                        kv = io - nq - nkv
                        dst = vT[:, kv * TT + c * 512:kv * TT + (c + 1) * 512]
                        bias = bvs[:, kv:kv + 1] if hbias else None
                    if bias is not None:
                        v.tensor_scalar(dst, bank[:, 0:512], bias, None, ALU.add
                                        ).then_inc(s_qv, 1)
                    else:
                        v.tensor_copy(dst, bank[:, 0:512]).then_inc(s_qv, 1)
            for g in range((nkv * ntt) // 8):
                v.wait_ge(s_tr, g + 1)
                v.tensor_copy(vtok[:, g * 1024:(g + 1) * 1024], ptr[:]).then_inc(s_te, 1)
            def epi(np_, cs_, nk_):
                v.wait_ge(s_av, cs_ + nk_)
                v.reciprocal(rrow[:, 0:512], psDp[np_ % 2][0:1, 0:512]
                             ).then_inc(s_r, 1)
                v.wait_ge(s_b, np_ + 1)
                v.tensor_copy(rbc[:], psE[:, 0:512]).then_inc(s_v2, 1)
                v.wait_ge(s_v2, 2 * np_ + 1)
                if np_ >= 2:
                    v.wait_ge(stp[np_ % 2], 16 * (np_ // 2))
                v.tensor_tensor(ctxc[:, (np_ % 2) * 512:(np_ % 2 + 1) * 512],
                                psCp[np_ % 2][:, 0:512], rbc[:], ALU.mult
                                ).then_inc(s_v2, 1)
            prev = None
            for h in range(nq):
                for qc in range(nch):
                    npair = h * nch + qc
                    cs = cumS(h, qc)
                    nk = nkj(qc)
                    done = False
                    for kj in range(4 * qc, nk):
                        if kj >= 3 and not done and prev is not None:
                            epi(*prev)
                            done = True
                        bank = psA if kj % 2 == 0 else psB
                        vm = kj - 4 * qc
                        v.wait_ge(s_s, cs + kj + 1)
                        v.tensor_tensor(bank[:, 0:512], bank[:, 0:512],
                                        masks[:, vm * 512:(vm + 1) * 512], ALU.add
                                        ).then_inc(s_d, 1)
                    if not done and prev is not None:
                        epi(*prev)
                    prev = (npair, cs, nk)
            epi(*prev)

        @blk.scalar
        def _(sc):
            def store(np_):
                h_, qc_ = np_ // nch, np_ % nch
                sc.wait_ge(s_v2, 2 * np_ + 2)
                sc.dma_start(out=ctxo[ts(h_, 128), ts(qc_, 512)],
                             in_=ctxc[:, (np_ % 2) * 512:(np_ % 2 + 1) * 512]
                             ).then_inc(stp[np_ % 2], 16)
            pv = None
            for h in range(nq):
                for qc in range(nch):
                    npair = h * nch + qc
                    cs = cumS(h, qc)
                    nk = nkj(qc)
                    for kj in range(nk):
                        diag = kj >= 4 * qc
                        bank = psA if kj % 2 == 0 else psB
                        if diag:
                            sc.wait_ge(s_d, 4 * npair + (kj - 4 * qc) + 1)
                        else:
                            sc.wait_ge(s_s, cs + kj + 1)
                        if cs + kj >= 2:
                            sc.wait_ge(s_av, cs + kj - 1)
                        sc.activation(prb[kj % 2][:], bank[:, 0:512], AF.Exp,
                                      scale=ISQ).then_inc(s_e, 1)
                    if pv is not None:
                        store(pv)
                    pv = npair
            store(pv)
            npt = nq * nch
            sc.wait_ge(st0, 16 * ((npt + 1) // 2))
            sc.wait_ge(st1, 16 * (npt // 2))

    return nc


def _consts():
    C = {}
    C["identb"] = np.eye(128).astype(BF)
    C["identf"] = np.eye(128, dtype=np.float32)
    C["onesf"] = np.ones((1, 128), np.float32)
    C["onesb"] = np.ones((128, 1), BF)
    C["onescf"] = np.ones((128, 1), np.float32)
    dm = np.zeros((4, 128, 512), np.float32)
    for v in range(4):
        kj = v * 128 + np.arange(128)[:, None]
        qi = np.arange(512)[None, :]
        dm[v] = np.where(kj > qi, NEG, 0.0)
    C["dmask"] = dm
    rot = np.zeros((128, 128), np.float32)
    for i in range(64):
        rot[64 + i, i] = -1.0
        rot[i, 64 + i] = 1.0
    C["rotb"] = rot.astype(BF)
    C["iota1k"] = np.broadcast_to(np.arange(1024, dtype=np.float32), (128, 1024)).copy()
    C["trif"] = (np.arange(128)[:, None] <= np.arange(128)[None, :]).astype(np.float32)
    C["tcolf"] = np.arange(128, dtype=np.float32)[:, None]
    C["invfs"] = np.concatenate(
        [1.0 / RTH ** (np.arange(0, 128, 2) / 128.0)] * 2)[None, :].astype(np.float32)
    return C


def _run(nc, maps):
    return run_bass_kernel_spmd(nc, maps, core_ids=list(range(NCORE)))


def kernel(**inputs):
    f = lambda k: np.ascontiguousarray(np.asarray(inputs[k], np.float32))
    C = _consts()
    x0 = f("L0_attn_a_x").reshape(T, D)
    wrr = f("L0_ffn_w_router").reshape(1, D)
    p_ln = _cached("ln", lambda: build_ln(1024))
    p_atta = _cached("atta", lambda: build_att(T, 2, 1, False))
    p_attb = _cached("attb", lambda: build_att(T, 2, 2, True))
    p_proj = _cached("proj", lambda: build_proj(1024))
    p_fsel = _cached("fsel", lambda: build_fsel())
    p_fmain = _cached("fmain", lambda: build_fmain())
    p_scat = _cached("scat", lambda: build_scat(1024))
    p_cast = _cached("cast", lambda: build_cast())

    def L_ln(x):
        maps = [{"xs": np.ascontiguousarray(x[c * 1024:(c + 1) * 1024]), "wr": wrr,
                 "onesf": C["onesf"], "identb": C["identb"]} for c in range(NCORE)]
        r = _run(p_ln, maps)
        lntok = np.concatenate([np.asarray(r.results[c]["lntok"]) for c in range(NCORE)], 0)
        lnT = np.concatenate([np.asarray(r.results[c]["lnT"]).reshape(16, 128, 1024)
                              for c in range(NCORE)], 2)
        sco = np.concatenate([np.asarray(r.results[c]["sco"]) for c in range(NCORE)], 1)
        return lntok, lnT, sco

    def L_att(prog, lnT, mk):
        maps = []
        for c in range(NCORE):
            m = mk(c)
            m.update({"lnT": lnT, "identb": C["identb"], "onesb": C["onesb"],
                      "onesf": C["onesf"], "dmask": C["dmask"]})
            maps.append(m)
        r = _run(prog, maps)
        return np.concatenate([np.asarray(r.results[c]["ctxo"]) for c in range(NCORE)], 0)

    def L_proj(ctxT, xprev, W, brow):
        onesbr = np.ones((1, 128), BF)
        maps = [{"cT": np.ascontiguousarray(ctxT[:, c * 1024:(c + 1) * 1024]
                                            ).reshape(16, 128, 1024),
                 "xprev": np.ascontiguousarray(xprev[c * 1024:(c + 1) * 1024]),
                 "Wb": W, "brow": brow, "onesbr": onesbr} for c in range(NCORE)]
        r = _run(p_proj, maps)
        return np.concatenate([np.asarray(r.results[c]["xnew"]) for c in range(NCORE)], 0)

    fwq, fwk, fwv = f("L0_ffn_wq"), f("L0_ffn_wk"), f("L0_ffn_wv")
    fw3 = np.ascontiguousarray(np.concatenate([fwq, fwk, fwv], 1))
    biasc = np.ascontiguousarray(np.concatenate(
        [f("L0_ffn_bq").ravel(), f("L0_ffn_bk").ravel(), f("L0_ffn_bv").ravel()]
        ).reshape(32, 128).T)
    ln1c = np.ascontiguousarray(f("L0_ffn_ln1_w").reshape(16, 128).T)
    ln2bc = np.broadcast_to(f("L0_ffn_ln2_w").reshape(1, D), (128, D)).copy()
    fwg, fwu = f("L0_ffn_w_gate"), f("L0_ffn_w_up")
    fwd, fwo = f("L0_ffn_w_down"), f("L0_ffn_wo")
    woa, wpb = f("L0_attn_a_Wo"), f("L0_attn_b_w_proj")
    allf = np.concatenate([fwg.ravel(), fwu.ravel(), fwd.ravel(), fw3.ravel(),
                           fwo.ravel(), woa.ravel(), wpb.ravel()]).reshape(8, 1088, 8192)
    rc = _run(p_cast, [{"cin": np.ascontiguousarray(allf[c])} for c in range(NCORE)])
    allb = np.concatenate([np.asarray(rc.results[c]["cout"]).ravel()
                           for c in range(NCORE)])
    E = 2048 * 8192
    fwgb, fwub = allb[0:E].reshape(D, IMLP), allb[E:2 * E].reshape(D, IMLP)
    fwdb = allb[2 * E:3 * E].reshape(IMLP, D)
    fw3b = allb[3 * E:3 * E + D * 4096].reshape(D, 4096)
    o = 3 * E + D * 4096
    fwob = allb[o:o + D * D].reshape(D, D)
    woab = allb[o + D * D:o + 2 * D * D].reshape(D, D)
    wpbb = allb[o + 2 * D * D:o + 3 * D * D].reshape(D, D)
    amasks = [np.where(np.arange(SSEL)[None, :] <= (128 * c + np.arange(128))[:, None],
                       0.0, NEG).astype(np.float32) for c in range(NCORE)]

    def ffn(xin, lntok, sco):
        maps = [{"sco": sco, "lnfull": lntok, "fw3b": fw3b, "ln1c": ln1c,
                 "biasc": biasc, "sbase": np.array([[c * 128]], np.uint32),
                 "invfs": C["invfs"], "identf": C["identf"], "identb": C["identb"],
                 "rotb": C["rotb"], "iota1k": C["iota1k"], "trif": C["trif"],
                 "tcolf": C["tcolf"], "onescf": C["onescf"], "onesf": C["onesf"]}
                for c in range(NCORE)]
        rs = _run(p_fsel, maps)
        kkTf = np.concatenate([np.asarray(rs.results[c]["kkTs"]) for c in range(NCORE)], 1)
        vfull = np.concatenate([np.asarray(rs.results[c]["vso"]) for c in range(NCORE)], 0)
        posco = np.asarray(rs.results[0]["posco"])
        acolo = np.asarray(rs.results[0]["acolo"])
        maps = [{"qTs": np.asarray(rs.results[c]["qTs"]), "kkT": kkTf, "vfull": vfull,
                 "seltok": np.asarray(rs.results[c]["selo"]),
                 "myscol": np.ascontiguousarray(posco[1, c * 128:(c + 1) * 128]
                                                ).reshape(128, 1).astype(np.float32),
                 "amask": amasks[c], "fwob": fwob, "fwgb": fwgb, "fwub": fwub,
                 "fwdb": fwdb, "ln2bc": ln2bc, "identb": C["identb"]}
                for c in range(NCORE)]
        rm = _run(p_fmain, maps)
        gated = np.concatenate([np.asarray(rm.results[c]["gato"]) for c in range(NCORE)], 0)
        maps = [{"xprev": np.ascontiguousarray(xin[c * 1024:(c + 1) * 1024]),
                 "lntk": np.ascontiguousarray(lntok[c * 1024:(c + 1) * 1024]),
                 "gated": gated,
                 "acol": np.ascontiguousarray(acolo[:, 8 * c:8 * c + 8])}
                for c in range(NCORE)]
        rr = _run(p_scat, maps)
        return np.concatenate([np.asarray(rr.results[c]["xnew"]) for c in range(NCORE)], 0)

    _, lnT0, _ = L_ln(x0)
    wq, wk, wv = f("L0_attn_a_Wq"), f("L0_attn_a_Wk"), f("L0_attn_a_Wv")
    ctxa = L_att(p_atta, lnT0, lambda c: {"wqkv": np.ascontiguousarray(np.concatenate(
        [wq[:, 256 * c:256 * (c + 1)], wk[:, 128 * (c // 2):128 * (c // 2 + 1)],
         wv[:, 128 * (c // 2):128 * (c // 2 + 1)]], 1))})
    x1 = L_proj(ctxa, x0, woab, f("L0_attn_a_bo").reshape(1, D))

    lntok1, _, sco1 = L_ln(x1)
    x2 = ffn(x1, lntok1, sco1)

    _, lnT2, _ = L_ln(x2)
    wab, bab = f("L0_attn_b_w_attn"), f("L0_attn_b_b_attn").ravel()

    def mk_b(c):
        sl = lambda o: wab[:, o + 256 * c:o + 256 * (c + 1)]
        bsl = lambda o: np.ascontiguousarray(
            bab[o + 256 * c:o + 256 * (c + 1)].reshape(2, 128).T)
        return {"wqkv": np.ascontiguousarray(np.concatenate(
                    [sl(0), sl(2048), sl(4096)], 1)),
                "bqc": bsl(0), "bkc": bsl(2048), "bvc": bsl(4096)}
    ctxb = L_att(p_attb, lnT2, mk_b)
    x3 = L_proj(ctxb, x2, wpbb, f("L0_attn_b_b_proj").reshape(1, D))

    lntok3, _, sco3 = L_ln(x3)
    x4 = ffn(x3, lntok3, sco3)
    return np.ascontiguousarray(x4.reshape(1, T, D).astype(np.float32))


def build_proj(NT=1024):
    nc = bass.Bass("TRN2", target_bir_lowering=False)
    nt = NT // 128
    cT = nc.dram_tensor("cT", [16, 128, NT], BF16, kind="ExternalInput").ap()
    xprev = nc.dram_tensor("xprev", [NT, D], F32, kind="ExternalInput").ap()
    brow = nc.dram_tensor("brow", [1, D], F32, kind="ExternalInput").ap()
    onesbr = nc.dram_tensor("onesbr", [1, 128], BF16, kind="ExternalInput").ap()
    xnew = nc.dram_tensor("xnew", [NT, D], F32, kind="ExternalOutput").ap()
    Wb = nc.dram_tensor("Wb", [D, D], BF16, kind="ExternalInput").ap()
    bb = nc.dram_tensor("bb", [1, D], BF16).ap()
    es = ExitStack()
    sb = lambda n, shp, dt: es.enter_context(nc.sbuf_tensor(n, shp, dt))
    ps = lambda n, shp, dt=F32: es.enter_context(nc.psum_tensor(n, shp, dt))
    sem = lambda n: es.enter_context(nc.semaphore(n))
    with es:
        cts = sb("cts", [128, 16 * NT], BF16)
        ws = sb("ws", [128, 16 * 2048], BF16)
        xt = sb("xt", [128, D], F32)
        xt2 = sb("xt2", [128, D], F32)
        xo = sb("xo", [128, D], F32)
        xo2 = sb("xo2", [128, D], F32)
        bbs = sb("bbs", [1, D], BF16)
        onbr = sb("onbr", [1, 128], BF16)
        psA = ps("psA", [128, 512])
        psB = ps("psB", [128, 512])
        sg = sem("sg"); slw = sem("slw"); sl0 = sem("sl0"); sl1 = sem("sl1")
        sp = sem("sp"); sv = sem("sv"); st0 = sem("st0"); st1 = sem("st1")
        sw2 = sem("sw2")
        blk = es.enter_context(nc.Block())
        slp = [sl0, sl1]
        stp = [st0, st1]
        xb = [xt, xt2]
        xob = [xo, xo2]

        @blk.gpsimd
        def _(g):
            g.dma_start(out=bb, in_=brow).then_inc(sg, 16)

        @blk.sync
        def _(s_):
            s_.dma_start(out=onbr[:], in_=onesbr).then_inc(slw, 16)
            s_.dma_start(out=cts.rearrange("p (dt n) -> p dt n", n=NT),
                         in_=cT.rearrange("dt p n -> p dt n")).then_inc(slw, 16)
            s_.wait_ge(sg, 16)
            s_.dma_start(out=bbs[:], in_=bb).then_inc(slw, 16)
            for dm in range(0, 16, 2):
                s_.dma_start(out=ws[:, dm * 2048:(dm + 1) * 2048],
                             in_=Wb[ts(dm, 128), :]).then_inc(slw, 16)
            for i in range(nt):
                if i >= 2:
                    s_.wait_ge(stp[i % 2], 16 * (i // 2))
                s_.dma_start(out=xb[i % 2][:], in_=xprev[ts(i, 128), :]
                             ).then_inc(slp[i % 2], 16)

        @blk.tensor
        def _(t):
            t.wait_ge(slw, 11 * 16)
            t.wait_ge(sw2, 8 * 16)
            for i in range(nt):
                for ch in range(4):
                    u = i * 4 + ch
                    if u >= 2:
                        t.wait_ge(sv, u - 1)
                    bank = psA if u % 2 == 0 else psB
                    for dm in range(16):
                        t.matmul(bank[:, 0:512],
                                 cts[:, dm * NT + i * 128:dm * NT + (i + 1) * 128],
                                 ws[:, dm * 2048 + ch * 512:dm * 2048 + (ch + 1) * 512],
                                 start=(dm == 0), stop=False)
                    t.matmul(bank[:, 0:512], onbr[:, :],
                             bbs[:, ch * 512:(ch + 1) * 512],
                             start=False, stop=True).then_inc(sp, 1)

        @blk.vector
        def _(v):
            for i in range(nt):
                v.wait_ge(slp[i % 2], 16 * (i // 2 + 1))
                if i >= 2:
                    v.wait_ge(stp[i % 2], 16 * (i // 2))
                for ch in range(4):
                    u = i * 4 + ch
                    v.wait_ge(sp, u + 1)
                    bank = psA if u % 2 == 0 else psB
                    v.scalar_tensor_tensor(
                        out=xob[i % 2][:, ch * 512:(ch + 1) * 512], in0=bank[:, 0:512],
                        scalar=1.0, in1=xb[i % 2][:, ch * 512:(ch + 1) * 512],
                        op0=ALU.mult, op1=ALU.add).then_inc(sv, 1)

        @blk.scalar
        def _(sc):
            for dm in range(1, 16, 2):
                sc.dma_start(out=ws[:, dm * 2048:(dm + 1) * 2048],
                             in_=Wb[ts(dm, 128), :]).then_inc(sw2, 16)
            for i in range(nt):
                sc.wait_ge(sv, i * 4 + 4)
                sc.dma_start(out=xnew[ts(i, 128), :], in_=xob[i % 2][:]
                             ).then_inc(stp[i % 2], 16)
            sc.wait_ge(st0, 16 * ((nt + 1) // 2))
            sc.wait_ge(st1, 16 * (nt // 2))

    return nc


def build_scat(NT=1024):
    nc = bass.Bass("TRN2", target_bir_lowering=False)
    nt = NT // 128
    xprev = nc.dram_tensor("xprev", [NT, D], F32, kind="ExternalInput").ap()
    lntk = nc.dram_tensor("lntk", [NT, D], BF16, kind="ExternalInput").ap()
    gated = nc.dram_tensor("gated", [SSEL, D], BF16, kind="ExternalInput").ap()
    acol = nc.dram_tensor("acol", [128, 8], U32, kind="ExternalInput").ap()
    xnew = nc.dram_tensor("xnew", [NT, D], F32, kind="ExternalOutput").ap()
    es = ExitStack()
    sb = lambda n, shp, dt: es.enter_context(nc.sbuf_tensor(n, shp, dt))
    sem = lambda n: es.enter_context(nc.semaphore(n))
    with es:
        xt = sb("xt", [128, D], F32)
        xt2 = sb("xt2", [128, D], F32)
        lt = sb("lt", [128, D], BF16)
        lt2 = sb("lt2", [128, D], BF16)
        gb = sb("gb", [128, D], BF16)
        gb2 = sb("gb2", [128, D], BF16)
        xo = sb("xo", [128, D], F32)
        xo2 = sb("xo2", [128, D], F32)
        acs = sb("acs", [128, 8], U32)
        af = sb("af", [128, 8], F32)
        mcol = sb("mcol", [128, 8], F32)
        sg0 = sem("sg0"); sg1 = sem("sg1")
        sl0 = sem("sl0"); sl1 = sem("sl1"); sla = sem("sla"); smz = sem("smz")
        sv = sem("sv"); sz = sem("sz")
        st0 = sem("st0"); st1 = sem("st1")
        blk = es.enter_context(nc.Block())
        sgp = [sg0, sg1]
        slp = [sl0, sl1]
        stp = [st0, st1]
        xb = [xt, xt2]
        lb = [lt, lt2]
        gbb = [gb, gb2]
        xob = [xo, xo2]

        @blk.gpsimd
        def _(g):
            g.dma_start(out=acs[:], in_=acol).then_inc(sla, 16)
            g.memset(gb[:], 0.0).then_inc(smz, 1)
            g.memset(gb2[:], 0.0).then_inc(smz, 1)
            g.wait_ge(sla, 16)
            g.wait_ge(smz, 2)
            for i in range(nt):
                if i >= 2:
                    g.wait_ge(sv, 3 * (i - 1))
                g.indirect_dma_start(
                    gbb[i % 2][:], None, gated,
                    bass.IndirectOffsetOnAxis(ap=acs[:, i:i + 1], axis=0),
                    bounds_check=SSEL - 1, oob_is_err=False).then_inc(sgp[i % 2], 16)

        @blk.sync
        def _(s_):
            for i in range(nt):
                if i >= 2:
                    s_.wait_ge(stp[i % 2], 16 * (i // 2))
                s_.dma_start(out=xb[i % 2][:], in_=xprev[ts(i, 128), :]
                             ).then_inc(slp[i % 2], 16)
                s_.dma_start(out=lb[i % 2][:], in_=lntk[ts(i, 128), :]
                             ).then_inc(slp[i % 2], 16)

        @blk.vector
        def _(v):
            v.wait_ge(sla, 16)
            v.tensor_copy(af[:], acs[:]).then_inc(sz, 1)
            v.wait_ge(sz, 1)
            v.tensor_scalar(mcol[:], af[:], float(SSEL) - 0.5, None, ALU.is_ge
                            ).then_inc(sz, 1)
            for i in range(nt):
                v.wait_ge(sz, 2)
                v.wait_ge(slp[i % 2], 32 * (i // 2 + 1))
                v.wait_ge(sgp[i % 2], 16 * (i // 2 + 1))
                if i >= 2:
                    v.wait_ge(stp[i % 2], 16 * (i // 2))
                v.scalar_tensor_tensor(
                    out=xob[i % 2][:], in0=lb[i % 2][:], scalar=mcol[:, i:i + 1],
                    in1=xb[i % 2][:], op0=ALU.mult, op1=ALU.add).then_inc(sv, 1)
                v.wait_ge(sv, 3 * i + 1)
                v.tensor_tensor(xob[i % 2][:], xob[i % 2][:], gbb[i % 2][:], ALU.add
                                ).then_inc(sv, 1)
                v.wait_ge(sv, 3 * i + 2)
                v.tensor_scalar(gbb[i % 2][:], gbb[i % 2][:], 0.0, None, ALU.mult
                                ).then_inc(sv, 1)

        @blk.scalar
        def _(sc):
            for i in range(nt):
                sc.wait_ge(sv, 3 * i + 2)
                sc.dma_start(out=xnew[ts(i, 128), :], in_=xob[i % 2][:]
                             ).then_inc(stp[i % 2], 16)
            sc.wait_ge(st0, 16 * ((nt + 1) // 2))
            sc.wait_ge(st1, 16 * (nt // 2))

    return nc


NBIS = 46


class _Sched:
    KINDS = {"vector": ("sv", 1), "tensor": ("sp", 1), "scalar": ("sa", 1),
             "gpsimd": ("sz", 1), "sync_dma": ("sl", 16),
             "scalar_dma": ("sa2", 16), "gpsimd_dma": ("sg", 16)}

    def __init__(self):
        self.steps = []
        self.cnt = {v[0]: 0 for v in self.KINDS.values()}

    def add(self, kind, fn, lag=None):
        semn, inc = self.KINDS[kind]
        waits = {}
        for k, v in self.cnt.items():
            if lag:
                for lk, ln in lag.items():
                    if self.KINDS[lk][0] == k:
                        v = v - ln * self.KINDS[lk][1]
            if v > 0:
                waits[k] = v
        self.steps.append((kind, fn, waits))
        self.cnt[semn] += inc

    def emit_one(self, eng, sems, kinds):
        lw = {}
        for kind, fn, waits in self.steps:
            if kind not in kinds:
                continue
            semn, inc = self.KINDS[kind]
            for k, v in waits.items():
                if lw.get(k, -1) < v:
                    eng.wait_ge(sems[k], v)
                    lw[k] = v
            inst = fn(eng)
            if inst is None:
                raise RuntimeError("step returned no instruction")
            inst.then_inc(sems[semn], inc)


def build_fsel():
    nc = bass.Bass("TRN2", target_bir_lowering=False)
    sco = nc.dram_tensor("sco", [128, 64], F32, kind="ExternalInput").ap()
    lnfull = nc.dram_tensor("lnfull", [T, D], BF16, kind="ExternalInput").ap()
    ln1c = nc.dram_tensor("ln1c", [128, 16], F32, kind="ExternalInput").ap()
    biasc = nc.dram_tensor("biasc", [128, 32], F32, kind="ExternalInput").ap()
    sbase = nc.dram_tensor("sbase", [1, 1], U32, kind="ExternalInput").ap()
    invfs = nc.dram_tensor("invfs", [1, 128], F32, kind="ExternalInput").ap()
    identf = nc.dram_tensor("identf", [128, 128], F32, kind="ExternalInput").ap()
    identb = nc.dram_tensor("identb", [128, 128], BF16, kind="ExternalInput").ap()
    rotb = nc.dram_tensor("rotb", [128, 128], BF16, kind="ExternalInput").ap()
    iota1k = nc.dram_tensor("iota1k", [128, 1024], F32, kind="ExternalInput").ap()
    trif = nc.dram_tensor("trif", [128, 128], F32, kind="ExternalInput").ap()
    tcolf = nc.dram_tensor("tcolf", [128, 1], F32, kind="ExternalInput").ap()
    onescf = nc.dram_tensor("onescf", [128, 1], F32, kind="ExternalInput").ap()
    onesf = nc.dram_tensor("onesf", [1, 128], F32, kind="ExternalInput").ap()

    qTs = nc.dram_tensor("qTs", [16 * 128, 128], BF16, kind="ExternalOutput").ap()
    kkTs = nc.dram_tensor("kkTs", [8 * 128, 128], BF16, kind="ExternalOutput").ap()
    vso = nc.dram_tensor("vso", [128, 1024], BF16, kind="ExternalOutput").ap()
    selo = nc.dram_tensor("selo", [128, D], BF16, kind="ExternalOutput").ap()
    posco = nc.dram_tensor("posco", [2, SSEL], F32, kind="ExternalOutput").ap()
    acolo = nc.dram_tensor("acolo", [128, 64], U32, kind="ExternalOutput").ap()

    fw3b = nc.dram_tensor("fw3b", [D, 4096], BF16, kind="ExternalInput").ap()
    posd = nc.dram_tensor("posd", [SSEL, 1], F32).ap()

    es = ExitStack()
    sb = lambda n, shp, dt: es.enter_context(nc.sbuf_tensor(n, shp, dt))
    ps = lambda n, shp, dt=F32: es.enter_context(nc.psum_tensor(n, shp, dt))
    sem = lambda n: es.enter_context(nc.semaphore(n))
    with es:
        scsb = sb("scsb", [128, 64], F32)
        selw = sb("selw", [128, 64], F32)
        mge = sb("mge", [128, 64], F32)
        mgt = sb("mgt", [128, 64], F32)
        meq = sb("meq", [128, 64], F32)
        Ceq = sb("Ceq", [128, 64], F32)
        Csel = sb("Csel", [128, 64], F32)
        afb = sb("afb", [128, 64], F32)
        acu = sb("acu", [128, 64], U32)
        srow = sb("srow", [1, 16], F32)
        crow1 = sb("crow1", [1, 64], F32)
        crow2 = sb("crow2", [1, 64], F32)
        exbc = sb("exbc", [128, 64], F32)
        midc = sb("midc", [128, 1], F32)
        cntp = sb("cntp", [128, 1], F32)
        needc = sb("needc", [128, 1], F32)
        iot = sb("iot", [128, 1024], F32)
        trs = sb("trs", [128, 128], F32)
        tcs = sb("tcs", [128, 1], F32)
        onc = sb("onc", [128, 1], F32)
        onr = sb("onr", [1, 128], F32)
        idf = sb("idf", [128, 128], F32)
        idb = sb("idb", [128, 128], BF16)
        rob = sb("rob", [128, 128], BF16)
        Pf_ = [sb("Pf0", [128, 1024], F32), sb("Pf1", [128, 1024], F32)]
        stf_ = [sb("stf0", [128, 2], F32), sb("stf1", [128, 2], F32)]
        poscosb = sb("poscosb", [2, 1024], F32)
        sbs = sb("sbs", [1, 1], U32)
        sbf1 = sb("sbf1", [1, 1], F32)
        slotf = sb("slotf", [128, 1], F32)
        slotu = sb("slotu", [128, 1], U32)
        posg = sb("posg", [128, 1], F32)
        myidx = sb("myidx", [128, 1], U32)
        posrow = sb("posrow", [1, 128], F32)
        selrow = sb("selrow", [128, D], BF16)
        sq = sb("sq", [128, D], F32)
        rms = sb("rms", [128, 8], F32)
        hrow = sb("hrow", [128, D], BF16)
        hT = sb("hT", [128, D], BF16)
        fws = sb("fws", [128, 65536], BF16)
        lnc1 = sb("lnc1", [128, 16], F32)
        bcs = sb("bcs", [128, 32], F32)
        qko = sb("qko", [128, 4096], BF16)
        invr = sb("invr", [1, 128], F32)
        angs = sb("angs", [128, 128], F32)
        u1 = sb("u1", [128, 128], F32)
        ri = sb("ri", [128, 128], I32)
        rf = sb("rf", [128, 128], F32)
        sini = sb("sini", [128, 128], F32)
        sinv = sb("sinv", [128, 128], F32)
        cosv = sb("cosv", [128, 128], F32)
        rt1 = sb("rt1", [128, 128], F32)
        rt2 = sb("rt2", [128, 128], F32)
        vstok = sb("vstok", [128, 1024], BF16)
        psF = ps("psF", [128, 512])
        psG = ps("psG", [128, 512])
        psH1 = ps("psH1", [128, 512])
        psH2 = ps("psH2", [128, 512])
        psR = ps("psR", [128, 512])
        ptr = ps("ptr", [128, 1024], BF16)
        SEMS = {n: sem(n) for n in ["sv", "sp", "sa", "sz", "sl", "sa2", "sg"]}
        blk = es.enter_context(nc.Block())

        S = _Sched()

        S.add("gpsimd_dma", lambda g: g.dma_start(out=sbs[:], in_=sbase))
        jobs = [(scsb, sco), (iot, iota1k), (trs, trif), (tcs, tcolf),
                (onc, onescf), (onr, onesf), (idf, identf), (idb, identb),
                (rob, rotb), (lnc1, ln1c), (bcs, biasc), (invr, invfs)]
        for dst, src in jobs:
            S.add("sync_dma", lambda s_, dst=dst, src=src: s_.dma_start(out=dst[:], in_=src))
        BIG = {k: 10 ** 6 for k in _Sched.KINDS}
        for dm in range(16):
            q = "sync_dma" if dm % 2 == 0 else "scalar_dma"
            S.add(q, lambda e, dm=dm: e.dma_start(
                out=fws[:, dm * 4096:(dm + 1) * 4096], in_=fw3b[ts(dm, 128), :]),
                lag=BIG)

        S.add("vector", lambda v: v.tensor_reduce(midc[:], scsb[:], AX.X, ALU.max,
                                                  apply_absolute_value=True))
        S.add("tensor", lambda t: t.transpose(psG[0:1, 0:128], midc[:], idf[:]))
        S.add("vector", lambda v: v.tensor_reduce(srow[:, 4:5], psG[0:1, 0:128],
                                                  AX.X, ALU.max))
        S.add("vector", lambda v: v.tensor_scalar(srow[:, 1:2], srow[:, 4:5], 1.0,
                                                  None, ALU.add))
        S.add("vector", lambda v: v.tensor_scalar(srow[:, 0:1], srow[:, 1:2], -1.0,
                                                  None, ALU.mult))
        for it in range(NBIS):
            S.add("vector", lambda v: v.tensor_tensor(srow[:, 4:5], srow[:, 0:1],
                                                      srow[:, 1:2], ALU.add))
            S.add("vector", lambda v: v.tensor_scalar(srow[:, 2:3], srow[:, 4:5], 0.5,
                                                      None, ALU.mult))
            S.add("tensor", lambda t: t.matmul(psF[:, 0:1], onr[:, :], srow[:, 2:3],
                                               start=True, stop=True))
            S.add("vector", lambda v: v.tensor_copy(midc[:], psF[:, 0:1]))
            S.add("vector", lambda v: v.tensor_scalar(selw[:], scsb[:], midc[:], 0.0,
                                                      ALU.is_ge, ALU.add,
                                                      accum_out=cntp[:]))
            S.add("tensor", lambda t: t.matmul(psG[0:1, 0:1], onc[:], cntp[:],
                                               start=True, stop=True))
            S.add("vector", lambda v: v.tensor_scalar(srow[:, 3:4], psG[0:1, 0:1],
                                                      1023.5, None, ALU.is_ge))
            S.add("vector", lambda v: v.tensor_tensor(srow[:, 5:6], srow[:, 2:3],
                                                      srow[:, 0:1], ALU.subtract))
            S.add("vector", lambda v: v.tensor_tensor(srow[:, 6:7], srow[:, 5:6],
                                                      srow[:, 3:4], ALU.mult))
            S.add("vector", lambda v: v.tensor_tensor(srow[:, 0:1], srow[:, 0:1],
                                                      srow[:, 6:7], ALU.add))
            S.add("vector", lambda v: v.tensor_tensor(srow[:, 7:8], srow[:, 1:2],
                                                      srow[:, 2:3], ALU.subtract))
            S.add("vector", lambda v: v.tensor_tensor(srow[:, 8:9], srow[:, 7:8],
                                                      srow[:, 3:4], ALU.mult))
            S.add("vector", lambda v: v.tensor_tensor(srow[:, 9:10], srow[:, 2:3],
                                                      srow[:, 8:9], ALU.add))
            S.add("vector", lambda v: v.tensor_copy(srow[:, 1:2], srow[:, 9:10]))

        S.add("tensor", lambda t: t.matmul(psF[:, 0:1], onr[:, :], srow[:, 0:1],
                                           start=True, stop=True))
        S.add("vector", lambda v: v.tensor_copy(midc[:], psF[:, 0:1]))
        S.add("vector", lambda v: v.tensor_scalar(mge[:], scsb[:], midc[:], None,
                                                  ALU.is_ge))
        S.add("tensor", lambda t: t.matmul(psF[:, 0:1], onr[:, :], srow[:, 1:2],
                                           start=True, stop=True))
        S.add("vector", lambda v: v.tensor_copy(midc[:], psF[:, 0:1]))
        S.add("vector", lambda v: v.tensor_scalar(mgt[:], scsb[:], midc[:], 0.0,
                                                  ALU.is_ge, ALU.add,
                                                  accum_out=cntp[:]))
        S.add("vector", lambda v: v.tensor_tensor(meq[:], mge[:], mgt[:], ALU.subtract))
        S.add("tensor", lambda t: t.matmul(psG[0:1, 0:1], onc[:], cntp[:],
                                           start=True, stop=True))
        S.add("vector", lambda v: v.tensor_scalar(srow[:, 10:11], psG[0:1, 0:1],
                                                  -1.0, 1024.0, ALU.mult, ALU.add))
        S.add("tensor", lambda t: t.matmul(psF[:, 0:1], onr[:, :], srow[:, 10:11],
                                           start=True, stop=True))
        S.add("vector", lambda v: v.tensor_copy(needc[:], psF[:, 0:1]))

        def cumsum_steps(src, dst):
            S.add("tensor", lambda t: t.matmul(psG[:, 0:64], trs[:], src[:],
                                               start=True, stop=True))
            S.add("tensor", lambda t: t.matmul(psF[0:1, 0:64], onc[:], src[:],
                                               start=True, stop=True))
            S.add("vector", lambda v: v.tensor_copy(crow1[:], psF[0:1, 0:64]))
            for k in [1, 2, 4, 8, 16, 32]:
                S.add("vector", lambda v, k=k: v.tensor_copy(crow2[:, 0:k], crow1[:, 0:k]))
                S.add("vector", lambda v, k=k: v.tensor_tensor(
                    crow2[:, k:64], crow1[:, k:64], crow1[:, 0:64 - k], ALU.add))
                S.add("vector", lambda v: v.tensor_copy(crow1[:], crow2[:]))
            S.add("vector", lambda v: v.tensor_tensor(crow2[:], crow1[:],
                                                      psF[0:1, 0:64], ALU.subtract))
            S.add("tensor", lambda t: t.matmul(psF[:, 0:64], onr[:, :], crow2[:],
                                               start=True, stop=True))
            S.add("vector", lambda v: v.tensor_copy(exbc[:], psF[:, 0:64]))
            S.add("vector", lambda v: v.tensor_tensor(dst[:], psG[:, 0:64], exbc[:],
                                                      ALU.add))

        cumsum_steps(meq, Ceq)
        S.add("vector", lambda v: v.tensor_scalar(Ceq[:], Ceq[:], needc[:], None,
                                                  ALU.is_le))
        S.add("vector", lambda v: v.tensor_tensor(Ceq[:], Ceq[:], meq[:], ALU.mult))
        S.add("vector", lambda v: v.tensor_tensor(selw[:], mgt[:], Ceq[:], ALU.add))
        cumsum_steps(selw, Csel)
        S.add("vector", lambda v: v.tensor_scalar(afb[:], Csel[:], -1.0, None, ALU.add))
        S.add("vector", lambda v: v.tensor_tensor(afb[:], afb[:], selw[:], ALU.mult))
        S.add("vector", lambda v: v.tensor_scalar(Csel[:], selw[:], -1e6, 1e6,
                                                  ALU.mult, ALU.add))
        S.add("vector", lambda v: v.tensor_tensor(afb[:], afb[:], Csel[:], ALU.add))
        S.add("vector", lambda v: v.tensor_copy(acu[:], afb[:]))
        S.add("scalar_dma", lambda sc: sc.dma_start(out=acolo, in_=acu[:]))

        for f in range(64):
            pf = Pf_[f % 2]
            stf = stf_[f % 2]
            S.add("vector", lambda v, f=f, stf=stf: v.tensor_scalar(
                stf[:, 0:1], tcs[:], float(f * 128), None, ALU.add))
            S.add("vector", lambda v, f=f, stf=stf: v.tensor_copy(
                stf[:, 1:2], scsb[:, f:f + 1]))
            S.add("vector", lambda v, f=f, pf=pf: v.tensor_scalar(
                pf[:], iot[:], afb[:, f:f + 1], 0.0, ALU.subtract, ALU.is_equal))
            S.add("tensor", lambda t, f=f, pf=pf, stf=stf: t.matmul(
                psH1[0:2, 0:512], stf[:], pf[:, 0:512],
                start=(f == 0), stop=(f == 63)))
            S.add("tensor", lambda t, f=f, pf=pf, stf=stf: t.matmul(
                psH2[0:2, 0:512], stf[:], pf[:, 512:1024],
                start=(f == 0), stop=(f == 63)))
        S.add("vector", lambda v: v.tensor_copy(poscosb[:, 0:512], psH1[0:2, 0:512]))
        S.add("vector", lambda v: v.tensor_copy(poscosb[:, 512:1024], psH2[0:2, 0:512]))
        S.add("scalar_dma", lambda sc: sc.dma_start(out=posco, in_=poscosb[:]))
        S.add("scalar_dma", lambda sc: sc.dma_start(
            out=posd.rearrange("a b -> b a"), in_=poscosb[0:1, :]))

        S.add("vector", lambda v: v.tensor_copy(sbf1[:], sbs[:]))
        S.add("tensor", lambda t: t.matmul(psF[:, 0:1], onr[:, :], sbf1[:],
                                           start=True, stop=True))
        S.add("vector", lambda v: v.tensor_copy(slotf[:], psF[:, 0:1]))
        S.add("vector", lambda v: v.tensor_tensor(slotf[:], slotf[:], tcs[:], ALU.add))
        S.add("vector", lambda v: v.tensor_copy(slotu[:], slotf[:]))
        S.add("gpsimd_dma", lambda g: g.indirect_dma_start(
            posg[:], None, posd, bass.IndirectOffsetOnAxis(ap=slotu[:], axis=0)))
        S.add("vector", lambda v: v.tensor_copy(myidx[:], posg[:]))
        S.add("gpsimd_dma", lambda g: g.indirect_dma_start(
            selrow[:], None, lnfull, bass.IndirectOffsetOnAxis(ap=myidx[:], axis=0)))
        S.add("tensor", lambda t: t.transpose(psF[0:1, 0:128], posg[:], idf[:]))
        S.add("vector", lambda v: v.tensor_copy(posrow[:], psF[0:1, 0:128]))
        S.add("scalar_dma", lambda sc: sc.dma_start(out=selo, in_=selrow[:]))

        S.add("vector", lambda v: v.tensor_tensor(sq[:], selrow[:], selrow[:], ALU.mult))
        S.add("vector", lambda v: v.tensor_reduce(rms[:, 0:1], sq[:], AX.X, ALU.add))
        S.add("vector", lambda v: v.tensor_scalar(rms[:, 1:2], rms[:, 0:1],
                                                  1.0 / 2048.0, 1e-6, ALU.mult, ALU.add))
        S.add("vector", lambda v: v.reciprocal(rms[:, 2:3], rms[:, 1:2]))
        S.add("scalar", lambda sc: sc.activation(rms[:, 3:4], rms[:, 2:3], AF.Sqrt))
        S.add("vector", lambda v: v.tensor_scalar(hrow[:], selrow[:], rms[:, 3:4],
                                                  None, ALU.mult))
        for g_ in range(2):
            bank = ptr
            def trg(t, g_=g_):
                for k in range(8):
                    dt = g_ * 8 + k
                    mm = t.transpose(ptr[:, k * 128:(k + 1) * 128],
                                     hrow[:, dt * 128:(dt + 1) * 128], idb[:])
                return mm
            S.add("tensor", trg)
            for k in range(8):
                dt = g_ * 8 + k
                S.add("vector", lambda v, dt=dt, k=k: v.tensor_scalar(
                    hT[:, dt * 128:(dt + 1) * 128], ptr[:, k * 128:(k + 1) * 128],
                    lnc1[:, dt:dt + 1], None, ALU.mult))

        for io in range(32):
            bank = psH1 if io % 2 == 0 else psH2
            def qmm(t, io=io, bank=bank):
                for dm in range(16):
                    mm = t.matmul(bank[:, 0:128],
                                  fws[:, (dm * 32 + io) * 128:(dm * 32 + io + 1) * 128],
                                  hT[:, dm * 128:(dm + 1) * 128],
                                  start=(dm == 0), stop=(dm == 15))
                return mm
            S.add("tensor", qmm)
            S.add("vector", lambda v, io=io, bank=bank: v.tensor_scalar(
                qko[:, io * 128:(io + 1) * 128], bank[:, 0:128],
                bcs[:, io:io + 1], None, ALU.add))

        S.add("tensor", lambda t: t.matmul(psR[:, 0:128], invr[:, :], posrow[:],
                                           start=True, stop=True))
        S.add("vector", lambda v: v.tensor_copy(angs[:], psR[:, 0:128]))
        for nm, off in [("sin", 0.0), ("cos", 0.25)]:
            dstv = sinv if nm == "sin" else cosv
            S.add("vector", lambda v, off=off: v.tensor_scalar(
                u1[:], angs[:], 1.0 / TWOPI, off, ALU.mult, ALU.add))
            S.add("vector", lambda v: v.tensor_copy(ri[:], u1[:]))
            S.add("vector", lambda v: v.tensor_copy(rf[:], ri[:]))
            S.add("vector", lambda v: v.tensor_tensor(sini[:], u1[:], rf[:],
                                                      ALU.subtract))
            S.add("vector", lambda v: v.tensor_scalar(rf[:], sini[:], 0.5, None,
                                                      ALU.is_gt))
            S.add("vector", lambda v: v.tensor_tensor(sini[:], sini[:], rf[:],
                                                      ALU.subtract))
            S.add("vector", lambda v: v.tensor_scalar(rf[:], sini[:], -0.5, None,
                                                      ALU.is_lt))
            S.add("vector", lambda v: v.tensor_tensor(sini[:], sini[:], rf[:],
                                                      ALU.add))
            S.add("vector", lambda v: v.tensor_scalar(sini[:], sini[:], TWOPI,
                                                      None, ALU.mult))
            S.add("scalar", lambda sc, dstv=dstv: sc.activation(dstv[:], sini[:],
                                                                AF.Sin))
        for ti in range(24):
            S.add("tensor", lambda t, ti=ti: t.matmul(
                psR[:, 0:128], rob[:], qko[:, ti * 128:(ti + 1) * 128],
                start=True, stop=True))
            S.add("vector", lambda v, ti=ti: v.tensor_tensor(
                rt1[:], qko[:, ti * 128:(ti + 1) * 128], cosv[:], ALU.mult))
            S.add("vector", lambda v: v.tensor_tensor(rt2[:], psR[:, 0:128], sinv[:],
                                                      ALU.mult))
            S.add("vector", lambda v, ti=ti: v.tensor_tensor(
                qko[:, ti * 128:(ti + 1) * 128], rt1[:], rt2[:], ALU.add))

        def vtr(t):
            for k in range(8):
                mm = t.transpose(ptr[:, k * 128:(k + 1) * 128],
                                 qko[:, (24 + k) * 128:(25 + k) * 128], idb[:])
            return mm
        S.add("tensor", vtr)
        S.add("vector", lambda v: v.tensor_copy(vstok[:], ptr[:]))

        S.add("scalar_dma", lambda sc: sc.dma_start(
            out=qTs.rearrange("(dt p) s -> p dt s", p=128),
            in_=qko[:, 0:2048].rearrange("p (dt s) -> p dt s", s=128)))
        S.add("scalar_dma", lambda sc: sc.dma_start(
            out=kkTs.rearrange("(dt p) s -> p dt s", p=128),
            in_=qko[:, 2048:3072].rearrange("p (dt s) -> p dt s", s=128)))
        S.add("scalar_dma", lambda sc: sc.dma_start(out=vso, in_=vstok[:]))

        @blk.sync
        def _(s_):
            S.emit_one(s_, SEMS, ["sync_dma"])

        @blk.gpsimd
        def _(g):
            S.emit_one(g, SEMS, ["gpsimd", "gpsimd_dma"])

        @blk.vector
        def _(v):
            S.emit_one(v, SEMS, ["vector"])

        @blk.tensor
        def _(t):
            S.emit_one(t, SEMS, ["tensor"])

        @blk.scalar
        def _(sc):
            S.emit_one(sc, SEMS, ["scalar", "scalar_dma"])
            sc.wait_ge(SEMS["sa2"], S.cnt["sa2"])
            sc.wait_ge(SEMS["sg"], S.cnt["sg"])
            sc.wait_ge(SEMS["sv"], S.cnt["sv"])

    return nc

def build_fmain():
    nc = bass.Bass("TRN2", target_bir_lowering=False)
    qTs = nc.dram_tensor("qTs", [16 * 128, 128], BF16, kind="ExternalInput").ap()
    kkT = nc.dram_tensor("kkT", [8 * 128, SSEL], BF16, kind="ExternalInput").ap()
    vfull = nc.dram_tensor("vfull", [SSEL, 1024], BF16, kind="ExternalInput").ap()
    seltok = nc.dram_tensor("seltok", [128, D], BF16, kind="ExternalInput").ap()
    myscol = nc.dram_tensor("myscol", [128, 1], F32, kind="ExternalInput").ap()
    amask = nc.dram_tensor("amask", [128, SSEL], F32, kind="ExternalInput").ap()
    ln2bc = nc.dram_tensor("ln2bc", [128, D], F32, kind="ExternalInput").ap()
    identb = nc.dram_tensor("identb", [128, 128], BF16, kind="ExternalInput").ap()
    gato = nc.dram_tensor("gato", [128, D], BF16, kind="ExternalOutput").ap()

    fwob = nc.dram_tensor("fwob", [D, D], BF16, kind="ExternalInput").ap()
    fwgb = nc.dram_tensor("fwgb", [D, IMLP], BF16, kind="ExternalInput").ap()
    fwub = nc.dram_tensor("fwub", [D, IMLP], BF16, kind="ExternalInput").ap()
    fwdb = nc.dram_tensor("fwdb", [IMLP, D], BF16, kind="ExternalInput").ap()

    es = ExitStack()
    sb = lambda n, shp, dt: es.enter_context(nc.sbuf_tensor(n, shp, dt))
    ps = lambda n, shp, dt=F32: es.enter_context(nc.psum_tensor(n, shp, dt))
    sem = lambda n: es.enter_context(nc.semaphore(n))
    with es:
        qs = sb("qs", [128, 2048], BF16)
        ks = sb("ks", [128, 8192], BF16)
        vss = sb("vss", [128, 8192], BF16)
        Pb = sb("Pb", [128, 1024], BF16)
        Pe = sb("Pe", [128, 1024], F32)
        PT = sb("PT", [128, 1024], BF16)
        ctxT = sb("ctxT", [128, 2048], BF16)
        sel = sb("sel", [128, D], BF16)
        self32 = sb("self32", [128, D], F32)
        x1tok = sb("x1tok", [128, D], F32)
        sqtok = sb("sqtok", [128, D], F32)
        h2k = sb("h2k", [128, D], BF16)
        h2T = sb("h2T", [128, D], BF16)
        gatok = sb("gatok", [128, D], BF16)
        msk = sb("msk", [128, SSEL], F32)
        lnbc = sb("lnbc", [128, D], F32)
        wsl = [sb(f"wsl{i}", [128, 2048], BF16) for i in range(2)]
        wgl = [sb(f"wgl{i}", [128, 2048], BF16) for i in range(2)]
        wul = [sb(f"wul{i}", [128, 2048], BF16) for i in range(2)]
        wdl = [sb(f"wdl{i}", [128, 2048], BF16) for i in range(2)]
        g1 = sb("g1", [128, 128], F32)
        mt = sb("mt", [128, 128], BF16)
        sm2 = sb("sm2", [128, 16], F32)
        rstdc = sb("rstdc", [128, 1], F32)
        gcol = sb("gcol", [128, 1], F32)
        msc = sb("msc", [128, 1], F32)
        idb = sb("idb", [128, 128], BF16)
        psA = ps("psA", [128, 512])
        psB = ps("psB", [128, 512])
        psC = ps("psC", [128, 512])
        psD = ps("psD", [128, 512])
        psE = ps("psE", [128, 512])
        psFb = ps("psFb", [128, 512])
        ptr = ps("ptr", [128, 1024], BF16)
        BANKS = [psC, psD, psE, psFb]
        SEMS = {n: sem(n) for n in ["sv", "sp", "sa", "sz", "sl", "sa2", "sg"]}
        swg = [sem("swg0"), sem("swg1")]
        swu = [sem("swu0"), sem("swu1")]
        swd = [sem("swd0"), sem("swd1")]
        smp = sem("smp")
        smv = sem("smv")
        sma = sem("sma")
        sst = sem("sst")
        blk = es.enter_context(nc.Block())

        S = _Sched()

        jobs = [(idb, identb), (msk, amask), (sel, seltok), (msc, myscol),
                (lnbc, ln2bc)]
        for dst, src in jobs:
            S.add("sync_dma", lambda s_, dst=dst, src=src: s_.dma_start(out=dst[:], in_=src))
        S.add("sync_dma", lambda s_: s_.dma_start(
            out=qs.rearrange("p (dt s) -> p dt s", s=128),
            in_=qTs.rearrange("(dt p) s -> p dt s", p=128)))
        S.add("sync_dma", lambda s_: s_.dma_start(
            out=ks.rearrange("p (dt s) -> p dt s", s=SSEL),
            in_=kkT.rearrange("(dt p) s -> p dt s", p=128)))
        S.add("sync_dma", lambda s_: s_.dma_start(
            out=vss.rearrange("p (st dv) -> p st dv", dv=1024),
            in_=vfull.rearrange("(st p) dv -> p st dv", p=128)))

        for h in range(16):
            kv = h // 2
            S.add("tensor", lambda t, h=h, kv=kv: t.matmul(
                psA[:, 0:512], qs[:, h * 128:(h + 1) * 128],
                ks[:, kv * 1024:kv * 1024 + 512], start=True, stop=True))
            S.add("tensor", lambda t, h=h, kv=kv: t.matmul(
                psB[:, 0:512], qs[:, h * 128:(h + 1) * 128],
                ks[:, kv * 1024 + 512:(kv + 1) * 1024], start=True, stop=True))
            S.add("vector", lambda v: v.tensor_tensor(psA[:, 0:512], psA[:, 0:512],
                                                      msk[:, 0:512], ALU.add))
            S.add("vector", lambda v: v.tensor_tensor(psB[:, 0:512], psB[:, 0:512],
                                                      msk[:, 512:1024], ALU.add))
            S.add("vector", lambda v: v.reduce_max(sm2[:, 0:1], psA[:, 0:512], AX.X))
            S.add("vector", lambda v: v.reduce_max(sm2[:, 1:2], psB[:, 0:512], AX.X))
            S.add("vector", lambda v: v.tensor_tensor(sm2[:, 2:3], sm2[:, 0:1],
                                                      sm2[:, 1:2], ALU.max))
            S.add("vector", lambda v: v.tensor_scalar(sm2[:, 3:4], sm2[:, 2:3],
                                                      -ISQ, None, ALU.mult))
            S.add("scalar", lambda sc: sc.activation(Pe[:, 0:512], psA[:, 0:512],
                                                     AF.Exp, bias=sm2[:, 3:4], scale=ISQ))
            S.add("scalar", lambda sc: sc.activation(Pe[:, 512:1024], psB[:, 0:512],
                                                     AF.Exp, bias=sm2[:, 3:4], scale=ISQ))
            S.add("vector", lambda v: v.tensor_reduce(sm2[:, 4:5], Pe[:], AX.X, ALU.add))
            S.add("vector", lambda v: v.reciprocal(sm2[:, 5:6], sm2[:, 4:5]))
            S.add("vector", lambda v: v.tensor_scalar(Pb[:], Pe[:], sm2[:, 5:6],
                                                      None, ALU.mult))
            def trg(t):
                for k in range(8):
                    mm = t.transpose(ptr[:, k * 128:(k + 1) * 128],
                                     Pb[:, k * 128:(k + 1) * 128], idb[:])
                return mm
            S.add("tensor", trg)
            S.add("vector", lambda v: v.tensor_copy(PT[:], ptr[:]))
            def avg(t, kv=kv):
                for st_ in range(8):
                    mm = t.matmul(psC[:, 0:128],
                                  vss[:, st_ * 1024 + kv * 128:st_ * 1024 + (kv + 1) * 128],
                                  PT[:, st_ * 128:(st_ + 1) * 128],
                                  start=(st_ == 0), stop=(st_ == 7))
                return mm
            S.add("tensor", avg)
            S.add("vector", lambda v, h=h: v.tensor_copy(ctxT[:, h * 128:(h + 1) * 128],
                                                         psC[:, 0:128]))

        S.add("vector", lambda v: v.tensor_copy(self32[:], sel[:]))
        for dc in range(16):
            S.add("sync_dma", lambda s_, dc=dc: s_.dma_start(
                out=wsl[dc % 2][:], in_=fwob[ts(dc, 128), :]), lag={"tensor": 1})
            def womm(t, dc=dc):
                for ch in range(4):
                    mm = t.matmul(BANKS[ch][:, 0:512],
                                  ctxT[:, dc * 128:(dc + 1) * 128],
                                  wsl[dc % 2][:, ch * 512:(ch + 1) * 512],
                                  start=(dc == 0), stop=(dc == 15))
                return mm
            S.add("tensor", womm)
        for ch in range(4):
            S.add("vector", lambda v, ch=ch: v.scalar_tensor_tensor(
                out=x1tok[:, ch * 512:(ch + 1) * 512],
                in0=BANKS[ch][:, 0:512], scalar=1.0,
                in1=self32[:, ch * 512:(ch + 1) * 512], op0=ALU.mult, op1=ALU.add))

        S.add("vector", lambda v: v.tensor_tensor(sqtok[:], x1tok[:], x1tok[:], ALU.mult))
        S.add("vector", lambda v: v.tensor_reduce(sm2[:, 6:7], sqtok[:], AX.X, ALU.add))
        S.add("vector", lambda v: v.tensor_scalar(sm2[:, 7:8], sm2[:, 6:7],
                                                  1.0 / 2048.0, 1e-6, ALU.mult, ALU.add))
        S.add("vector", lambda v: v.reciprocal(sm2[:, 8:9], sm2[:, 7:8]))
        S.add("scalar", lambda sc: sc.activation(rstdc[:], sm2[:, 8:9], AF.Sqrt))
        S.add("vector", lambda v: v.tensor_scalar(sqtok[:], x1tok[:], rstdc[:, 0:1],
                                                  None, ALU.mult))
        S.add("vector", lambda v: v.tensor_tensor(h2k[:], sqtok[:], lnbc[:], ALU.mult))
        for g_ in range(2):
            def trg2(t, g_=g_):
                for k in range(8):
                    dt = g_ * 8 + k
                    mm = t.transpose(ptr[:, k * 128:(k + 1) * 128],
                                     h2k[:, dt * 128:(dt + 1) * 128], idb[:])
                return mm
            S.add("tensor", trg2)
            S.add("vector", lambda v, g_=g_: v.tensor_copy(
                h2T[:, g_ * 1024:(g_ + 1) * 1024], ptr[:]))
        S.add("scalar", lambda sc: sc.activation(gcol[:], msc[:], AF.Sigmoid))

        TOT = dict(S.cnt)

        def ld_gu(e, buf, wsrc, I, sems_):
            return e.dma_start(
                out=buf[I % 2].rearrange("p (dm q) -> p dm q", q=128),
                in_=wsrc.rearrange("(dm p) q -> p dm q", p=128)
                [:, :, I * 128:(I + 1) * 128]).then_inc(sems_[I % 2], 16)

        @blk.sync
        def _(s_):
            S.emit_one(s_, SEMS, ["sync_dma"])
            for I in range(64):
                if I >= 2:
                    s_.wait_ge(smp, 3 * (I - 1))
                ld_gu(s_, wgl, fwgb, I, swg)

        @blk.gpsimd
        def _(g):
            S.emit_one(g, SEMS, ["gpsimd", "gpsimd_dma"])
            for I in range(64):
                if I >= 2:
                    g.wait_ge(smp, 3 * (I - 1))
                g.dma_start(out=wdl[I % 2][:], in_=fwdb[ts(I, 128), :]
                            ).then_inc(swd[I % 2], 16)

        @blk.tensor
        def _(t):
            S.emit_one(t, SEMS, ["tensor"])
            for k in ["sv", "sa"]:
                t.wait_ge(SEMS[k], TOT[k])
            for I in range(64):
                for sw in (swg, swu, swd):
                    t.wait_ge(sw[I % 2], 16 * (I // 2 + 1))
                if I >= 1:
                    t.wait_ge(smv, 2 * I)
                for dm in range(16):
                    mm = t.matmul(psA[:, 0:128], wgl[I % 2][:, dm * 128:(dm + 1) * 128],
                                  h2T[:, dm * 128:(dm + 1) * 128],
                                  start=(dm == 0), stop=(dm == 15))
                mm.then_inc(smp, 1)
                for dm in range(16):
                    mm = t.matmul(psB[:, 0:128], wul[I % 2][:, dm * 128:(dm + 1) * 128],
                                  h2T[:, dm * 128:(dm + 1) * 128],
                                  start=(dm == 0), stop=(dm == 15))
                mm.then_inc(smp, 1)
                t.wait_ge(smv, 2 * I + 2)
                for ch in range(4):
                    mm = t.matmul(BANKS[ch][:, 0:512], mt[:],
                                  wdl[I % 2][:, ch * 512:(ch + 1) * 512],
                                  start=(I == 0), stop=(I == 63))
                mm.then_inc(smp, 1)

        @blk.vector
        def _(v):
            S.emit_one(v, SEMS, ["vector"])
            for k in ["sa", "sp"]:
                v.wait_ge(SEMS[k], TOT[k])
            for I in range(64):
                v.wait_ge(sma, I + 1)
                v.wait_ge(smp, 3 * I + 2)
                v.tensor_tensor(mt[:], psB[:, 0:128], g1[:], ALU.mult).then_inc(smv, 2)
            v.wait_ge(smp, 192)
            for ch in range(4):
                v.scalar_tensor_tensor(
                    out=sqtok[:, ch * 512:(ch + 1) * 512],
                    in0=BANKS[ch][:, 0:512], scalar=1.0,
                    in1=x1tok[:, ch * 512:(ch + 1) * 512], op0=ALU.mult, op1=ALU.add
                    ).then_inc(smv, 1)
            v.wait_ge(smv, 132)
            v.tensor_tensor(sqtok[:], sqtok[:], self32[:], ALU.subtract).then_inc(smv, 1)
            v.wait_ge(smv, 133)
            v.tensor_scalar(sqtok[:], sqtok[:], gcol[:, 0:1], None, ALU.mult
                            ).then_inc(smv, 1)
            v.wait_ge(smv, 134)
            v.tensor_tensor(gatok[:], sqtok[:], self32[:], ALU.add).then_inc(smv, 1)

        @blk.scalar
        def _(sc):
            S.emit_one(sc, SEMS, ["scalar", "scalar_dma"])
            sc.wait_ge(SEMS["sp"], TOT["sp"])
            ld_gu(sc, wul, fwub, 0, swu)
            ld_gu(sc, wul, fwub, 1, swu)
            for I in range(64):
                sc.wait_ge(smp, 3 * I + 1)
                sc.activation(g1[:], psA[:, 0:128], AF.Silu).then_inc(sma, 1)
                if I + 2 < 64:
                    sc.wait_ge(smp, 3 * I + 2)
                    ld_gu(sc, wul, fwub, I + 2, swu)
            sc.wait_ge(smv, 135)
            sc.dma_start(out=gato, in_=gatok[:]).then_inc(sst, 16)
            sc.wait_ge(sst, 16)
            sc.wait_ge(SEMS["sg"], S.cnt["sg"])
            sc.wait_ge(SEMS["sa2"], S.cnt["sa2"])

    return nc


def build_cast():
    nc = bass.Bass("TRN2", target_bir_lowering=False)
    cin = nc.dram_tensor("cin", [1088, 8192], F32, kind="ExternalInput").ap()
    cout = nc.dram_tensor("cout", [1088, 8192], BF16, kind="ExternalOutput").ap()
    es = ExitStack()
    with es:
        scp = es.enter_context(nc.semaphore("scp"))
        blk = es.enter_context(nc.Block())

        @blk.gpsimd
        def _(g):
            g.dma_start(out=cout, in_=cin).then_inc(scp, 16)
            g.wait_ge(scp, 16)
    return nc


# version 05
# speedup vs baseline: 1.7852x; 1.0245x over previous
import numpy as np
import ml_dtypes
from contextlib import ExitStack
import concourse.bass as bass
import concourse.mybir as mybir
from concourse.bass_utils import run_bass_kernel_spmd

F32 = mybir.dt.float32
BF16 = mybir.dt.bfloat16
U32 = mybir.dt.uint32
I32 = mybir.dt.int32
AF = mybir.ActivationFunctionType
ALU = mybir.AluOpType
AX = mybir.AxisListType
BF = ml_dtypes.bfloat16
ts = bass.ts
ds = bass.ds

T = 8192
D = 2048
HD = 128
SSEL = 1024
IMLP = 8192
NEG = -1.0e30
RTH = 10000.0
ISQ = float(1.0 / np.sqrt(np.float32(128.0)))
NCORE = 8
TWOPI = float(2.0 * np.pi)

_CACHE = {}


def _cached(key, fn):
    if key not in _CACHE:
        _CACHE[key] = fn()
    return _CACHE[key]


def build_ln(NT=1024):
    nc = bass.Bass("TRN2", target_bir_lowering=False)
    nt = NT // 128
    xs = nc.dram_tensor("xs", [NT, D], F32, kind="ExternalInput").ap()
    wr = nc.dram_tensor("wr", [1, D], F32, kind="ExternalInput").ap()
    onesf = nc.dram_tensor("onesf", [1, 128], F32, kind="ExternalInput").ap()
    identb = nc.dram_tensor("identb", [128, 128], BF16, kind="ExternalInput").ap()
    lntok = nc.dram_tensor("lntok", [NT, D], BF16, kind="ExternalOutput").ap()
    lnT = nc.dram_tensor("lnT", [16 * 128, NT], BF16, kind="ExternalOutput").ap()
    sco = nc.dram_tensor("sco", [128, nt], F32, kind="ExternalOutput").ap()
    es = ExitStack()
    sb = lambda n, shp, dt: es.enter_context(nc.sbuf_tensor(n, shp, dt))
    ps = lambda n, shp, dt=F32: es.enter_context(nc.psum_tensor(n, shp, dt))
    sem = lambda n: es.enter_context(nc.semaphore(n))
    with es:
        xt = sb("xt", [128, D], F32)
        xt2 = sb("xt2", [128, D], F32)
        tok = sb("tok", [128, D], BF16)
        tok2 = sb("tok2", [128, D], BF16)
        lnb = sb("lnb", [128, 2048], BF16)
        lnb2 = sb("lnb2", [128, 2048], BF16)
        sm = sb("sm", [128, 64], F32)
        wrbc = sb("wrbc", [128, D], F32)
        wrsum = sb("wrsum", [128, 1], F32)
        wrrow = sb("wrrow", [1, D], F32)
        onesfs = sb("onesfs", [1, 128], F32)
        idents = sb("idents", [128, 128], BF16)
        sct = sb("sct", [128, D], F32)
        psa = ps("psa", [128, 512])
        ptr = ps("ptr", [128, 1024], BF16)
        ptr2 = ps("ptr2", [128, 1024], BF16)
        slc = sem("slc"); sl0 = sem("sl0"); sl1 = sem("sl1")
        sv = sem("sv"); sa = sem("sa"); sp = sem("sp"); se = sem("se")
        st0 = sem("st0"); st1 = sem("st1"); stf = sem("stf")
        blk = es.enter_context(nc.Block())
        slp = [sl0, sl1]
        stp = [st0, st1]
        xb = [xt, xt2]
        tb = [tok, tok2]
        lb = [lnb, lnb2]
        NV = 9

        @blk.sync
        def _(s_):
            s_.dma_start(out=wrrow[:], in_=wr).then_inc(slc, 16)
            s_.dma_start(out=onesfs[:], in_=onesf).then_inc(slc, 16)
            s_.dma_start(out=idents[:], in_=identb).then_inc(slc, 16)
            for i in range(nt):
                if i >= 2:
                    s_.wait_ge(stp[i % 2], 32 * (i // 2))
                s_.dma_start(out=xb[i % 2][:], in_=xs[ts(i, 128), :]).then_inc(slp[i % 2], 16)

        @blk.tensor
        def _(t):
            t.wait_ge(slc, 48)
            for j in range(4):
                if j:
                    t.wait_ge(sv, j)
                t.matmul(psa[:, 0:512], onesfs[:, :], wrrow[:, j * 512:(j + 1) * 512],
                         start=True, stop=True).then_inc(sp, 1)
            for i in range(nt):
                t.wait_ge(sv, 6 + i * NV + 5)
                for g in range(2):
                    if i * 2 + g >= 2:
                        t.wait_ge(se, i * 2 + g - 1)
                    bank = ptr if g == 0 else ptr2
                    for k in range(8):
                        dt = g * 8 + k
                        mm = t.transpose(bank[:, k * 128:(k + 1) * 128],
                                         tb[i % 2][:, dt * 128:(dt + 1) * 128], idents[:])
                    mm.then_inc(sp, 1)

        @blk.vector
        def _(v):
            for j in range(4):
                v.wait_ge(sp, j + 1)
                v.tensor_copy(wrbc[:, j * 512:(j + 1) * 512], psa[:, 0:512]).then_inc(sv, 1)
            v.wait_ge(sv, 4)
            v.tensor_reduce(wrsum[:], wrbc[:], AX.X, ALU.add).then_inc(sv, 2)
            for i in range(nt):
                base = 6 + i * NV
                x = xb[i % 2]
                v.wait_ge(slp[i % 2], 16 * (i // 2 + 1))
                for q in range(4):
                    mm = v.bn_stats(sm[:, 6 * q:6 * q + 6], x[:, 512 * q:512 * (q + 1)])
                mm.then_inc(sv, 1)
                v.wait_ge(sv, base + 1)
                v.bn_aggr(sm[:, 24:26], sm[:, 0:24]).then_inc(sv, 1)
                v.wait_ge(sv, base + 2)
                v.tensor_scalar(sm[:, 26:27], sm[:, 25:26], 1e-5, None, ALU.add
                                ).then_inc(sv, 1)
                v.wait_ge(sv, base + 3)
                v.reciprocal(sm[:, 27:28], sm[:, 26:27]).then_inc(sv, 1)
                v.wait_ge(sa, i + 1)
                if i >= 2:
                    v.wait_ge(stp[i % 2], 32 * (i // 2))
                v.tensor_scalar(tb[i % 2][:], x[:], sm[:, 24:25], sm[:, 28:29],
                                ALU.subtract, ALU.mult)
                v.tensor_tensor(sct[:], x[:], wrbc[:], ALU.mult).then_inc(sv, 1)
                v.wait_ge(sv, base + 5)
                v.tensor_reduce(sm[:, 30:31], sct[:], AX.X, ALU.add).then_inc(sv, 1)
                v.wait_ge(sv, base + 6)
                v.tensor_tensor(sm[:, 32:33], sm[:, 24:25], wrsum[:], ALU.mult
                                ).then_inc(sv, 1)
                v.wait_ge(sv, base + 7)
                v.tensor_tensor(sm[:, 33:34], sm[:, 30:31], sm[:, 32:33], ALU.subtract
                                ).then_inc(sv, 1)
                v.wait_ge(sv, base + 8)
                v.tensor_tensor(sm[:, 40 + i:41 + i], sm[:, 33:34],
                                sm[:, 28:29], ALU.mult).then_inc(sv, 1)
                for g in range(2):
                    v.wait_ge(sp, 4 + i * 2 + g + 1)
                    bank = ptr if g == 0 else ptr2
                    v.tensor_copy(
                        lb[i % 2].rearrange("p (dt q) -> p dt q", q=128)[
                            :, g * 8:(g + 1) * 8, :],
                        bank[:].rearrange("p (k q) -> p k q", q=128)).then_inc(se, 1)

        @blk.scalar
        def _(sc):
            for i in range(nt):
                base = 6 + i * NV
                sc.wait_ge(sv, base + 4)
                sc.activation(sm[:, 28:29], sm[:, 27:28], AF.Sqrt).then_inc(sa, 1)
                sc.wait_ge(sv, base + 5)
                sc.dma_start(out=lntok[ts(i, 128), :], in_=tb[i % 2][:]
                             ).then_inc(stp[i % 2], 16)
                sc.wait_ge(se, (i + 1) * 2)
                sc.dma_start(out=lnT.rearrange("(dt p) n -> p dt n", p=128)[:, :, ts(i, 128)],
                             in_=lb[i % 2].rearrange("p (dt q) -> p dt q", q=128)
                             ).then_inc(stp[i % 2], 16)
            sc.wait_ge(sv, 6 + nt * NV)
            sc.dma_start(out=sco[:, :], in_=sm[:, 40:40 + nt]).then_inc(stf, 16)
            sc.wait_ge(st0, 32 * ((nt + 1) // 2))
            sc.wait_ge(st1, 32 * (nt // 2))
            sc.wait_ge(stf, 16)

    return nc


def build_att(TT, nq, nkv, hbias):
    nc = bass.Bass("TRN2", target_bir_lowering=False)
    nch = TT // 512
    ntt = TT // 128
    kvmap = [h * nkv // nq for h in range(nq)]
    nio = nq + 2 * nkv

    lnT = nc.dram_tensor("lnT", [16, 128, TT], BF16, kind="ExternalInput").ap()
    wqkv = nc.dram_tensor("wqkv", [D, nio * 128], F32, kind="ExternalInput").ap()
    if hbias:
        bqc = nc.dram_tensor("bqc", [128, nq], F32, kind="ExternalInput").ap()
        bkc = nc.dram_tensor("bkc", [128, nkv], F32, kind="ExternalInput").ap()
        bvc = nc.dram_tensor("bvc", [128, nkv], F32, kind="ExternalInput").ap()
    identb = nc.dram_tensor("identb", [128, 128], BF16, kind="ExternalInput").ap()
    onesb = nc.dram_tensor("onesb", [128, 1], BF16, kind="ExternalInput").ap()
    onesf = nc.dram_tensor("onesf", [1, 128], F32, kind="ExternalInput").ap()
    dmask = nc.dram_tensor("dmask", [4, 128, 512], F32, kind="ExternalInput").ap()
    ctxo = nc.dram_tensor("ctxo", [nq * 128, TT], BF16, kind="ExternalOutput").ap()

    wqkvb = nc.dram_tensor("wqkvb", [D, nio * 128], BF16).ap()

    def nkj(qc):
        return 4 * (qc + 1)

    def cumS(h, qc):
        tot = 0
        for hh in range(nq):
            for qq in range(nch):
                if (hh, qq) < (h, qc):
                    tot += nkj(qq)
        return tot

    es = ExitStack()
    sb = lambda n, shp, dt: es.enter_context(nc.sbuf_tensor(n, shp, dt))
    ps = lambda n, shp, dt=F32: es.enter_context(nc.psum_tensor(n, shp, dt))
    sem = lambda n: es.enter_context(nc.semaphore(n))
    with es:
        lnc = sb("lnc", [128, 16 * 512], BF16)
        lnc2 = sb("lnc2", [128, 16 * 512], BF16)
        wqs = sb("wqs", [128, 16 * nio * 128], BF16)
        qT = sb("qT", [128, nq * TT], BF16)
        kT = sb("kT", [128, nkv * TT], BF16)
        vT = sb("vT", [128, nkv * TT], BF16)
        vtok = sb("vtok", [128, nkv * TT], BF16)
        prb0 = sb("prb0", [128, 512], BF16)
        prb1 = sb("prb1", [128, 512], BF16)
        rrow = sb("rrow", [1, 512], F32)
        rbc = sb("rbc", [128, 512], F32)
        ctxc = sb("ctxc", [128, nq * 512], BF16)
        idents = sb("idents", [128, 128], BF16)
        onec = sb("onec", [128, 1], BF16)
        onesfs = sb("onesfs", [1, 128], F32)
        masks = sb("masks", [128, 2048], F32)
        bqs = sb("bqs", [128, max(nq, 1)], F32)
        bks = sb("bks", [128, max(nkv, 1)], F32)
        bvs = sb("bvs", [128, max(nkv, 1)], F32)
        psA = ps("psA", [128, 512])
        psB = ps("psB", [128, 512])
        psCp = [ps("psC0", [128, 512]), ps("psC1", [128, 512])]
        psDp = [ps("psD0", [128, 512]), ps("psD1", [128, 512])]
        psE = ps("psE", [128, 512])
        ptr = ps("ptr", [128, 1024], BF16)
        sg = sem("sg"); slw = sem("slw"); sl0 = sem("sl0"); sl1 = sem("sl1")
        s_qp = sem("s_qp"); s_qv = sem("s_qv")
        s_tr = sem("s_tr"); s_te = sem("s_te"); s_s = sem("s_s"); s_d = sem("s_d")
        s_e = sem("s_e"); s_av = sem("s_av"); s_r = sem("s_r"); s_b = sem("s_b")
        s_v2 = sem("s_v2"); st0 = sem("st0"); st1 = sem("st1")
        blk = es.enter_context(nc.Block())
        slp = [sl0, sl1]
        stp = [st0, st1]
        lncb = [lnc, lnc2]
        prb = [prb0, prb1]
        NLD = 7 + (3 if hbias else 0) + 16

        @blk.gpsimd
        def _(g):
            g.dma_start(out=wqkvb, in_=wqkv).then_inc(sg, 16)

        @blk.sync
        def _(s_):
            s_.dma_start(out=idents[:], in_=identb).then_inc(slw, 16)
            s_.dma_start(out=onec[:], in_=onesb).then_inc(slw, 16)
            s_.dma_start(out=onesfs[:], in_=onesf).then_inc(slw, 16)
            for v_ in range(4):
                s_.dma_start(out=masks[:, v_ * 512:(v_ + 1) * 512], in_=dmask[v_, :, :]
                             ).then_inc(slw, 16)
            if hbias:
                s_.dma_start(out=bqs[:], in_=bqc).then_inc(slw, 16)
                s_.dma_start(out=bks[:], in_=bkc).then_inc(slw, 16)
                s_.dma_start(out=bvs[:], in_=bvc).then_inc(slw, 16)
            s_.wait_ge(sg, 16)
            for dm in range(16):
                s_.dma_start(out=wqs[:, dm * nio * 128:(dm + 1) * nio * 128],
                             in_=wqkvb[ts(dm, 128), :]).then_inc(slw, 16)
            for c in range(nch):
                if c >= 2:
                    s_.wait_ge(s_qp, (c - 1) * nio)
                s_.dma_start(out=lncb[c % 2].rearrange("p (dt q) -> p dt q", q=512),
                             in_=lnT[:, :, ts(c, 512)].rearrange("dt p q -> p dt q")
                             ).then_inc(slp[c % 2], 16)

        @blk.tensor
        def _(t):
            def warmup(nmm):
                for _ in range(nmm):
                    t.matmul(psE[:, 0:512], idents[:, :], qT[:, 0:512],
                             start=True, stop=True)
            t.wait_ge(slw, NLD * 16)
            for c in range(nch):
                t.wait_ge(slp[c % 2], 16 * (c // 2 + 1))
                for io in range(nio):
                    u = c * nio + io
                    if u >= 2:
                        t.wait_ge(s_qv, u - 1)
                    bank = psA if io % 2 == 0 else psB
                    for dm in range(16):
                        mm = t.matmul(bank[:, 0:512],
                                      wqs[:, (dm * nio + io) * 128:(dm * nio + io + 1) * 128],
                                      lncb[c % 2][:, dm * 512:(dm + 1) * 512],
                                      start=(dm == 0), stop=(dm == 15))
                        if dm == 15:
                            mm.then_inc(s_qp, 1)
            t.wait_ge(s_qv, nch * nio)
            ngr = (nkv * ntt) // 8
            for g in range(ngr):
                if g >= 1:
                    t.wait_ge(s_te, g)
                for k in range(8):
                    i = g * 8 + k
                    mm = t.transpose(ptr[:, k * 128:(k + 1) * 128],
                                     vT[:, i * 128:(i + 1) * 128], idents[:])
                mm.then_inc(s_tr, 1)
            t.wait_ge(s_te, ngr)
            warmup(14)

            def bcast(np_):
                if np_ >= 1:
                    t.wait_ge(s_v2, 2 * np_ - 1)
                t.wait_ge(s_r, np_ + 1)
                t.matmul(psE[:, 0:512], onesfs[:, :], rrow[:, 0:512],
                         start=True, stop=True).then_inc(s_b, 1)
            units = []
            for h in range(nq):
                for qc in range(nch):
                    for kj in range(nkj(qc)):
                        units.append((h, qc, kj, h * nch + qc, nkj(qc)))

            def emitS(j):
                h, qc, kj, npair, nk = units[j]
                kv = kvmap[h]
                bank = psA if kj % 2 == 0 else psB
                if j >= 2:
                    t.wait_ge(s_e, j - 1)
                t.matmul(bank[:, 0:512],
                         kT[:, kv * TT + kj * 128:kv * TT + (kj + 1) * 128],
                         qT[:, h * TT + qc * 512:h * TT + (qc + 1) * 512],
                         start=True, stop=True).then_inc(s_s, 1)

            emitS(0)
            for i, (h, qc, kj, npair, nk) in enumerate(units):
                if i + 1 < len(units):
                    emitS(i + 1)
                kv = kvmap[h]
                t.wait_ge(s_e, i + 1)
                if kj == 0 and npair >= 2:
                    t.wait_ge(s_v2, 2 * npair - 2)
                t.matmul(psCp[npair % 2][:, 0:512],
                         vtok[:, kv * TT + kj * 128:kv * TT + (kj + 1) * 128],
                         prb[kj % 2][:], start=(kj == 0), stop=(kj == nk - 1))
                if kj == 0 and npair >= 2:
                    t.wait_ge(s_r, npair - 1)
                t.matmul(psDp[npair % 2][0:1, 0:512], onec[:], prb[kj % 2][:],
                         start=(kj == 0), stop=(kj == nk - 1)).then_inc(s_av, 1)
                if kj == 2 and npair >= 1:
                    bcast(npair - 1)

            bcast(nq * nch - 1)

        @blk.vector
        def _(v):
            for c in range(nch):
                for io in range(nio):
                    u = c * nio + io
                    v.wait_ge(s_qp, u + 1)
                    bank = psA if io % 2 == 0 else psB
                    if io < nq:
                        dst = qT[:, io * TT + c * 512:io * TT + (c + 1) * 512]
                        bias = bqs[:, io:io + 1] if hbias else None
                    elif io < nq + nkv:
                        kv = io - nq
                        dst = kT[:, kv * TT + c * 512:kv * TT + (c + 1) * 512]
                        bias = bks[:, kv:kv + 1] if hbias else None
                    else:
                        kv = io - nq - nkv
                        dst = vT[:, kv * TT + c * 512:kv * TT + (c + 1) * 512]
                        bias = bvs[:, kv:kv + 1] if hbias else None
                    if bias is not None:
                        v.tensor_scalar(dst, bank[:, 0:512], bias, None, ALU.add
                                        ).then_inc(s_qv, 1)
                    else:
                        v.tensor_copy(dst, bank[:, 0:512]).then_inc(s_qv, 1)
            for g in range((nkv * ntt) // 8):
                v.wait_ge(s_tr, g + 1)
                v.tensor_copy(vtok[:, g * 1024:(g + 1) * 1024], ptr[:]).then_inc(s_te, 1)
            def epi(np_, cs_, nk_):
                v.wait_ge(s_av, cs_ + nk_)
                v.reciprocal(rrow[:, 0:512], psDp[np_ % 2][0:1, 0:512]
                             ).then_inc(s_r, 1)
                v.wait_ge(s_b, np_ + 1)
                v.tensor_copy(rbc[:], psE[:, 0:512]).then_inc(s_v2, 1)
                v.wait_ge(s_v2, 2 * np_ + 1)
                if np_ >= 2:
                    v.wait_ge(stp[np_ % 2], 16 * (np_ // 2))
                v.tensor_tensor(ctxc[:, (np_ % 2) * 512:(np_ % 2 + 1) * 512],
                                psCp[np_ % 2][:, 0:512], rbc[:], ALU.mult
                                ).then_inc(s_v2, 1)
            prev = None
            for h in range(nq):
                for qc in range(nch):
                    npair = h * nch + qc
                    cs = cumS(h, qc)
                    nk = nkj(qc)
                    done = False
                    for kj in range(4 * qc, nk):
                        if kj >= 3 and not done and prev is not None:
                            epi(*prev)
                            done = True
                        bank = psA if kj % 2 == 0 else psB
                        vm = kj - 4 * qc
                        v.wait_ge(s_s, cs + kj + 1)
                        v.tensor_tensor(bank[:, 0:512], bank[:, 0:512],
                                        masks[:, vm * 512:(vm + 1) * 512], ALU.add
                                        ).then_inc(s_d, 1)
                    if not done and prev is not None:
                        epi(*prev)
                    prev = (npair, cs, nk)
            epi(*prev)

        @blk.scalar
        def _(sc):
            def store(np_):
                h_, qc_ = np_ // nch, np_ % nch
                sc.wait_ge(s_v2, 2 * np_ + 2)
                sc.dma_start(out=ctxo[ts(h_, 128), ts(qc_, 512)],
                             in_=ctxc[:, (np_ % 2) * 512:(np_ % 2 + 1) * 512]
                             ).then_inc(stp[np_ % 2], 16)
            pv = None
            for h in range(nq):
                for qc in range(nch):
                    npair = h * nch + qc
                    cs = cumS(h, qc)
                    nk = nkj(qc)
                    for kj in range(nk):
                        diag = kj >= 4 * qc
                        bank = psA if kj % 2 == 0 else psB
                        if diag:
                            sc.wait_ge(s_d, 4 * npair + (kj - 4 * qc) + 1)
                        else:
                            sc.wait_ge(s_s, cs + kj + 1)
                        if cs + kj >= 2:
                            sc.wait_ge(s_av, cs + kj - 1)
                        sc.activation(prb[kj % 2][:], bank[:, 0:512], AF.Exp,
                                      scale=ISQ).then_inc(s_e, 1)
                    if pv is not None:
                        store(pv)
                    pv = npair
            store(pv)
            npt = nq * nch
            sc.wait_ge(st0, 16 * ((npt + 1) // 2))
            sc.wait_ge(st1, 16 * (npt // 2))

    return nc


def _consts():
    C = {}
    C["identb"] = np.eye(128).astype(BF)
    C["identf"] = np.eye(128, dtype=np.float32)
    C["onesf"] = np.ones((1, 128), np.float32)
    C["onesb"] = np.ones((128, 1), BF)
    C["onescf"] = np.ones((128, 1), np.float32)
    dm = np.zeros((4, 128, 512), np.float32)
    for v in range(4):
        kj = v * 128 + np.arange(128)[:, None]
        qi = np.arange(512)[None, :]
        dm[v] = np.where(kj > qi, NEG, 0.0)
    C["dmask"] = dm
    rot = np.zeros((128, 128), np.float32)
    for i in range(64):
        rot[64 + i, i] = -1.0
        rot[i, 64 + i] = 1.0
    C["rotb"] = rot.astype(BF)
    C["iota1k"] = np.broadcast_to(np.arange(1024, dtype=np.float32), (128, 1024)).copy()
    C["trif"] = (np.arange(128)[:, None] <= np.arange(128)[None, :]).astype(np.float32)
    C["tcolf"] = np.arange(128, dtype=np.float32)[:, None]
    C["invfs"] = np.concatenate(
        [1.0 / RTH ** (np.arange(0, 128, 2) / 128.0)] * 2)[None, :].astype(np.float32)
    return C


def _run(nc, maps):
    return run_bass_kernel_spmd(nc, maps, core_ids=list(range(NCORE)))


def kernel(**inputs):
    f = lambda k: np.ascontiguousarray(np.asarray(inputs[k], np.float32))
    C = _consts()
    x0 = f("L0_attn_a_x").reshape(T, D)
    wrr = f("L0_ffn_w_router").reshape(1, D)
    p_ln = _cached("ln", lambda: build_ln(1024))
    p_atta = _cached("atta", lambda: build_att(T, 2, 1, False))
    p_attb = _cached("attb", lambda: build_att(T, 2, 2, True))
    p_proj = _cached("proj", lambda: build_proj(1024))
    p_fsel = _cached("fsel", lambda: build_fsel())
    p_fmain = _cached("fmain", lambda: build_fmain())
    p_scat = _cached("scat", lambda: build_scat(1024))
    p_cast = _cached("cast", lambda: build_cast())

    def L_ln(x):
        maps = [{"xs": np.ascontiguousarray(x[c * 1024:(c + 1) * 1024]), "wr": wrr,
                 "onesf": C["onesf"], "identb": C["identb"]} for c in range(NCORE)]
        r = _run(p_ln, maps)
        lntok = np.concatenate([np.asarray(r.results[c]["lntok"]) for c in range(NCORE)], 0)
        lnT = np.concatenate([np.asarray(r.results[c]["lnT"]).reshape(16, 128, 1024)
                              for c in range(NCORE)], 2)
        sco = np.concatenate([np.asarray(r.results[c]["sco"]) for c in range(NCORE)], 1)
        return lntok, lnT, sco

    def L_att(prog, lnT, mk):
        maps = []
        for c in range(NCORE):
            m = mk(c)
            m.update({"lnT": lnT, "identb": C["identb"], "onesb": C["onesb"],
                      "onesf": C["onesf"], "dmask": C["dmask"]})
            maps.append(m)
        r = _run(prog, maps)
        return np.concatenate([np.asarray(r.results[c]["ctxo"]) for c in range(NCORE)], 0)

    def L_proj(ctxT, xprev, W, brow):
        onesbr = np.ones((1, 128), BF)
        maps = [{"cT": np.ascontiguousarray(ctxT[:, c * 1024:(c + 1) * 1024]
                                            ).reshape(16, 128, 1024),
                 "xprev": np.ascontiguousarray(xprev[c * 1024:(c + 1) * 1024]),
                 "Wb": W, "brow": brow, "onesbr": onesbr} for c in range(NCORE)]
        r = _run(p_proj, maps)
        return np.concatenate([np.asarray(r.results[c]["xnew"]) for c in range(NCORE)], 0)

    fwq, fwk, fwv = f("L0_ffn_wq"), f("L0_ffn_wk"), f("L0_ffn_wv")
    fw3 = np.ascontiguousarray(np.concatenate([fwq, fwk, fwv], 1))
    biasc = np.ascontiguousarray(np.concatenate(
        [f("L0_ffn_bq").ravel(), f("L0_ffn_bk").ravel(), f("L0_ffn_bv").ravel()]
        ).reshape(32, 128).T)
    ln1c = np.ascontiguousarray(f("L0_ffn_ln1_w").reshape(16, 128).T)
    ln2bc = np.broadcast_to(f("L0_ffn_ln2_w").reshape(1, D), (128, D)).copy()
    fwg, fwu = f("L0_ffn_w_gate"), f("L0_ffn_w_up")
    fwd, fwo = f("L0_ffn_w_down"), f("L0_ffn_wo")
    woa, wpb = f("L0_attn_a_Wo"), f("L0_attn_b_w_proj")
    allf = np.concatenate([fwg.ravel(), fwu.ravel(), fwd.ravel(), fw3.ravel(),
                           fwo.ravel(), woa.ravel(), wpb.ravel()]).reshape(8, 1088, 8192)
    rc = _run(p_cast, [{"cin": np.ascontiguousarray(allf[c])} for c in range(NCORE)])
    allb = np.concatenate([np.asarray(rc.results[c]["cout"]).ravel()
                           for c in range(NCORE)])
    E = 2048 * 8192
    tile = lambda w: np.ascontiguousarray(
        w.reshape(16, 128, 64, 128).transpose(2, 1, 0, 3).reshape(64, 128, D))
    fwgb, fwub = tile(allb[0:E]), tile(allb[E:2 * E])
    fwdb = allb[2 * E:3 * E].reshape(IMLP, D)
    fw3b = allb[3 * E:3 * E + D * 4096].reshape(D, 4096)
    o = 3 * E + D * 4096
    fwob = allb[o:o + D * D].reshape(D, D)
    woab = allb[o + D * D:o + 2 * D * D].reshape(D, D)
    wpbb = allb[o + 2 * D * D:o + 3 * D * D].reshape(D, D)
    amasks = [np.where(np.arange(SSEL)[None, :] <= (128 * c + np.arange(128))[:, None],
                       0.0, NEG).astype(np.float32) for c in range(NCORE)]

    def ffn(xin, lntok, sco):
        maps = [{"sco": sco, "lnfull": lntok, "fw3b": fw3b, "ln1c": ln1c,
                 "biasc": biasc, "sbase": np.array([[c * 128]], np.uint32),
                 "invfs": C["invfs"], "identf": C["identf"], "identb": C["identb"],
                 "rotb": C["rotb"], "iota1k": C["iota1k"], "trif": C["trif"],
                 "tcolf": C["tcolf"], "onescf": C["onescf"], "onesf": C["onesf"]}
                for c in range(NCORE)]
        rs = _run(p_fsel, maps)
        kkTf = np.concatenate([np.asarray(rs.results[c]["kkTs"]) for c in range(NCORE)], 1)
        vfull = np.concatenate([np.asarray(rs.results[c]["vso"]) for c in range(NCORE)], 0)
        posco = np.asarray(rs.results[0]["posco"])
        acolo = np.asarray(rs.results[0]["acolo"])
        maps = [{"qTs": np.asarray(rs.results[c]["qTs"]), "kkT": kkTf, "vfull": vfull,
                 "seltok": np.asarray(rs.results[c]["selo"]),
                 "myscol": np.ascontiguousarray(posco[1, c * 128:(c + 1) * 128]
                                                ).reshape(128, 1).astype(np.float32),
                 "amask": amasks[c], "fwob": fwob, "fwgb": fwgb, "fwub": fwub,
                 "fwdb": fwdb, "ln2bc": ln2bc, "identb": C["identb"]}
                for c in range(NCORE)]
        rm = _run(p_fmain, maps)
        gated = np.concatenate([np.asarray(rm.results[c]["gato"]) for c in range(NCORE)], 0)
        maps = [{"xprev": np.ascontiguousarray(xin[c * 1024:(c + 1) * 1024]),
                 "lntk": np.ascontiguousarray(lntok[c * 1024:(c + 1) * 1024]),
                 "gated": gated,
                 "acol": np.ascontiguousarray(acolo[:, 8 * c:8 * c + 8])}
                for c in range(NCORE)]
        rr = _run(p_scat, maps)
        return np.concatenate([np.asarray(rr.results[c]["xnew"]) for c in range(NCORE)], 0)

    _, lnT0, _ = L_ln(x0)
    wq, wk, wv = f("L0_attn_a_Wq"), f("L0_attn_a_Wk"), f("L0_attn_a_Wv")
    ctxa = L_att(p_atta, lnT0, lambda c: {"wqkv": np.ascontiguousarray(np.concatenate(
        [wq[:, 256 * c:256 * (c + 1)], wk[:, 128 * (c // 2):128 * (c // 2 + 1)],
         wv[:, 128 * (c // 2):128 * (c // 2 + 1)]], 1))})
    x1 = L_proj(ctxa, x0, woab, f("L0_attn_a_bo").reshape(1, D))

    lntok1, _, sco1 = L_ln(x1)
    x2 = ffn(x1, lntok1, sco1)

    _, lnT2, _ = L_ln(x2)
    wab, bab = f("L0_attn_b_w_attn"), f("L0_attn_b_b_attn").ravel()

    def mk_b(c):
        sl = lambda o: wab[:, o + 256 * c:o + 256 * (c + 1)]
        bsl = lambda o: np.ascontiguousarray(
            bab[o + 256 * c:o + 256 * (c + 1)].reshape(2, 128).T)
        return {"wqkv": np.ascontiguousarray(np.concatenate(
                    [sl(0), sl(2048), sl(4096)], 1)),
                "bqc": bsl(0), "bkc": bsl(2048), "bvc": bsl(4096)}
    ctxb = L_att(p_attb, lnT2, mk_b)
    x3 = L_proj(ctxb, x2, wpbb, f("L0_attn_b_b_proj").reshape(1, D))

    lntok3, _, sco3 = L_ln(x3)
    x4 = ffn(x3, lntok3, sco3)
    return np.ascontiguousarray(x4.reshape(1, T, D).astype(np.float32))


def build_proj(NT=1024):
    nc = bass.Bass("TRN2", target_bir_lowering=False)
    nt = NT // 128
    cT = nc.dram_tensor("cT", [16, 128, NT], BF16, kind="ExternalInput").ap()
    xprev = nc.dram_tensor("xprev", [NT, D], F32, kind="ExternalInput").ap()
    brow = nc.dram_tensor("brow", [1, D], F32, kind="ExternalInput").ap()
    onesbr = nc.dram_tensor("onesbr", [1, 128], BF16, kind="ExternalInput").ap()
    xnew = nc.dram_tensor("xnew", [NT, D], F32, kind="ExternalOutput").ap()
    Wb = nc.dram_tensor("Wb", [D, D], BF16, kind="ExternalInput").ap()
    bb = nc.dram_tensor("bb", [1, D], BF16).ap()
    es = ExitStack()
    sb = lambda n, shp, dt: es.enter_context(nc.sbuf_tensor(n, shp, dt))
    ps = lambda n, shp, dt=F32: es.enter_context(nc.psum_tensor(n, shp, dt))
    sem = lambda n: es.enter_context(nc.semaphore(n))
    with es:
        cts = sb("cts", [128, 16 * NT], BF16)
        ws = sb("ws", [128, 16 * 2048], BF16)
        xt = sb("xt", [128, D], F32)
        xt2 = sb("xt2", [128, D], F32)
        xo = sb("xo", [128, D], F32)
        xo2 = sb("xo2", [128, D], F32)
        bbs = sb("bbs", [1, D], BF16)
        onbr = sb("onbr", [1, 128], BF16)
        psA = ps("psA", [128, 512])
        psB = ps("psB", [128, 512])
        sg = sem("sg"); slw = sem("slw"); sl0 = sem("sl0"); sl1 = sem("sl1")
        sp = sem("sp"); sv = sem("sv"); st0 = sem("st0"); st1 = sem("st1")
        sw2 = sem("sw2")
        blk = es.enter_context(nc.Block())
        slp = [sl0, sl1]
        stp = [st0, st1]
        xb = [xt, xt2]
        xob = [xo, xo2]

        @blk.gpsimd
        def _(g):
            g.dma_start(out=bb, in_=brow).then_inc(sg, 16)

        @blk.sync
        def _(s_):
            s_.dma_start(out=onbr[:], in_=onesbr).then_inc(slw, 16)
            s_.dma_start(out=cts.rearrange("p (dt n) -> p dt n", n=NT),
                         in_=cT.rearrange("dt p n -> p dt n")).then_inc(slw, 16)
            s_.wait_ge(sg, 16)
            s_.dma_start(out=bbs[:], in_=bb).then_inc(slw, 16)
            for dm in range(0, 16, 2):
                s_.dma_start(out=ws[:, dm * 2048:(dm + 1) * 2048],
                             in_=Wb[ts(dm, 128), :]).then_inc(slw, 16)
            for i in range(nt):
                if i >= 2:
                    s_.wait_ge(stp[i % 2], 16 * (i // 2))
                s_.dma_start(out=xb[i % 2][:], in_=xprev[ts(i, 128), :]
                             ).then_inc(slp[i % 2], 16)

        @blk.tensor
        def _(t):
            t.wait_ge(slw, 11 * 16)
            t.wait_ge(sw2, 8 * 16)
            for i in range(nt):
                for ch in range(4):
                    u = i * 4 + ch
                    if u >= 2:
                        t.wait_ge(sv, u - 1)
                    bank = psA if u % 2 == 0 else psB
                    for dm in range(16):
                        t.matmul(bank[:, 0:512],
                                 cts[:, dm * NT + i * 128:dm * NT + (i + 1) * 128],
                                 ws[:, dm * 2048 + ch * 512:dm * 2048 + (ch + 1) * 512],
                                 start=(dm == 0), stop=False)
                    t.matmul(bank[:, 0:512], onbr[:, :],
                             bbs[:, ch * 512:(ch + 1) * 512],
                             start=False, stop=True).then_inc(sp, 1)

        @blk.vector
        def _(v):
            for i in range(nt):
                v.wait_ge(slp[i % 2], 16 * (i // 2 + 1))
                if i >= 2:
                    v.wait_ge(stp[i % 2], 16 * (i // 2))
                for ch in range(4):
                    u = i * 4 + ch
                    v.wait_ge(sp, u + 1)
                    bank = psA if u % 2 == 0 else psB
                    v.scalar_tensor_tensor(
                        out=xob[i % 2][:, ch * 512:(ch + 1) * 512], in0=bank[:, 0:512],
                        scalar=1.0, in1=xb[i % 2][:, ch * 512:(ch + 1) * 512],
                        op0=ALU.mult, op1=ALU.add).then_inc(sv, 1)

        @blk.scalar
        def _(sc):
            for dm in range(1, 16, 2):
                sc.dma_start(out=ws[:, dm * 2048:(dm + 1) * 2048],
                             in_=Wb[ts(dm, 128), :]).then_inc(sw2, 16)
            for i in range(nt):
                sc.wait_ge(sv, i * 4 + 4)
                sc.dma_start(out=xnew[ts(i, 128), :], in_=xob[i % 2][:]
                             ).then_inc(stp[i % 2], 16)
            sc.wait_ge(st0, 16 * ((nt + 1) // 2))
            sc.wait_ge(st1, 16 * (nt // 2))

    return nc


def build_scat(NT=1024):
    nc = bass.Bass("TRN2", target_bir_lowering=False)
    nt = NT // 128
    xprev = nc.dram_tensor("xprev", [NT, D], F32, kind="ExternalInput").ap()
    lntk = nc.dram_tensor("lntk", [NT, D], BF16, kind="ExternalInput").ap()
    gated = nc.dram_tensor("gated", [SSEL, D], BF16, kind="ExternalInput").ap()
    acol = nc.dram_tensor("acol", [128, 8], U32, kind="ExternalInput").ap()
    xnew = nc.dram_tensor("xnew", [NT, D], F32, kind="ExternalOutput").ap()
    es = ExitStack()
    sb = lambda n, shp, dt: es.enter_context(nc.sbuf_tensor(n, shp, dt))
    sem = lambda n: es.enter_context(nc.semaphore(n))
    with es:
        xt = sb("xt", [128, D], F32)
        xt2 = sb("xt2", [128, D], F32)
        lt = sb("lt", [128, D], BF16)
        lt2 = sb("lt2", [128, D], BF16)
        gb = sb("gb", [128, D], BF16)
        gb2 = sb("gb2", [128, D], BF16)
        xo = sb("xo", [128, D], F32)
        xo2 = sb("xo2", [128, D], F32)
        acs = sb("acs", [128, 8], U32)
        af = sb("af", [128, 8], F32)
        mcol = sb("mcol", [128, 8], F32)
        sg0 = sem("sg0"); sg1 = sem("sg1")
        sl0 = sem("sl0"); sl1 = sem("sl1"); sla = sem("sla"); smz = sem("smz")
        sv = sem("sv"); sz = sem("sz")
        st0 = sem("st0"); st1 = sem("st1")
        blk = es.enter_context(nc.Block())
        sgp = [sg0, sg1]
        slp = [sl0, sl1]
        stp = [st0, st1]
        xb = [xt, xt2]
        lb = [lt, lt2]
        gbb = [gb, gb2]
        xob = [xo, xo2]

        @blk.gpsimd
        def _(g):
            g.dma_start(out=acs[:], in_=acol).then_inc(sla, 16)
            g.memset(gb[:], 0.0).then_inc(smz, 1)
            g.memset(gb2[:], 0.0).then_inc(smz, 1)
            g.wait_ge(sla, 16)
            g.wait_ge(smz, 2)
            for i in range(nt):
                if i >= 2:
                    g.wait_ge(sv, 3 * (i - 1))
                g.indirect_dma_start(
                    gbb[i % 2][:], None, gated,
                    bass.IndirectOffsetOnAxis(ap=acs[:, i:i + 1], axis=0),
                    bounds_check=SSEL - 1, oob_is_err=False).then_inc(sgp[i % 2], 16)

        @blk.sync
        def _(s_):
            for i in range(nt):
                if i >= 2:
                    s_.wait_ge(stp[i % 2], 16 * (i // 2))
                s_.dma_start(out=xb[i % 2][:], in_=xprev[ts(i, 128), :]
                             ).then_inc(slp[i % 2], 16)
                s_.dma_start(out=lb[i % 2][:], in_=lntk[ts(i, 128), :]
                             ).then_inc(slp[i % 2], 16)

        @blk.vector
        def _(v):
            v.wait_ge(sla, 16)
            v.tensor_copy(af[:], acs[:]).then_inc(sz, 1)
            v.wait_ge(sz, 1)
            v.tensor_scalar(mcol[:], af[:], float(SSEL) - 0.5, None, ALU.is_ge
                            ).then_inc(sz, 1)
            for i in range(nt):
                v.wait_ge(sz, 2)
                v.wait_ge(slp[i % 2], 32 * (i // 2 + 1))
                v.wait_ge(sgp[i % 2], 16 * (i // 2 + 1))
                if i >= 2:
                    v.wait_ge(stp[i % 2], 16 * (i // 2))
                v.scalar_tensor_tensor(
                    out=xob[i % 2][:], in0=lb[i % 2][:], scalar=mcol[:, i:i + 1],
                    in1=xb[i % 2][:], op0=ALU.mult, op1=ALU.add).then_inc(sv, 1)
                v.wait_ge(sv, 3 * i + 1)
                v.tensor_tensor(xob[i % 2][:], xob[i % 2][:], gbb[i % 2][:], ALU.add
                                ).then_inc(sv, 1)
                v.wait_ge(sv, 3 * i + 2)
                v.tensor_scalar(gbb[i % 2][:], gbb[i % 2][:], 0.0, None, ALU.mult
                                ).then_inc(sv, 1)

        @blk.scalar
        def _(sc):
            for i in range(nt):
                sc.wait_ge(sv, 3 * i + 2)
                sc.dma_start(out=xnew[ts(i, 128), :], in_=xob[i % 2][:]
                             ).then_inc(stp[i % 2], 16)
            sc.wait_ge(st0, 16 * ((nt + 1) // 2))
            sc.wait_ge(st1, 16 * (nt // 2))

    return nc


NBIS = 46


class _Sched:
    KINDS = {"vector": ("sv", 1), "tensor": ("sp", 1), "scalar": ("sa", 1),
             "gpsimd": ("sz", 1), "sync_dma": ("sl", 16),
             "scalar_dma": ("sa2", 16), "gpsimd_dma": ("sg", 16)}

    def __init__(self):
        self.steps = []
        self.cnt = {v[0]: 0 for v in self.KINDS.values()}

    def add(self, kind, fn, lag=None):
        semn, inc = self.KINDS[kind]
        waits = {}
        for k, v in self.cnt.items():
            if lag:
                for lk, ln in lag.items():
                    if self.KINDS[lk][0] == k:
                        v = v - ln * self.KINDS[lk][1]
            if v > 0:
                waits[k] = v
        self.steps.append((kind, fn, waits))
        self.cnt[semn] += inc

    def emit_one(self, eng, sems, kinds):
        lw = {}
        for kind, fn, waits in self.steps:
            if kind not in kinds:
                continue
            semn, inc = self.KINDS[kind]
            for k, v in waits.items():
                if lw.get(k, -1) < v:
                    eng.wait_ge(sems[k], v)
                    lw[k] = v
            inst = fn(eng)
            if inst is None:
                raise RuntimeError("step returned no instruction")
            inst.then_inc(sems[semn], inc)


def build_fsel():
    nc = bass.Bass("TRN2", target_bir_lowering=False)
    sco = nc.dram_tensor("sco", [128, 64], F32, kind="ExternalInput").ap()
    lnfull = nc.dram_tensor("lnfull", [T, D], BF16, kind="ExternalInput").ap()
    ln1c = nc.dram_tensor("ln1c", [128, 16], F32, kind="ExternalInput").ap()
    biasc = nc.dram_tensor("biasc", [128, 32], F32, kind="ExternalInput").ap()
    sbase = nc.dram_tensor("sbase", [1, 1], U32, kind="ExternalInput").ap()
    invfs = nc.dram_tensor("invfs", [1, 128], F32, kind="ExternalInput").ap()
    identf = nc.dram_tensor("identf", [128, 128], F32, kind="ExternalInput").ap()
    identb = nc.dram_tensor("identb", [128, 128], BF16, kind="ExternalInput").ap()
    rotb = nc.dram_tensor("rotb", [128, 128], BF16, kind="ExternalInput").ap()
    iota1k = nc.dram_tensor("iota1k", [128, 1024], F32, kind="ExternalInput").ap()
    trif = nc.dram_tensor("trif", [128, 128], F32, kind="ExternalInput").ap()
    tcolf = nc.dram_tensor("tcolf", [128, 1], F32, kind="ExternalInput").ap()
    onescf = nc.dram_tensor("onescf", [128, 1], F32, kind="ExternalInput").ap()
    onesf = nc.dram_tensor("onesf", [1, 128], F32, kind="ExternalInput").ap()

    qTs = nc.dram_tensor("qTs", [16 * 128, 128], BF16, kind="ExternalOutput").ap()
    kkTs = nc.dram_tensor("kkTs", [8 * 128, 128], BF16, kind="ExternalOutput").ap()
    vso = nc.dram_tensor("vso", [128, 1024], BF16, kind="ExternalOutput").ap()
    selo = nc.dram_tensor("selo", [128, D], BF16, kind="ExternalOutput").ap()
    posco = nc.dram_tensor("posco", [2, SSEL], F32, kind="ExternalOutput").ap()
    acolo = nc.dram_tensor("acolo", [128, 64], U32, kind="ExternalOutput").ap()

    fw3b = nc.dram_tensor("fw3b", [D, 4096], BF16, kind="ExternalInput").ap()
    posd = nc.dram_tensor("posd", [SSEL, 1], F32).ap()

    es = ExitStack()
    sb = lambda n, shp, dt: es.enter_context(nc.sbuf_tensor(n, shp, dt))
    ps = lambda n, shp, dt=F32: es.enter_context(nc.psum_tensor(n, shp, dt))
    sem = lambda n: es.enter_context(nc.semaphore(n))
    with es:
        scsb = sb("scsb", [128, 64], F32)
        selw = sb("selw", [128, 64], F32)
        mge = sb("mge", [128, 64], F32)
        mgt = sb("mgt", [128, 64], F32)
        meq = sb("meq", [128, 64], F32)
        Ceq = sb("Ceq", [128, 64], F32)
        Csel = sb("Csel", [128, 64], F32)
        afb = sb("afb", [128, 64], F32)
        acu = sb("acu", [128, 64], U32)
        srow = sb("srow", [1, 16], F32)
        crow1 = sb("crow1", [1, 64], F32)
        crow2 = sb("crow2", [1, 64], F32)
        exbc = sb("exbc", [128, 64], F32)
        midc = sb("midc", [128, 1], F32)
        cntp = sb("cntp", [128, 1], F32)
        needc = sb("needc", [128, 1], F32)
        iot = sb("iot", [128, 1024], F32)
        trs = sb("trs", [128, 128], F32)
        tcs = sb("tcs", [128, 1], F32)
        onc = sb("onc", [128, 1], F32)
        onr = sb("onr", [1, 128], F32)
        idf = sb("idf", [128, 128], F32)
        idb = sb("idb", [128, 128], BF16)
        rob = sb("rob", [128, 128], BF16)
        Pf_ = [sb("Pf0", [128, 1024], F32), sb("Pf1", [128, 1024], F32)]
        stf_ = [sb("stf0", [128, 2], F32), sb("stf1", [128, 2], F32)]
        poscosb = sb("poscosb", [2, 1024], F32)
        sbs = sb("sbs", [1, 1], U32)
        sbf1 = sb("sbf1", [1, 1], F32)
        slotf = sb("slotf", [128, 1], F32)
        slotu = sb("slotu", [128, 1], U32)
        posg = sb("posg", [128, 1], F32)
        myidx = sb("myidx", [128, 1], U32)
        posrow = sb("posrow", [1, 128], F32)
        selrow = sb("selrow", [128, D], BF16)
        sq = sb("sq", [128, D], F32)
        rms = sb("rms", [128, 8], F32)
        hrow = sb("hrow", [128, D], BF16)
        hT = sb("hT", [128, D], BF16)
        fws = sb("fws", [128, 65536], BF16)
        lnc1 = sb("lnc1", [128, 16], F32)
        bcs = sb("bcs", [128, 32], F32)
        qko = sb("qko", [128, 4096], BF16)
        invr = sb("invr", [1, 128], F32)
        angs = sb("angs", [128, 128], F32)
        u1 = sb("u1", [128, 128], F32)
        ri = sb("ri", [128, 128], I32)
        rf = sb("rf", [128, 128], F32)
        sini = sb("sini", [128, 128], F32)
        sinv = sb("sinv", [128, 128], F32)
        cosv = sb("cosv", [128, 128], F32)
        rt1 = sb("rt1", [128, 128], F32)
        rt2 = sb("rt2", [128, 128], F32)
        vstok = sb("vstok", [128, 1024], BF16)
        psF = ps("psF", [128, 512])
        psG = ps("psG", [128, 512])
        psH1 = ps("psH1", [128, 512])
        psH2 = ps("psH2", [128, 512])
        psR = ps("psR", [128, 512])
        ptr = ps("ptr", [128, 1024], BF16)
        SEMS = {n: sem(n) for n in ["sv", "sp", "sa", "sz", "sl", "sa2", "sg"]}
        blk = es.enter_context(nc.Block())

        S = _Sched()

        S.add("gpsimd_dma", lambda g: g.dma_start(out=sbs[:], in_=sbase))
        jobs = [(scsb, sco), (iot, iota1k), (trs, trif), (tcs, tcolf),
                (onc, onescf), (onr, onesf), (idf, identf), (idb, identb),
                (rob, rotb), (lnc1, ln1c), (bcs, biasc), (invr, invfs)]
        for dst, src in jobs:
            S.add("sync_dma", lambda s_, dst=dst, src=src: s_.dma_start(out=dst[:], in_=src))
        BIG = {k: 10 ** 6 for k in _Sched.KINDS}
        for dm in range(16):
            q = "sync_dma" if dm % 2 == 0 else "scalar_dma"
            S.add(q, lambda e, dm=dm: e.dma_start(
                out=fws[:, dm * 4096:(dm + 1) * 4096], in_=fw3b[ts(dm, 128), :]),
                lag=BIG)

        S.add("vector", lambda v: v.tensor_reduce(midc[:], scsb[:], AX.X, ALU.max,
                                                  apply_absolute_value=True))
        S.add("tensor", lambda t: t.transpose(psG[0:1, 0:128], midc[:], idf[:]))
        S.add("vector", lambda v: v.tensor_reduce(srow[:, 4:5], psG[0:1, 0:128],
                                                  AX.X, ALU.max))
        S.add("vector", lambda v: v.tensor_scalar(srow[:, 1:2], srow[:, 4:5], 1.0,
                                                  None, ALU.add))
        S.add("vector", lambda v: v.tensor_scalar(srow[:, 0:1], srow[:, 1:2], -1.0,
                                                  None, ALU.mult))
        for it in range(NBIS):
            S.add("vector", lambda v: v.tensor_tensor(srow[:, 4:5], srow[:, 0:1],
                                                      srow[:, 1:2], ALU.add))
            S.add("vector", lambda v: v.tensor_scalar(srow[:, 2:3], srow[:, 4:5], 0.5,
                                                      None, ALU.mult))
            S.add("tensor", lambda t: t.matmul(psF[:, 0:1], onr[:, :], srow[:, 2:3],
                                               start=True, stop=True))
            S.add("vector", lambda v: v.tensor_copy(midc[:], psF[:, 0:1]))
            S.add("vector", lambda v: v.tensor_scalar(selw[:], scsb[:], midc[:], 0.0,
                                                      ALU.is_ge, ALU.add,
                                                      accum_out=cntp[:]))
            S.add("tensor", lambda t: t.matmul(psG[0:1, 0:1], onc[:], cntp[:],
                                               start=True, stop=True))
            S.add("vector", lambda v: v.tensor_scalar(srow[:, 3:4], psG[0:1, 0:1],
                                                      1023.5, None, ALU.is_ge))
            S.add("vector", lambda v: v.tensor_tensor(srow[:, 5:6], srow[:, 2:3],
                                                      srow[:, 0:1], ALU.subtract))
            S.add("vector", lambda v: v.tensor_tensor(srow[:, 6:7], srow[:, 5:6],
                                                      srow[:, 3:4], ALU.mult))
            S.add("vector", lambda v: v.tensor_tensor(srow[:, 0:1], srow[:, 0:1],
                                                      srow[:, 6:7], ALU.add))
            S.add("vector", lambda v: v.tensor_tensor(srow[:, 7:8], srow[:, 1:2],
                                                      srow[:, 2:3], ALU.subtract))
            S.add("vector", lambda v: v.tensor_tensor(srow[:, 8:9], srow[:, 7:8],
                                                      srow[:, 3:4], ALU.mult))
            S.add("vector", lambda v: v.tensor_tensor(srow[:, 9:10], srow[:, 2:3],
                                                      srow[:, 8:9], ALU.add))
            S.add("vector", lambda v: v.tensor_copy(srow[:, 1:2], srow[:, 9:10]))

        S.add("tensor", lambda t: t.matmul(psF[:, 0:1], onr[:, :], srow[:, 0:1],
                                           start=True, stop=True))
        S.add("vector", lambda v: v.tensor_copy(midc[:], psF[:, 0:1]))
        S.add("vector", lambda v: v.tensor_scalar(mge[:], scsb[:], midc[:], None,
                                                  ALU.is_ge))
        S.add("tensor", lambda t: t.matmul(psF[:, 0:1], onr[:, :], srow[:, 1:2],
                                           start=True, stop=True))
        S.add("vector", lambda v: v.tensor_copy(midc[:], psF[:, 0:1]))
        S.add("vector", lambda v: v.tensor_scalar(mgt[:], scsb[:], midc[:], 0.0,
                                                  ALU.is_ge, ALU.add,
                                                  accum_out=cntp[:]))
        S.add("vector", lambda v: v.tensor_tensor(meq[:], mge[:], mgt[:], ALU.subtract))
        S.add("tensor", lambda t: t.matmul(psG[0:1, 0:1], onc[:], cntp[:],
                                           start=True, stop=True))
        S.add("vector", lambda v: v.tensor_scalar(srow[:, 10:11], psG[0:1, 0:1],
                                                  -1.0, 1024.0, ALU.mult, ALU.add))
        S.add("tensor", lambda t: t.matmul(psF[:, 0:1], onr[:, :], srow[:, 10:11],
                                           start=True, stop=True))
        S.add("vector", lambda v: v.tensor_copy(needc[:], psF[:, 0:1]))

        def cumsum_steps(src, dst):
            S.add("tensor", lambda t: t.matmul(psG[:, 0:64], trs[:], src[:],
                                               start=True, stop=True))
            S.add("tensor", lambda t: t.matmul(psF[0:1, 0:64], onc[:], src[:],
                                               start=True, stop=True))
            S.add("vector", lambda v: v.tensor_copy(crow1[:], psF[0:1, 0:64]))
            for k in [1, 2, 4, 8, 16, 32]:
                S.add("vector", lambda v, k=k: v.tensor_copy(crow2[:, 0:k], crow1[:, 0:k]))
                S.add("vector", lambda v, k=k: v.tensor_tensor(
                    crow2[:, k:64], crow1[:, k:64], crow1[:, 0:64 - k], ALU.add))
                S.add("vector", lambda v: v.tensor_copy(crow1[:], crow2[:]))
            S.add("vector", lambda v: v.tensor_tensor(crow2[:], crow1[:],
                                                      psF[0:1, 0:64], ALU.subtract))
            S.add("tensor", lambda t: t.matmul(psF[:, 0:64], onr[:, :], crow2[:],
                                               start=True, stop=True))
            S.add("vector", lambda v: v.tensor_copy(exbc[:], psF[:, 0:64]))
            S.add("vector", lambda v: v.tensor_tensor(dst[:], psG[:, 0:64], exbc[:],
                                                      ALU.add))

        cumsum_steps(meq, Ceq)
        S.add("vector", lambda v: v.tensor_scalar(Ceq[:], Ceq[:], needc[:], None,
                                                  ALU.is_le))
        S.add("vector", lambda v: v.tensor_tensor(Ceq[:], Ceq[:], meq[:], ALU.mult))
        S.add("vector", lambda v: v.tensor_tensor(selw[:], mgt[:], Ceq[:], ALU.add))
        cumsum_steps(selw, Csel)
        S.add("vector", lambda v: v.tensor_scalar(afb[:], Csel[:], -1.0, None, ALU.add))
        S.add("vector", lambda v: v.tensor_tensor(afb[:], afb[:], selw[:], ALU.mult))
        S.add("vector", lambda v: v.tensor_scalar(Csel[:], selw[:], -1e6, 1e6,
                                                  ALU.mult, ALU.add))
        S.add("vector", lambda v: v.tensor_tensor(afb[:], afb[:], Csel[:], ALU.add))
        S.add("vector", lambda v: v.tensor_copy(acu[:], afb[:]))
        S.add("scalar_dma", lambda sc: sc.dma_start(out=acolo, in_=acu[:]))

        for f in range(64):
            pf = Pf_[f % 2]
            stf = stf_[f % 2]
            S.add("vector", lambda v, f=f, stf=stf: v.tensor_scalar(
                stf[:, 0:1], tcs[:], float(f * 128), None, ALU.add))
            S.add("vector", lambda v, f=f, stf=stf: v.tensor_copy(
                stf[:, 1:2], scsb[:, f:f + 1]))
            S.add("vector", lambda v, f=f, pf=pf: v.tensor_scalar(
                pf[:], iot[:], afb[:, f:f + 1], 0.0, ALU.subtract, ALU.is_equal))
            S.add("tensor", lambda t, f=f, pf=pf, stf=stf: t.matmul(
                psH1[0:2, 0:512], stf[:], pf[:, 0:512],
                start=(f == 0), stop=(f == 63)))
            S.add("tensor", lambda t, f=f, pf=pf, stf=stf: t.matmul(
                psH2[0:2, 0:512], stf[:], pf[:, 512:1024],
                start=(f == 0), stop=(f == 63)))
        S.add("vector", lambda v: v.tensor_copy(poscosb[:, 0:512], psH1[0:2, 0:512]))
        S.add("vector", lambda v: v.tensor_copy(poscosb[:, 512:1024], psH2[0:2, 0:512]))
        S.add("scalar_dma", lambda sc: sc.dma_start(out=posco, in_=poscosb[:]))
        S.add("scalar_dma", lambda sc: sc.dma_start(
            out=posd.rearrange("a b -> b a"), in_=poscosb[0:1, :]))

        S.add("vector", lambda v: v.tensor_copy(sbf1[:], sbs[:]))
        S.add("tensor", lambda t: t.matmul(psF[:, 0:1], onr[:, :], sbf1[:],
                                           start=True, stop=True))
        S.add("vector", lambda v: v.tensor_copy(slotf[:], psF[:, 0:1]))
        S.add("vector", lambda v: v.tensor_tensor(slotf[:], slotf[:], tcs[:], ALU.add))
        S.add("vector", lambda v: v.tensor_copy(slotu[:], slotf[:]))
        S.add("gpsimd_dma", lambda g: g.indirect_dma_start(
            posg[:], None, posd, bass.IndirectOffsetOnAxis(ap=slotu[:], axis=0)))
        S.add("vector", lambda v: v.tensor_copy(myidx[:], posg[:]))
        S.add("gpsimd_dma", lambda g: g.indirect_dma_start(
            selrow[:], None, lnfull, bass.IndirectOffsetOnAxis(ap=myidx[:], axis=0)))
        S.add("tensor", lambda t: t.transpose(psF[0:1, 0:128], posg[:], idf[:]))
        S.add("vector", lambda v: v.tensor_copy(posrow[:], psF[0:1, 0:128]))
        S.add("scalar_dma", lambda sc: sc.dma_start(out=selo, in_=selrow[:]))

        S.add("vector", lambda v: v.tensor_tensor(sq[:], selrow[:], selrow[:], ALU.mult))
        S.add("vector", lambda v: v.tensor_reduce(rms[:, 0:1], sq[:], AX.X, ALU.add))
        S.add("vector", lambda v: v.tensor_scalar(rms[:, 1:2], rms[:, 0:1],
                                                  1.0 / 2048.0, 1e-6, ALU.mult, ALU.add))
        S.add("vector", lambda v: v.reciprocal(rms[:, 2:3], rms[:, 1:2]))
        S.add("scalar", lambda sc: sc.activation(rms[:, 3:4], rms[:, 2:3], AF.Sqrt))
        S.add("vector", lambda v: v.tensor_scalar(hrow[:], selrow[:], rms[:, 3:4],
                                                  None, ALU.mult))
        for g_ in range(2):
            bank = ptr
            def trg(t, g_=g_):
                for k in range(8):
                    dt = g_ * 8 + k
                    mm = t.transpose(ptr[:, k * 128:(k + 1) * 128],
                                     hrow[:, dt * 128:(dt + 1) * 128], idb[:])
                return mm
            S.add("tensor", trg)
            for k in range(8):
                dt = g_ * 8 + k
                S.add("vector", lambda v, dt=dt, k=k: v.tensor_scalar(
                    hT[:, dt * 128:(dt + 1) * 128], ptr[:, k * 128:(k + 1) * 128],
                    lnc1[:, dt:dt + 1], None, ALU.mult))

        for io in range(32):
            bank = psH1 if io % 2 == 0 else psH2
            def qmm(t, io=io, bank=bank):
                for dm in range(16):
                    mm = t.matmul(bank[:, 0:128],
                                  fws[:, (dm * 32 + io) * 128:(dm * 32 + io + 1) * 128],
                                  hT[:, dm * 128:(dm + 1) * 128],
                                  start=(dm == 0), stop=(dm == 15))
                return mm
            S.add("tensor", qmm)
            S.add("vector", lambda v, io=io, bank=bank: v.tensor_scalar(
                qko[:, io * 128:(io + 1) * 128], bank[:, 0:128],
                bcs[:, io:io + 1], None, ALU.add))

        S.add("tensor", lambda t: t.matmul(psR[:, 0:128], invr[:, :], posrow[:],
                                           start=True, stop=True))
        S.add("vector", lambda v: v.tensor_copy(angs[:], psR[:, 0:128]))
        for nm, off in [("sin", 0.0), ("cos", 0.25)]:
            dstv = sinv if nm == "sin" else cosv
            S.add("vector", lambda v, off=off: v.tensor_scalar(
                u1[:], angs[:], 1.0 / TWOPI, off, ALU.mult, ALU.add))
            S.add("vector", lambda v: v.tensor_copy(ri[:], u1[:]))
            S.add("vector", lambda v: v.tensor_copy(rf[:], ri[:]))
            S.add("vector", lambda v: v.tensor_tensor(sini[:], u1[:], rf[:],
                                                      ALU.subtract))
            S.add("vector", lambda v: v.tensor_scalar(rf[:], sini[:], 0.5, None,
                                                      ALU.is_gt))
            S.add("vector", lambda v: v.tensor_tensor(sini[:], sini[:], rf[:],
                                                      ALU.subtract))
            S.add("vector", lambda v: v.tensor_scalar(rf[:], sini[:], -0.5, None,
                                                      ALU.is_lt))
            S.add("vector", lambda v: v.tensor_tensor(sini[:], sini[:], rf[:],
                                                      ALU.add))
            S.add("vector", lambda v: v.tensor_scalar(sini[:], sini[:], TWOPI,
                                                      None, ALU.mult))
            S.add("scalar", lambda sc, dstv=dstv: sc.activation(dstv[:], sini[:],
                                                                AF.Sin))
        for ti in range(24):
            S.add("tensor", lambda t, ti=ti: t.matmul(
                psR[:, 0:128], rob[:], qko[:, ti * 128:(ti + 1) * 128],
                start=True, stop=True))
            S.add("vector", lambda v, ti=ti: v.tensor_tensor(
                rt1[:], qko[:, ti * 128:(ti + 1) * 128], cosv[:], ALU.mult))
            S.add("vector", lambda v: v.tensor_tensor(rt2[:], psR[:, 0:128], sinv[:],
                                                      ALU.mult))
            S.add("vector", lambda v, ti=ti: v.tensor_tensor(
                qko[:, ti * 128:(ti + 1) * 128], rt1[:], rt2[:], ALU.add))

        def vtr(t):
            for k in range(8):
                mm = t.transpose(ptr[:, k * 128:(k + 1) * 128],
                                 qko[:, (24 + k) * 128:(25 + k) * 128], idb[:])
            return mm
        S.add("tensor", vtr)
        S.add("vector", lambda v: v.tensor_copy(vstok[:], ptr[:]))

        S.add("scalar_dma", lambda sc: sc.dma_start(
            out=qTs.rearrange("(dt p) s -> p dt s", p=128),
            in_=qko[:, 0:2048].rearrange("p (dt s) -> p dt s", s=128)))
        S.add("scalar_dma", lambda sc: sc.dma_start(
            out=kkTs.rearrange("(dt p) s -> p dt s", p=128),
            in_=qko[:, 2048:3072].rearrange("p (dt s) -> p dt s", s=128)))
        S.add("scalar_dma", lambda sc: sc.dma_start(out=vso, in_=vstok[:]))

        @blk.sync
        def _(s_):
            S.emit_one(s_, SEMS, ["sync_dma"])

        @blk.gpsimd
        def _(g):
            S.emit_one(g, SEMS, ["gpsimd", "gpsimd_dma"])

        @blk.vector
        def _(v):
            S.emit_one(v, SEMS, ["vector"])

        @blk.tensor
        def _(t):
            S.emit_one(t, SEMS, ["tensor"])

        @blk.scalar
        def _(sc):
            S.emit_one(sc, SEMS, ["scalar", "scalar_dma"])
            sc.wait_ge(SEMS["sa2"], S.cnt["sa2"])
            sc.wait_ge(SEMS["sg"], S.cnt["sg"])
            sc.wait_ge(SEMS["sv"], S.cnt["sv"])

    return nc

def build_fmain():
    nc = bass.Bass("TRN2", target_bir_lowering=False)
    qTs = nc.dram_tensor("qTs", [16 * 128, 128], BF16, kind="ExternalInput").ap()
    kkT = nc.dram_tensor("kkT", [8 * 128, SSEL], BF16, kind="ExternalInput").ap()
    vfull = nc.dram_tensor("vfull", [SSEL, 1024], BF16, kind="ExternalInput").ap()
    seltok = nc.dram_tensor("seltok", [128, D], BF16, kind="ExternalInput").ap()
    myscol = nc.dram_tensor("myscol", [128, 1], F32, kind="ExternalInput").ap()
    amask = nc.dram_tensor("amask", [128, SSEL], F32, kind="ExternalInput").ap()
    ln2bc = nc.dram_tensor("ln2bc", [128, D], F32, kind="ExternalInput").ap()
    identb = nc.dram_tensor("identb", [128, 128], BF16, kind="ExternalInput").ap()
    gato = nc.dram_tensor("gato", [128, D], BF16, kind="ExternalOutput").ap()

    fwob = nc.dram_tensor("fwob", [D, D], BF16, kind="ExternalInput").ap()
    fwgb = nc.dram_tensor("fwgb", [64, 128, D], BF16, kind="ExternalInput").ap()
    fwub = nc.dram_tensor("fwub", [64, 128, D], BF16, kind="ExternalInput").ap()
    fwdb = nc.dram_tensor("fwdb", [IMLP, D], BF16, kind="ExternalInput").ap()

    es = ExitStack()
    sb = lambda n, shp, dt: es.enter_context(nc.sbuf_tensor(n, shp, dt))
    ps = lambda n, shp, dt=F32: es.enter_context(nc.psum_tensor(n, shp, dt))
    sem = lambda n: es.enter_context(nc.semaphore(n))
    with es:
        qs = sb("qs", [128, 2048], BF16)
        ks = sb("ks", [128, 8192], BF16)
        vss = sb("vss", [128, 8192], BF16)
        Pb = sb("Pb", [128, 1024], BF16)
        Pe = sb("Pe", [128, 1024], F32)
        PT = sb("PT", [128, 1024], BF16)
        ctxT = sb("ctxT", [128, 2048], BF16)
        sel = sb("sel", [128, D], BF16)
        self32 = sb("self32", [128, D], F32)
        x1tok = sb("x1tok", [128, D], F32)
        sqtok = sb("sqtok", [128, D], F32)
        h2k = sb("h2k", [128, D], BF16)
        h2T = sb("h2T", [128, D], BF16)
        gatok = sb("gatok", [128, D], BF16)
        msk = sb("msk", [128, SSEL], F32)
        lnbc = sb("lnbc", [128, D], F32)
        wsl = [sb(f"wsl{i}", [128, 2048], BF16) for i in range(2)]
        wgl = [sb(f"wgl{i}", [128, 2048], BF16) for i in range(2)]
        wul = [sb(f"wul{i}", [128, 2048], BF16) for i in range(2)]
        wdl = [sb(f"wdl{i}", [128, 2048], BF16) for i in range(2)]
        g1 = sb("g1", [128, 128], F32)
        mt = sb("mt", [128, 128], BF16)
        sm2 = sb("sm2", [128, 16], F32)
        rstdc = sb("rstdc", [128, 1], F32)
        gcol = sb("gcol", [128, 1], F32)
        msc = sb("msc", [128, 1], F32)
        idb = sb("idb", [128, 128], BF16)
        psA = ps("psA", [128, 512])
        psB = ps("psB", [128, 512])
        psC = ps("psC", [128, 512])
        psD = ps("psD", [128, 512])
        psE = ps("psE", [128, 512])
        psFb = ps("psFb", [128, 512])
        ptr = ps("ptr", [128, 1024], BF16)
        BANKS = [psC, psD, psE, psFb]
        SEMS = {n: sem(n) for n in ["sv", "sp", "sa", "sz", "sl", "sa2", "sg"]}
        swg = [sem("swg0"), sem("swg1")]
        swu = [sem("swu0"), sem("swu1")]
        swd = [sem("swd0"), sem("swd1")]
        smp = sem("smp")
        smv = sem("smv")
        sma = sem("sma")
        sst = sem("sst")
        blk = es.enter_context(nc.Block())

        S = _Sched()

        jobs = [(idb, identb), (msk, amask), (sel, seltok), (msc, myscol),
                (lnbc, ln2bc)]
        for dst, src in jobs:
            S.add("sync_dma", lambda s_, dst=dst, src=src: s_.dma_start(out=dst[:], in_=src))
        S.add("sync_dma", lambda s_: s_.dma_start(
            out=qs.rearrange("p (dt s) -> p dt s", s=128),
            in_=qTs.rearrange("(dt p) s -> p dt s", p=128)))
        S.add("sync_dma", lambda s_: s_.dma_start(
            out=ks.rearrange("p (dt s) -> p dt s", s=SSEL),
            in_=kkT.rearrange("(dt p) s -> p dt s", p=128)))
        S.add("sync_dma", lambda s_: s_.dma_start(
            out=vss.rearrange("p (st dv) -> p st dv", dv=1024),
            in_=vfull.rearrange("(st p) dv -> p st dv", p=128)))

        for h in range(16):
            kv = h // 2
            S.add("tensor", lambda t, h=h, kv=kv: t.matmul(
                psA[:, 0:512], qs[:, h * 128:(h + 1) * 128],
                ks[:, kv * 1024:kv * 1024 + 512], start=True, stop=True),
                lag=({"vector": 8, "scalar": 1} if h else None))
            S.add("tensor", lambda t, h=h, kv=kv: t.matmul(
                psB[:, 0:512], qs[:, h * 128:(h + 1) * 128],
                ks[:, kv * 1024 + 512:(kv + 1) * 1024], start=True, stop=True),
                lag=({"vector": 7} if h else None))
            S.add("vector", lambda v: v.tensor_tensor(psA[:, 0:512], psA[:, 0:512],
                                                      msk[:, 0:512], ALU.add))
            S.add("vector", lambda v: v.tensor_tensor(psB[:, 0:512], psB[:, 0:512],
                                                      msk[:, 512:1024], ALU.add))
            S.add("vector", lambda v: v.reduce_max(sm2[:, 0:1], psA[:, 0:512], AX.X))
            S.add("vector", lambda v: v.reduce_max(sm2[:, 1:2], psB[:, 0:512], AX.X))
            S.add("vector", lambda v: v.tensor_tensor(sm2[:, 2:3], sm2[:, 0:1],
                                                      sm2[:, 1:2], ALU.max))
            S.add("vector", lambda v: v.tensor_scalar(sm2[:, 3:4], sm2[:, 2:3],
                                                      -ISQ, None, ALU.mult))
            S.add("scalar", lambda sc: sc.activation(Pe[:, 0:512], psA[:, 0:512],
                                                     AF.Exp, bias=sm2[:, 3:4], scale=ISQ))
            S.add("scalar", lambda sc: sc.activation(Pe[:, 512:1024], psB[:, 0:512],
                                                     AF.Exp, bias=sm2[:, 3:4], scale=ISQ))
            S.add("vector", lambda v: v.tensor_reduce(sm2[:, 4:5], Pe[:], AX.X, ALU.add))
            S.add("vector", lambda v: v.reciprocal(sm2[:, 5:6], sm2[:, 4:5]))
            S.add("vector", lambda v: v.tensor_scalar(Pb[:], Pe[:], sm2[:, 5:6],
                                                      None, ALU.mult))
            def trg(t):
                for k in range(8):
                    mm = t.transpose(ptr[:, k * 128:(k + 1) * 128],
                                     Pb[:, k * 128:(k + 1) * 128], idb[:])
                return mm
            S.add("tensor", trg)
            S.add("vector", lambda v: v.tensor_copy(PT[:], ptr[:]))
            def avg(t, kv=kv):
                for st_ in range(8):
                    mm = t.matmul(psC[:, 0:128],
                                  vss[:, st_ * 1024 + kv * 128:st_ * 1024 + (kv + 1) * 128],
                                  PT[:, st_ * 128:(st_ + 1) * 128],
                                  start=(st_ == 0), stop=(st_ == 7))
                return mm
            S.add("tensor", avg)
            S.add("vector", lambda v, h=h: v.tensor_copy(ctxT[:, h * 128:(h + 1) * 128],
                                                         psC[:, 0:128]))

        S.add("vector", lambda v: v.tensor_copy(self32[:], sel[:]))
        for dc in range(16):
            S.add("sync_dma", lambda s_, dc=dc: s_.dma_start(
                out=wsl[dc % 2][:], in_=fwob[ts(dc, 128), :]), lag={"tensor": 1})
            def womm(t, dc=dc):
                for ch in range(4):
                    mm = t.matmul(BANKS[ch][:, 0:512],
                                  ctxT[:, dc * 128:(dc + 1) * 128],
                                  wsl[dc % 2][:, ch * 512:(ch + 1) * 512],
                                  start=(dc == 0), stop=(dc == 15))
                return mm
            S.add("tensor", womm)
        for ch in range(4):
            S.add("vector", lambda v, ch=ch: v.scalar_tensor_tensor(
                out=x1tok[:, ch * 512:(ch + 1) * 512],
                in0=BANKS[ch][:, 0:512], scalar=1.0,
                in1=self32[:, ch * 512:(ch + 1) * 512], op0=ALU.mult, op1=ALU.add))

        S.add("vector", lambda v: v.tensor_tensor(sqtok[:], x1tok[:], x1tok[:], ALU.mult))
        S.add("vector", lambda v: v.tensor_reduce(sm2[:, 6:7], sqtok[:], AX.X, ALU.add))
        S.add("vector", lambda v: v.tensor_scalar(sm2[:, 7:8], sm2[:, 6:7],
                                                  1.0 / 2048.0, 1e-6, ALU.mult, ALU.add))
        S.add("vector", lambda v: v.reciprocal(sm2[:, 8:9], sm2[:, 7:8]))
        S.add("scalar", lambda sc: sc.activation(rstdc[:], sm2[:, 8:9], AF.Sqrt))
        S.add("vector", lambda v: v.tensor_scalar(sqtok[:], x1tok[:], rstdc[:, 0:1],
                                                  None, ALU.mult))
        S.add("vector", lambda v: v.tensor_tensor(h2k[:], sqtok[:], lnbc[:], ALU.mult))
        for g_ in range(2):
            def trg2(t, g_=g_):
                for k in range(8):
                    dt = g_ * 8 + k
                    mm = t.transpose(ptr[:, k * 128:(k + 1) * 128],
                                     h2k[:, dt * 128:(dt + 1) * 128], idb[:])
                return mm
            S.add("tensor", trg2)
            S.add("vector", lambda v, g_=g_: v.tensor_copy(
                h2T[:, g_ * 1024:(g_ + 1) * 1024], ptr[:]))
        S.add("scalar", lambda sc: sc.activation(gcol[:], msc[:], AF.Sigmoid))

        TOT = dict(S.cnt)

        def ld_gu(e, buf, wsrc, I, sems_):
            return e.dma_start(out=buf[I % 2][:], in_=wsrc[I]
                               ).then_inc(sems_[I % 2], 16)

        @blk.sync
        def _(s_):
            S.emit_one(s_, SEMS, ["sync_dma"])
            for I in range(64):
                if I >= 2:
                    s_.wait_ge(smp, 3 * (I - 1))
                ld_gu(s_, wgl, fwgb, I, swg)

        @blk.gpsimd
        def _(g):
            S.emit_one(g, SEMS, ["gpsimd", "gpsimd_dma"])
            for I in range(64):
                if I >= 2:
                    g.wait_ge(smp, 3 * (I - 1))
                g.dma_start(out=wdl[I % 2][:], in_=fwdb[ts(I, 128), :]
                            ).then_inc(swd[I % 2], 16)

        @blk.tensor
        def _(t):
            S.emit_one(t, SEMS, ["tensor"])
            for k in ["sv", "sa"]:
                t.wait_ge(SEMS[k], TOT[k])
            for I in range(64):
                for sw in (swg, swu, swd):
                    t.wait_ge(sw[I % 2], 16 * (I // 2 + 1))
                if I >= 1:
                    t.wait_ge(smv, 2 * I)
                for dm in range(16):
                    mm = t.matmul(psA[:, 0:128], wgl[I % 2][:, dm * 128:(dm + 1) * 128],
                                  h2T[:, dm * 128:(dm + 1) * 128],
                                  start=(dm == 0), stop=(dm == 15))
                mm.then_inc(smp, 1)
                for dm in range(16):
                    mm = t.matmul(psB[:, 0:128], wul[I % 2][:, dm * 128:(dm + 1) * 128],
                                  h2T[:, dm * 128:(dm + 1) * 128],
                                  start=(dm == 0), stop=(dm == 15))
                mm.then_inc(smp, 1)
                t.wait_ge(smv, 2 * I + 2)
                for ch in range(4):
                    mm = t.matmul(BANKS[ch][:, 0:512], mt[:],
                                  wdl[I % 2][:, ch * 512:(ch + 1) * 512],
                                  start=(I == 0), stop=(I == 63))
                mm.then_inc(smp, 1)

        @blk.vector
        def _(v):
            S.emit_one(v, SEMS, ["vector"])
            for k in ["sa", "sp"]:
                v.wait_ge(SEMS[k], TOT[k])
            for I in range(64):
                v.wait_ge(sma, I + 1)
                v.wait_ge(smp, 3 * I + 2)
                v.tensor_tensor(mt[:], psB[:, 0:128], g1[:], ALU.mult).then_inc(smv, 2)
            v.wait_ge(smp, 192)
            for ch in range(4):
                v.scalar_tensor_tensor(
                    out=sqtok[:, ch * 512:(ch + 1) * 512],
                    in0=BANKS[ch][:, 0:512], scalar=1.0,
                    in1=x1tok[:, ch * 512:(ch + 1) * 512], op0=ALU.mult, op1=ALU.add
                    ).then_inc(smv, 1)
            v.wait_ge(smv, 132)
            v.tensor_tensor(sqtok[:], sqtok[:], self32[:], ALU.subtract).then_inc(smv, 1)
            v.wait_ge(smv, 133)
            v.tensor_scalar(sqtok[:], sqtok[:], gcol[:, 0:1], None, ALU.mult
                            ).then_inc(smv, 1)
            v.wait_ge(smv, 134)
            v.tensor_tensor(gatok[:], sqtok[:], self32[:], ALU.add).then_inc(smv, 1)

        @blk.scalar
        def _(sc):
            S.emit_one(sc, SEMS, ["scalar", "scalar_dma"])
            sc.wait_ge(SEMS["sp"], TOT["sp"])
            ld_gu(sc, wul, fwub, 0, swu)
            ld_gu(sc, wul, fwub, 1, swu)
            for I in range(64):
                sc.wait_ge(smp, 3 * I + 1)
                sc.activation(g1[:], psA[:, 0:128], AF.Silu).then_inc(sma, 1)
                if I + 2 < 64:
                    sc.wait_ge(smp, 3 * I + 2)
                    ld_gu(sc, wul, fwub, I + 2, swu)
            sc.wait_ge(smv, 135)
            sc.dma_start(out=gato, in_=gatok[:]).then_inc(sst, 16)
            sc.wait_ge(sst, 16)
            sc.wait_ge(SEMS["sg"], S.cnt["sg"])
            sc.wait_ge(SEMS["sa2"], S.cnt["sa2"])

    return nc


def build_cast():
    nc = bass.Bass("TRN2", target_bir_lowering=False)
    cin = nc.dram_tensor("cin", [1088, 8192], F32, kind="ExternalInput").ap()
    cout = nc.dram_tensor("cout", [1088, 8192], BF16, kind="ExternalOutput").ap()
    es = ExitStack()
    with es:
        scp = es.enter_context(nc.semaphore("scp"))
        blk = es.enter_context(nc.Block())

        @blk.gpsimd
        def _(g):
            g.dma_start(out=cout, in_=cin).then_inc(scp, 16)
            g.wait_ge(scp, 16)
    return nc


# version 06 (final)
# speedup vs baseline: 1.8919x; 1.0598x over previous
import numpy as np
import ml_dtypes
from contextlib import ExitStack
import concourse.bass as bass
import concourse.mybir as mybir
from concourse.bass_utils import run_bass_kernel_spmd

F32 = mybir.dt.float32
BF16 = mybir.dt.bfloat16
U32 = mybir.dt.uint32
I32 = mybir.dt.int32
AF = mybir.ActivationFunctionType
ALU = mybir.AluOpType
AX = mybir.AxisListType
BF = ml_dtypes.bfloat16
ts = bass.ts
ds = bass.ds

T = 8192
D = 2048
HD = 128
SSEL = 1024
IMLP = 8192
NEG = -1.0e30
RTH = 10000.0
ISQ = float(1.0 / np.sqrt(np.float32(128.0)))
NCORE = 8
TWOPI = float(2.0 * np.pi)

_CACHE = {}


def _cached(key, fn):
    if key not in _CACHE:
        _CACHE[key] = fn()
    return _CACHE[key]


def build_ln(NT=1024):
    nc = bass.Bass("TRN2", target_bir_lowering=False)
    nt = NT // 128
    xs = nc.dram_tensor("xs", [NT, D], F32, kind="ExternalInput").ap()
    wr = nc.dram_tensor("wr", [1, D], F32, kind="ExternalInput").ap()
    onesf = nc.dram_tensor("onesf", [1, 128], F32, kind="ExternalInput").ap()
    identb = nc.dram_tensor("identb", [128, 128], BF16, kind="ExternalInput").ap()
    lntok = nc.dram_tensor("lntok", [NT, D], BF16, kind="ExternalOutput").ap()
    lnT = nc.dram_tensor("lnT", [16 * 128, NT], BF16, kind="ExternalOutput").ap()
    sco = nc.dram_tensor("sco", [128, nt], F32, kind="ExternalOutput").ap()
    es = ExitStack()
    sb = lambda n, shp, dt: es.enter_context(nc.sbuf_tensor(n, shp, dt))
    ps = lambda n, shp, dt=F32: es.enter_context(nc.psum_tensor(n, shp, dt))
    sem = lambda n: es.enter_context(nc.semaphore(n))
    with es:
        xt = sb("xt", [128, D], F32)
        xt2 = sb("xt2", [128, D], F32)
        tok = sb("tok", [128, D], BF16)
        tok2 = sb("tok2", [128, D], BF16)
        lnb = sb("lnb", [128, 2048], BF16)
        lnb2 = sb("lnb2", [128, 2048], BF16)
        sm = sb("sm", [128, 64], F32)
        wrbc = sb("wrbc", [128, D], F32)
        wrsum = sb("wrsum", [128, 1], F32)
        wrrow = sb("wrrow", [1, D], F32)
        onesfs = sb("onesfs", [1, 128], F32)
        idents = sb("idents", [128, 128], BF16)
        sct = sb("sct", [128, D], F32)
        psa = ps("psa", [128, 512])
        ptr = ps("ptr", [128, 1024], BF16)
        ptr2 = ps("ptr2", [128, 1024], BF16)
        slc = sem("slc"); sl0 = sem("sl0"); sl1 = sem("sl1")
        sv = sem("sv"); sa = sem("sa"); sp = sem("sp"); se = sem("se")
        st0 = sem("st0"); st1 = sem("st1"); stf = sem("stf")
        blk = es.enter_context(nc.Block())
        slp = [sl0, sl1]
        stp = [st0, st1]
        xb = [xt, xt2]
        tb = [tok, tok2]
        lb = [lnb, lnb2]
        NV = 9

        @blk.sync
        def _(s_):
            s_.dma_start(out=wrrow[:], in_=wr).then_inc(slc, 16)
            s_.dma_start(out=onesfs[:], in_=onesf).then_inc(slc, 16)
            s_.dma_start(out=idents[:], in_=identb).then_inc(slc, 16)
            for i in range(nt):
                if i >= 2:
                    s_.wait_ge(stp[i % 2], 32 * (i // 2))
                s_.dma_start(out=xb[i % 2][:], in_=xs[ts(i, 128), :]).then_inc(slp[i % 2], 16)

        @blk.tensor
        def _(t):
            t.wait_ge(slc, 48)
            for j in range(4):
                if j:
                    t.wait_ge(sv, j)
                t.matmul(psa[:, 0:512], onesfs[:, :], wrrow[:, j * 512:(j + 1) * 512],
                         start=True, stop=True).then_inc(sp, 1)
            for i in range(nt):
                t.wait_ge(sv, 6 + i * NV + 5)
                for g in range(2):
                    if i * 2 + g >= 2:
                        t.wait_ge(se, i * 2 + g - 1)
                    bank = ptr if g == 0 else ptr2
                    for k in range(8):
                        dt = g * 8 + k
                        mm = t.transpose(bank[:, k * 128:(k + 1) * 128],
                                         tb[i % 2][:, dt * 128:(dt + 1) * 128], idents[:])
                    mm.then_inc(sp, 1)

        @blk.vector
        def _(v):
            for j in range(4):
                v.wait_ge(sp, j + 1)
                v.tensor_copy(wrbc[:, j * 512:(j + 1) * 512], psa[:, 0:512]).then_inc(sv, 1)
            v.wait_ge(sv, 4)
            v.tensor_reduce(wrsum[:], wrbc[:], AX.X, ALU.add).then_inc(sv, 2)
            for i in range(nt):
                base = 6 + i * NV
                x = xb[i % 2]
                v.wait_ge(slp[i % 2], 16 * (i // 2 + 1))
                for q in range(4):
                    mm = v.bn_stats(sm[:, 6 * q:6 * q + 6], x[:, 512 * q:512 * (q + 1)])
                mm.then_inc(sv, 1)
                v.wait_ge(sv, base + 1)
                v.bn_aggr(sm[:, 24:26], sm[:, 0:24]).then_inc(sv, 1)
                v.wait_ge(sv, base + 2)
                v.tensor_scalar(sm[:, 26:27], sm[:, 25:26], 1e-5, None, ALU.add
                                ).then_inc(sv, 1)
                v.wait_ge(sv, base + 3)
                v.reciprocal(sm[:, 27:28], sm[:, 26:27]).then_inc(sv, 1)
                v.wait_ge(sa, i + 1)
                if i >= 2:
                    v.wait_ge(stp[i % 2], 32 * (i // 2))
                v.tensor_scalar(tb[i % 2][:], x[:], sm[:, 24:25], sm[:, 28:29],
                                ALU.subtract, ALU.mult)
                v.tensor_tensor(sct[:], x[:], wrbc[:], ALU.mult).then_inc(sv, 1)
                v.wait_ge(sv, base + 5)
                v.tensor_reduce(sm[:, 30:31], sct[:], AX.X, ALU.add).then_inc(sv, 1)
                v.wait_ge(sv, base + 6)
                v.tensor_tensor(sm[:, 32:33], sm[:, 24:25], wrsum[:], ALU.mult
                                ).then_inc(sv, 1)
                v.wait_ge(sv, base + 7)
                v.tensor_tensor(sm[:, 33:34], sm[:, 30:31], sm[:, 32:33], ALU.subtract
                                ).then_inc(sv, 1)
                v.wait_ge(sv, base + 8)
                v.tensor_tensor(sm[:, 40 + i:41 + i], sm[:, 33:34],
                                sm[:, 28:29], ALU.mult).then_inc(sv, 1)
                for g in range(2):
                    v.wait_ge(sp, 4 + i * 2 + g + 1)
                    bank = ptr if g == 0 else ptr2
                    v.tensor_copy(
                        lb[i % 2].rearrange("p (dt q) -> p dt q", q=128)[
                            :, g * 8:(g + 1) * 8, :],
                        bank[:].rearrange("p (k q) -> p k q", q=128)).then_inc(se, 1)

        @blk.scalar
        def _(sc):
            for i in range(nt):
                base = 6 + i * NV
                sc.wait_ge(sv, base + 4)
                sc.activation(sm[:, 28:29], sm[:, 27:28], AF.Sqrt).then_inc(sa, 1)
                sc.wait_ge(sv, base + 5)
                sc.dma_start(out=lntok[ts(i, 128), :], in_=tb[i % 2][:]
                             ).then_inc(stp[i % 2], 16)
                sc.wait_ge(se, (i + 1) * 2)
                sc.dma_start(out=lnT.rearrange("(dt p) n -> p dt n", p=128)[:, :, ts(i, 128)],
                             in_=lb[i % 2].rearrange("p (dt q) -> p dt q", q=128)
                             ).then_inc(stp[i % 2], 16)
            sc.wait_ge(sv, 6 + nt * NV)
            sc.dma_start(out=sco[:, :], in_=sm[:, 40:40 + nt]).then_inc(stf, 16)
            sc.wait_ge(st0, 32 * ((nt + 1) // 2))
            sc.wait_ge(st1, 32 * (nt // 2))
            sc.wait_ge(stf, 16)

    return nc


def build_att(TT, nq, nkv, hbias):
    nc = bass.Bass("TRN2", target_bir_lowering=False)
    nch = TT // 512
    ntt = TT // 128
    kvmap = [h * nkv // nq for h in range(nq)]
    nio = nq + 2 * nkv

    lnT = nc.dram_tensor("lnT", [16, 128, TT], BF16, kind="ExternalInput").ap()
    wqkv = nc.dram_tensor("wqkv", [D, nio * 128], F32, kind="ExternalInput").ap()
    if hbias:
        bqc = nc.dram_tensor("bqc", [128, nq], F32, kind="ExternalInput").ap()
        bkc = nc.dram_tensor("bkc", [128, nkv], F32, kind="ExternalInput").ap()
        bvc = nc.dram_tensor("bvc", [128, nkv], F32, kind="ExternalInput").ap()
    identb = nc.dram_tensor("identb", [128, 128], BF16, kind="ExternalInput").ap()
    onesb = nc.dram_tensor("onesb", [128, 1], BF16, kind="ExternalInput").ap()
    onesf = nc.dram_tensor("onesf", [1, 128], F32, kind="ExternalInput").ap()
    dmask = nc.dram_tensor("dmask", [4, 128, 512], F32, kind="ExternalInput").ap()
    ctxo = nc.dram_tensor("ctxo", [nq * 128, TT], BF16, kind="ExternalOutput").ap()

    wqkvb = nc.dram_tensor("wqkvb", [D, nio * 128], BF16).ap()

    def nkj(qc):
        return 4 * (qc + 1)

    def cumS(h, qc):
        tot = 0
        for hh in range(nq):
            for qq in range(nch):
                if (hh, qq) < (h, qc):
                    tot += nkj(qq)
        return tot

    es = ExitStack()
    sb = lambda n, shp, dt: es.enter_context(nc.sbuf_tensor(n, shp, dt))
    ps = lambda n, shp, dt=F32: es.enter_context(nc.psum_tensor(n, shp, dt))
    sem = lambda n: es.enter_context(nc.semaphore(n))
    with es:
        lnc = sb("lnc", [128, 16 * 512], BF16)
        lnc2 = sb("lnc2", [128, 16 * 512], BF16)
        wqs = sb("wqs", [128, 16 * nio * 128], BF16)
        qT = sb("qT", [128, nq * TT], BF16)
        kT = sb("kT", [128, nkv * TT], BF16)
        vT = sb("vT", [128, nkv * TT], BF16)
        vtok = sb("vtok", [128, nkv * TT], BF16)
        prb0 = sb("prb0", [128, 512], BF16)
        prb1 = sb("prb1", [128, 512], BF16)
        rrow = sb("rrow", [1, 512], F32)
        rbc = sb("rbc", [128, 512], F32)
        ctxc = sb("ctxc", [128, nq * 512], BF16)
        idents = sb("idents", [128, 128], BF16)
        onec = sb("onec", [128, 1], BF16)
        onesfs = sb("onesfs", [1, 128], F32)
        masks = sb("masks", [128, 2048], F32)
        bqs = sb("bqs", [128, max(nq, 1)], F32)
        bks = sb("bks", [128, max(nkv, 1)], F32)
        bvs = sb("bvs", [128, max(nkv, 1)], F32)
        psA = ps("psA", [128, 512])
        psB = ps("psB", [128, 512])
        psCp = [ps("psC0", [128, 512]), ps("psC1", [128, 512])]
        psDp = [ps("psD0", [128, 512]), ps("psD1", [128, 512])]
        psE = ps("psE", [128, 512])
        ptr = ps("ptr", [128, 1024], BF16)
        sg = sem("sg"); slw = sem("slw"); sl0 = sem("sl0"); sl1 = sem("sl1")
        s_qp = sem("s_qp"); s_qv = sem("s_qv")
        s_tr = sem("s_tr"); s_te = sem("s_te"); s_s = sem("s_s"); s_d = sem("s_d")
        s_e = sem("s_e"); s_av = sem("s_av"); s_r = sem("s_r"); s_b = sem("s_b")
        s_v2 = sem("s_v2"); st0 = sem("st0"); st1 = sem("st1")
        blk = es.enter_context(nc.Block())
        slp = [sl0, sl1]
        stp = [st0, st1]
        lncb = [lnc, lnc2]
        prb = [prb0, prb1]
        NLD = 7 + (3 if hbias else 0) + 16

        @blk.gpsimd
        def _(g):
            g.dma_start(out=wqkvb, in_=wqkv).then_inc(sg, 16)

        @blk.sync
        def _(s_):
            s_.dma_start(out=idents[:], in_=identb).then_inc(slw, 16)
            s_.dma_start(out=onec[:], in_=onesb).then_inc(slw, 16)
            s_.dma_start(out=onesfs[:], in_=onesf).then_inc(slw, 16)
            for v_ in range(4):
                s_.dma_start(out=masks[:, v_ * 512:(v_ + 1) * 512], in_=dmask[v_, :, :]
                             ).then_inc(slw, 16)
            if hbias:
                s_.dma_start(out=bqs[:], in_=bqc).then_inc(slw, 16)
                s_.dma_start(out=bks[:], in_=bkc).then_inc(slw, 16)
                s_.dma_start(out=bvs[:], in_=bvc).then_inc(slw, 16)
            s_.wait_ge(sg, 16)
            for dm in range(16):
                s_.dma_start(out=wqs[:, dm * nio * 128:(dm + 1) * nio * 128],
                             in_=wqkvb[ts(dm, 128), :]).then_inc(slw, 16)
            for c in range(nch):
                if c >= 2:
                    s_.wait_ge(s_qp, (c - 1) * nio)
                s_.dma_start(out=lncb[c % 2].rearrange("p (dt q) -> p dt q", q=512),
                             in_=lnT[:, :, ts(c, 512)].rearrange("dt p q -> p dt q")
                             ).then_inc(slp[c % 2], 16)

        @blk.tensor
        def _(t):
            def warmup(nmm):
                for _ in range(nmm):
                    t.matmul(psE[:, 0:512], idents[:, :], qT[:, 0:512],
                             start=True, stop=True)
            t.wait_ge(slw, NLD * 16)
            for c in range(nch):
                t.wait_ge(slp[c % 2], 16 * (c // 2 + 1))
                for io in range(nio):
                    u = c * nio + io
                    if u >= 2:
                        t.wait_ge(s_qv, u - 1)
                    bank = psA if io % 2 == 0 else psB
                    for dm in range(16):
                        mm = t.matmul(bank[:, 0:512],
                                      wqs[:, (dm * nio + io) * 128:(dm * nio + io + 1) * 128],
                                      lncb[c % 2][:, dm * 512:(dm + 1) * 512],
                                      start=(dm == 0), stop=(dm == 15))
                        if dm == 15:
                            mm.then_inc(s_qp, 1)
            t.wait_ge(s_qv, nch * nio)
            ngr = (nkv * ntt) // 8
            for g in range(ngr):
                if g >= 1:
                    t.wait_ge(s_te, g)
                for k in range(8):
                    i = g * 8 + k
                    mm = t.transpose(ptr[:, k * 128:(k + 1) * 128],
                                     vT[:, i * 128:(i + 1) * 128], idents[:])
                mm.then_inc(s_tr, 1)
            t.wait_ge(s_te, ngr)
            warmup(14)

            def bcast(np_):
                if np_ >= 1:
                    t.wait_ge(s_v2, 2 * np_ - 1)
                t.wait_ge(s_r, np_ + 1)
                t.matmul(psE[:, 0:512], onesfs[:, :], rrow[:, 0:512],
                         start=True, stop=True).then_inc(s_b, 1)
            units = []
            for h in range(nq):
                for qc in range(nch):
                    for kj in range(nkj(qc)):
                        units.append((h, qc, kj, h * nch + qc, nkj(qc)))

            def emitS(j):
                h, qc, kj, npair, nk = units[j]
                kv = kvmap[h]
                bank = psA if kj % 2 == 0 else psB
                if j >= 2:
                    t.wait_ge(s_e, j - 1)
                t.matmul(bank[:, 0:512],
                         kT[:, kv * TT + kj * 128:kv * TT + (kj + 1) * 128],
                         qT[:, h * TT + qc * 512:h * TT + (qc + 1) * 512],
                         start=True, stop=True).then_inc(s_s, 1)

            emitS(0)
            for i, (h, qc, kj, npair, nk) in enumerate(units):
                if i + 1 < len(units):
                    emitS(i + 1)
                kv = kvmap[h]
                t.wait_ge(s_e, i + 1)
                if kj == 0 and npair >= 2:
                    t.wait_ge(s_v2, 2 * npair - 2)
                t.matmul(psCp[npair % 2][:, 0:512],
                         vtok[:, kv * TT + kj * 128:kv * TT + (kj + 1) * 128],
                         prb[kj % 2][:], start=(kj == 0), stop=(kj == nk - 1))
                if kj == 0 and npair >= 2:
                    t.wait_ge(s_r, npair - 1)
                t.matmul(psDp[npair % 2][0:1, 0:512], onec[:], prb[kj % 2][:],
                         start=(kj == 0), stop=(kj == nk - 1)).then_inc(s_av, 1)
                if kj == 2 and npair >= 1:
                    bcast(npair - 1)

            bcast(nq * nch - 1)

        @blk.vector
        def _(v):
            for c in range(nch):
                for io in range(nio):
                    u = c * nio + io
                    v.wait_ge(s_qp, u + 1)
                    bank = psA if io % 2 == 0 else psB
                    if io < nq:
                        dst = qT[:, io * TT + c * 512:io * TT + (c + 1) * 512]
                        bias = bqs[:, io:io + 1] if hbias else None
                    elif io < nq + nkv:
                        kv = io - nq
                        dst = kT[:, kv * TT + c * 512:kv * TT + (c + 1) * 512]
                        bias = bks[:, kv:kv + 1] if hbias else None
                    else:
                        kv = io - nq - nkv
                        dst = vT[:, kv * TT + c * 512:kv * TT + (c + 1) * 512]
                        bias = bvs[:, kv:kv + 1] if hbias else None
                    if bias is not None:
                        v.tensor_scalar(dst, bank[:, 0:512], bias, None, ALU.add
                                        ).then_inc(s_qv, 1)
                    else:
                        v.tensor_copy(dst, bank[:, 0:512]).then_inc(s_qv, 1)
            for g in range((nkv * ntt) // 8):
                v.wait_ge(s_tr, g + 1)
                v.tensor_copy(vtok[:, g * 1024:(g + 1) * 1024], ptr[:]).then_inc(s_te, 1)
            def epi(np_, cs_, nk_):
                v.wait_ge(s_av, cs_ + nk_)
                v.reciprocal(rrow[:, 0:512], psDp[np_ % 2][0:1, 0:512]
                             ).then_inc(s_r, 1)
                v.wait_ge(s_b, np_ + 1)
                v.tensor_copy(rbc[:], psE[:, 0:512]).then_inc(s_v2, 1)
                v.wait_ge(s_v2, 2 * np_ + 1)
                if np_ >= 2:
                    v.wait_ge(stp[np_ % 2], 16 * (np_ // 2))
                v.tensor_tensor(ctxc[:, (np_ % 2) * 512:(np_ % 2 + 1) * 512],
                                psCp[np_ % 2][:, 0:512], rbc[:], ALU.mult
                                ).then_inc(s_v2, 1)
            prev = None
            for h in range(nq):
                for qc in range(nch):
                    npair = h * nch + qc
                    cs = cumS(h, qc)
                    nk = nkj(qc)
                    done = False
                    for kj in range(4 * qc, nk):
                        if kj >= 3 and not done and prev is not None:
                            epi(*prev)
                            done = True
                        bank = psA if kj % 2 == 0 else psB
                        vm = kj - 4 * qc
                        v.wait_ge(s_s, cs + kj + 1)
                        v.tensor_tensor(bank[:, 0:512], bank[:, 0:512],
                                        masks[:, vm * 512:(vm + 1) * 512], ALU.add
                                        ).then_inc(s_d, 1)
                    if not done and prev is not None:
                        epi(*prev)
                    prev = (npair, cs, nk)
            epi(*prev)

        @blk.scalar
        def _(sc):
            def store(np_):
                h_, qc_ = np_ // nch, np_ % nch
                sc.wait_ge(s_v2, 2 * np_ + 2)
                sc.dma_start(out=ctxo[ts(h_, 128), ts(qc_, 512)],
                             in_=ctxc[:, (np_ % 2) * 512:(np_ % 2 + 1) * 512]
                             ).then_inc(stp[np_ % 2], 16)
            pv = None
            for h in range(nq):
                for qc in range(nch):
                    npair = h * nch + qc
                    cs = cumS(h, qc)
                    nk = nkj(qc)
                    for kj in range(nk):
                        diag = kj >= 4 * qc
                        bank = psA if kj % 2 == 0 else psB
                        if diag:
                            sc.wait_ge(s_d, 4 * npair + (kj - 4 * qc) + 1)
                        else:
                            sc.wait_ge(s_s, cs + kj + 1)
                        if cs + kj >= 2:
                            sc.wait_ge(s_av, cs + kj - 1)
                        sc.activation(prb[kj % 2][:], bank[:, 0:512], AF.Exp,
                                      scale=ISQ).then_inc(s_e, 1)
                    if pv is not None:
                        store(pv)
                    pv = npair
            store(pv)
            npt = nq * nch
            sc.wait_ge(st0, 16 * ((npt + 1) // 2))
            sc.wait_ge(st1, 16 * (npt // 2))

    return nc


def _consts():
    C = {}
    C["identb"] = np.eye(128).astype(BF)
    C["identf"] = np.eye(128, dtype=np.float32)
    C["onesf"] = np.ones((1, 128), np.float32)
    C["onesb"] = np.ones((128, 1), BF)
    C["onescf"] = np.ones((128, 1), np.float32)
    dm = np.zeros((4, 128, 512), np.float32)
    for v in range(4):
        kj = v * 128 + np.arange(128)[:, None]
        qi = np.arange(512)[None, :]
        dm[v] = np.where(kj > qi, NEG, 0.0)
    C["dmask"] = dm
    rot = np.zeros((128, 128), np.float32)
    for i in range(64):
        rot[64 + i, i] = -1.0
        rot[i, 64 + i] = 1.0
    C["rotb"] = rot.astype(BF)
    C["iota1k"] = np.broadcast_to(np.arange(1024, dtype=np.float32), (128, 1024)).copy()
    C["trif"] = (np.arange(128)[:, None] <= np.arange(128)[None, :]).astype(np.float32)
    C["tcolf"] = np.arange(128, dtype=np.float32)[:, None]
    C["invfs"] = np.concatenate(
        [1.0 / RTH ** (np.arange(0, 128, 2) / 128.0)] * 2)[None, :].astype(np.float32)
    return C


def _run(nc, maps):
    return run_bass_kernel_spmd(nc, maps, core_ids=list(range(NCORE)))


def kernel(**inputs):
    f = lambda k: np.ascontiguousarray(np.asarray(inputs[k], np.float32))
    C = _consts()
    x0 = f("L0_attn_a_x").reshape(T, D)
    wrr = f("L0_ffn_w_router").reshape(1, D)
    p_ln = _cached("ln", lambda: build_ln(1024))
    p_atta = _cached("atta", lambda: build_att(T, 2, 1, False))
    p_attb = _cached("attb", lambda: build_att(T, 2, 2, True))
    p_proj = _cached("proj", lambda: build_proj(1024))
    p_fsel = _cached("fsel", lambda: build_fsel())
    p_fmain = _cached("fmain", lambda: build_fmain())
    p_scat = _cached("scat", lambda: build_scat(1024))
    p_cast = _cached("cast", lambda: build_cast())

    def L_ln(x):
        maps = [{"xs": np.ascontiguousarray(x[c * 1024:(c + 1) * 1024]), "wr": wrr,
                 "onesf": C["onesf"], "identb": C["identb"]} for c in range(NCORE)]
        r = _run(p_ln, maps)
        lntok = np.concatenate([np.asarray(r.results[c]["lntok"]) for c in range(NCORE)], 0)
        lnT = np.concatenate([np.asarray(r.results[c]["lnT"]).reshape(16, 128, 1024)
                              for c in range(NCORE)], 2)
        sco = np.concatenate([np.asarray(r.results[c]["sco"]) for c in range(NCORE)], 1)
        return lntok, lnT, sco

    def L_att(prog, lnT, mk):
        maps = []
        for c in range(NCORE):
            m = mk(c)
            m.update({"lnT": lnT, "identb": C["identb"], "onesb": C["onesb"],
                      "onesf": C["onesf"], "dmask": C["dmask"]})
            maps.append(m)
        r = _run(prog, maps)
        return np.concatenate([np.asarray(r.results[c]["ctxo"]) for c in range(NCORE)], 0)

    def L_proj(ctxT, xprev, W, brow):
        onesbr = np.ones((1, 128), BF)
        maps = [{"cT": np.ascontiguousarray(ctxT[:, c * 1024:(c + 1) * 1024]
                                            ).reshape(16, 128, 1024),
                 "xprev": np.ascontiguousarray(xprev[c * 1024:(c + 1) * 1024]),
                 "Wb": W, "brow": brow, "onesbr": onesbr} for c in range(NCORE)]
        r = _run(p_proj, maps)
        return np.concatenate([np.asarray(r.results[c]["xnew"]) for c in range(NCORE)], 0)

    fwq, fwk, fwv = f("L0_ffn_wq"), f("L0_ffn_wk"), f("L0_ffn_wv")
    fw3 = np.ascontiguousarray(np.concatenate([fwq, fwk, fwv], 1))
    biasc = np.ascontiguousarray(np.concatenate(
        [f("L0_ffn_bq").ravel(), f("L0_ffn_bk").ravel(), f("L0_ffn_bv").ravel()]
        ).reshape(32, 128).T)
    ln1c = np.ascontiguousarray(f("L0_ffn_ln1_w").reshape(16, 128).T)
    ln2bc = np.broadcast_to(f("L0_ffn_ln2_w").reshape(1, D), (128, D)).copy()
    fwg, fwu = f("L0_ffn_w_gate"), f("L0_ffn_w_up")
    fwd, fwo = f("L0_ffn_w_down"), f("L0_ffn_wo")
    woa, wpb = f("L0_attn_a_Wo"), f("L0_attn_b_w_proj")
    allf = np.concatenate([fwg.ravel(), fwu.ravel(), fwd.ravel(), fw3.ravel(),
                           fwo.ravel(), woa.ravel(), wpb.ravel()]).reshape(8, 1088, 8192)
    rc = _run(p_cast, [{"cin": np.ascontiguousarray(allf[c])} for c in range(NCORE)])
    allb = np.concatenate([np.asarray(rc.results[c]["cout"]).ravel()
                           for c in range(NCORE)])
    E = 2048 * 8192
    tile = lambda w: np.ascontiguousarray(
        w.reshape(16, 128, 64, 128).transpose(2, 1, 0, 3).reshape(64, 128, D))
    fwgb, fwub = tile(allb[0:E]), tile(allb[E:2 * E])
    fwdb = allb[2 * E:3 * E].reshape(IMLP, D)
    fw3b = allb[3 * E:3 * E + D * 4096].reshape(D, 4096)
    o = 3 * E + D * 4096
    fwob = allb[o:o + D * D].reshape(D, D)
    woab = allb[o + D * D:o + 2 * D * D].reshape(D, D)
    wpbb = allb[o + 2 * D * D:o + 3 * D * D].reshape(D, D)
    amasks = [np.where(np.arange(SSEL)[None, :] <= (128 * c + np.arange(128))[:, None],
                       0.0, NEG).astype(np.float32) for c in range(NCORE)]

    def ffn(xin, lntok, sco):
        maps = [{"sco": sco, "lnfull": lntok, "fw3b": fw3b, "ln1c": ln1c,
                 "biasc": biasc, "sbase": np.array([[c * 128]], np.uint32),
                 "invfs": C["invfs"], "identf": C["identf"], "identb": C["identb"],
                 "rotb": C["rotb"], "iota1k": C["iota1k"], "trif": C["trif"],
                 "tcolf": C["tcolf"], "onescf": C["onescf"], "onesf": C["onesf"]}
                for c in range(NCORE)]
        rs = _run(p_fsel, maps)
        kkTf = np.concatenate([np.asarray(rs.results[c]["kkTs"]) for c in range(NCORE)], 1)
        vfull = np.concatenate([np.asarray(rs.results[c]["vso"]) for c in range(NCORE)], 0)
        posco = np.asarray(rs.results[0]["posco"])
        acolo = np.asarray(rs.results[0]["acolo"])
        maps = [{"qTs": np.asarray(rs.results[c]["qTs"]), "kkT": kkTf, "vfull": vfull,
                 "seltok": np.asarray(rs.results[c]["selo"]),
                 "myscol": np.ascontiguousarray(posco[1, c * 128:(c + 1) * 128]
                                                ).reshape(128, 1).astype(np.float32),
                 "amask": amasks[c], "fwob": fwob, "fwgb": fwgb, "fwub": fwub,
                 "fwdb": fwdb, "ln2bc": ln2bc, "identb": C["identb"]}
                for c in range(NCORE)]
        rm = _run(p_fmain, maps)
        gated = np.concatenate([np.asarray(rm.results[c]["gato"]) for c in range(NCORE)], 0)
        maps = [{"xprev": np.ascontiguousarray(xin[c * 1024:(c + 1) * 1024]),
                 "lntk": np.ascontiguousarray(lntok[c * 1024:(c + 1) * 1024]),
                 "gated": gated,
                 "acol": np.ascontiguousarray(acolo[:, 8 * c:8 * c + 8])}
                for c in range(NCORE)]
        rr = _run(p_scat, maps)
        return np.concatenate([np.asarray(rr.results[c]["xnew"]) for c in range(NCORE)], 0)

    _, lnT0, _ = L_ln(x0)
    wq, wk, wv = f("L0_attn_a_Wq"), f("L0_attn_a_Wk"), f("L0_attn_a_Wv")
    ctxa = L_att(p_atta, lnT0, lambda c: {"wqkv": np.ascontiguousarray(np.concatenate(
        [wq[:, 256 * c:256 * (c + 1)], wk[:, 128 * (c // 2):128 * (c // 2 + 1)],
         wv[:, 128 * (c // 2):128 * (c // 2 + 1)]], 1))})
    x1 = L_proj(ctxa, x0, woab, f("L0_attn_a_bo").reshape(1, D))

    lntok1, _, sco1 = L_ln(x1)
    x2 = ffn(x1, lntok1, sco1)

    _, lnT2, _ = L_ln(x2)
    wab, bab = f("L0_attn_b_w_attn"), f("L0_attn_b_b_attn").ravel()

    def mk_b(c):
        sl = lambda o: wab[:, o + 256 * c:o + 256 * (c + 1)]
        bsl = lambda o: np.ascontiguousarray(
            bab[o + 256 * c:o + 256 * (c + 1)].reshape(2, 128).T)
        return {"wqkv": np.ascontiguousarray(np.concatenate(
                    [sl(0), sl(2048), sl(4096)], 1)),
                "bqc": bsl(0), "bkc": bsl(2048), "bvc": bsl(4096)}
    ctxb = L_att(p_attb, lnT2, mk_b)
    x3 = L_proj(ctxb, x2, wpbb, f("L0_attn_b_b_proj").reshape(1, D))

    lntok3, _, sco3 = L_ln(x3)
    x4 = ffn(x3, lntok3, sco3)
    return np.ascontiguousarray(x4.reshape(1, T, D).astype(np.float32))


def build_proj(NT=1024):
    nc = bass.Bass("TRN2", target_bir_lowering=False)
    nt = NT // 128
    cT = nc.dram_tensor("cT", [16, 128, NT], BF16, kind="ExternalInput").ap()
    xprev = nc.dram_tensor("xprev", [NT, D], F32, kind="ExternalInput").ap()
    brow = nc.dram_tensor("brow", [1, D], F32, kind="ExternalInput").ap()
    onesbr = nc.dram_tensor("onesbr", [1, 128], BF16, kind="ExternalInput").ap()
    xnew = nc.dram_tensor("xnew", [NT, D], F32, kind="ExternalOutput").ap()
    Wb = nc.dram_tensor("Wb", [D, D], BF16, kind="ExternalInput").ap()
    bb = nc.dram_tensor("bb", [1, D], BF16).ap()
    es = ExitStack()
    sb = lambda n, shp, dt: es.enter_context(nc.sbuf_tensor(n, shp, dt))
    ps = lambda n, shp, dt=F32: es.enter_context(nc.psum_tensor(n, shp, dt))
    sem = lambda n: es.enter_context(nc.semaphore(n))
    with es:
        cts = sb("cts", [128, 16 * NT], BF16)
        ws = sb("ws", [128, 16 * 2048], BF16)
        xt = sb("xt", [128, D], F32)
        xt2 = sb("xt2", [128, D], F32)
        xo = sb("xo", [128, D], F32)
        xo2 = sb("xo2", [128, D], F32)
        bbs = sb("bbs", [1, D], BF16)
        onbr = sb("onbr", [1, 128], BF16)
        psA = ps("psA", [128, 512])
        psB = ps("psB", [128, 512])
        sg = sem("sg"); slw = sem("slw"); sl0 = sem("sl0"); sl1 = sem("sl1")
        sp = sem("sp"); sv = sem("sv"); st0 = sem("st0"); st1 = sem("st1")
        sw2 = sem("sw2")
        blk = es.enter_context(nc.Block())
        slp = [sl0, sl1]
        stp = [st0, st1]
        xb = [xt, xt2]
        xob = [xo, xo2]

        @blk.gpsimd
        def _(g):
            g.dma_start(out=bb, in_=brow).then_inc(sg, 16)

        @blk.sync
        def _(s_):
            s_.dma_start(out=onbr[:], in_=onesbr).then_inc(slw, 16)
            s_.dma_start(out=cts.rearrange("p (dt n) -> p dt n", n=NT),
                         in_=cT.rearrange("dt p n -> p dt n")).then_inc(slw, 16)
            s_.wait_ge(sg, 16)
            s_.dma_start(out=bbs[:], in_=bb).then_inc(slw, 16)
            for dm in range(0, 16, 2):
                s_.dma_start(out=ws[:, dm * 2048:(dm + 1) * 2048],
                             in_=Wb[ts(dm, 128), :]).then_inc(slw, 16)
            for i in range(nt):
                if i >= 2:
                    s_.wait_ge(stp[i % 2], 16 * (i // 2))
                s_.dma_start(out=xb[i % 2][:], in_=xprev[ts(i, 128), :]
                             ).then_inc(slp[i % 2], 16)

        @blk.tensor
        def _(t):
            t.wait_ge(slw, 11 * 16)
            t.wait_ge(sw2, 8 * 16)
            for i in range(nt):
                for ch in range(4):
                    u = i * 4 + ch
                    if u >= 2:
                        t.wait_ge(sv, u - 1)
                    bank = psA if u % 2 == 0 else psB
                    for dm in range(16):
                        t.matmul(bank[:, 0:512],
                                 cts[:, dm * NT + i * 128:dm * NT + (i + 1) * 128],
                                 ws[:, dm * 2048 + ch * 512:dm * 2048 + (ch + 1) * 512],
                                 start=(dm == 0), stop=False)
                    t.matmul(bank[:, 0:512], onbr[:, :],
                             bbs[:, ch * 512:(ch + 1) * 512],
                             start=False, stop=True).then_inc(sp, 1)

        @blk.vector
        def _(v):
            for i in range(nt):
                v.wait_ge(slp[i % 2], 16 * (i // 2 + 1))
                if i >= 2:
                    v.wait_ge(stp[i % 2], 16 * (i // 2))
                for ch in range(4):
                    u = i * 4 + ch
                    v.wait_ge(sp, u + 1)
                    bank = psA if u % 2 == 0 else psB
                    v.scalar_tensor_tensor(
                        out=xob[i % 2][:, ch * 512:(ch + 1) * 512], in0=bank[:, 0:512],
                        scalar=1.0, in1=xb[i % 2][:, ch * 512:(ch + 1) * 512],
                        op0=ALU.mult, op1=ALU.add).then_inc(sv, 1)

        @blk.scalar
        def _(sc):
            for dm in range(1, 16, 2):
                sc.dma_start(out=ws[:, dm * 2048:(dm + 1) * 2048],
                             in_=Wb[ts(dm, 128), :]).then_inc(sw2, 16)
            for i in range(nt):
                sc.wait_ge(sv, i * 4 + 4)
                sc.dma_start(out=xnew[ts(i, 128), :], in_=xob[i % 2][:]
                             ).then_inc(stp[i % 2], 16)
            sc.wait_ge(st0, 16 * ((nt + 1) // 2))
            sc.wait_ge(st1, 16 * (nt // 2))

    return nc


def build_scat(NT=1024):
    nc = bass.Bass("TRN2", target_bir_lowering=False)
    nt = NT // 128
    xprev = nc.dram_tensor("xprev", [NT, D], F32, kind="ExternalInput").ap()
    lntk = nc.dram_tensor("lntk", [NT, D], BF16, kind="ExternalInput").ap()
    gated = nc.dram_tensor("gated", [SSEL, D], BF16, kind="ExternalInput").ap()
    acol = nc.dram_tensor("acol", [128, 8], U32, kind="ExternalInput").ap()
    xnew = nc.dram_tensor("xnew", [NT, D], F32, kind="ExternalOutput").ap()
    es = ExitStack()
    sb = lambda n, shp, dt: es.enter_context(nc.sbuf_tensor(n, shp, dt))
    sem = lambda n: es.enter_context(nc.semaphore(n))
    with es:
        xt = sb("xt", [128, D], F32)
        xt2 = sb("xt2", [128, D], F32)
        lt = sb("lt", [128, D], BF16)
        lt2 = sb("lt2", [128, D], BF16)
        gb = sb("gb", [128, D], BF16)
        gb2 = sb("gb2", [128, D], BF16)
        xo = sb("xo", [128, D], F32)
        xo2 = sb("xo2", [128, D], F32)
        acs = sb("acs", [128, 8], U32)
        af = sb("af", [128, 8], F32)
        mcol = sb("mcol", [128, 8], F32)
        sg0 = sem("sg0"); sg1 = sem("sg1")
        sl0 = sem("sl0"); sl1 = sem("sl1"); sla = sem("sla"); smz = sem("smz")
        sv = sem("sv"); sz = sem("sz")
        st0 = sem("st0"); st1 = sem("st1")
        blk = es.enter_context(nc.Block())
        sgp = [sg0, sg1]
        slp = [sl0, sl1]
        stp = [st0, st1]
        xb = [xt, xt2]
        lb = [lt, lt2]
        gbb = [gb, gb2]
        xob = [xo, xo2]

        @blk.gpsimd
        def _(g):
            g.dma_start(out=acs[:], in_=acol).then_inc(sla, 16)
            g.memset(gb[:], 0.0).then_inc(smz, 1)
            g.memset(gb2[:], 0.0).then_inc(smz, 1)
            g.wait_ge(sla, 16)
            g.wait_ge(smz, 2)
            for i in range(nt):
                if i >= 2:
                    g.wait_ge(sv, 3 * (i - 1))
                g.indirect_dma_start(
                    gbb[i % 2][:], None, gated,
                    bass.IndirectOffsetOnAxis(ap=acs[:, i:i + 1], axis=0),
                    bounds_check=SSEL - 1, oob_is_err=False).then_inc(sgp[i % 2], 16)

        @blk.sync
        def _(s_):
            for i in range(nt):
                if i >= 2:
                    s_.wait_ge(stp[i % 2], 16 * (i // 2))
                s_.dma_start(out=xb[i % 2][:], in_=xprev[ts(i, 128), :]
                             ).then_inc(slp[i % 2], 16)
                s_.dma_start(out=lb[i % 2][:], in_=lntk[ts(i, 128), :]
                             ).then_inc(slp[i % 2], 16)

        @blk.vector
        def _(v):
            v.wait_ge(sla, 16)
            v.tensor_copy(af[:], acs[:]).then_inc(sz, 1)
            v.wait_ge(sz, 1)
            v.tensor_scalar(mcol[:], af[:], float(SSEL) - 0.5, None, ALU.is_ge
                            ).then_inc(sz, 1)
            for i in range(nt):
                v.wait_ge(sz, 2)
                v.wait_ge(slp[i % 2], 32 * (i // 2 + 1))
                v.wait_ge(sgp[i % 2], 16 * (i // 2 + 1))
                if i >= 2:
                    v.wait_ge(stp[i % 2], 16 * (i // 2))
                v.scalar_tensor_tensor(
                    out=xob[i % 2][:], in0=lb[i % 2][:], scalar=mcol[:, i:i + 1],
                    in1=xb[i % 2][:], op0=ALU.mult, op1=ALU.add).then_inc(sv, 1)
                v.wait_ge(sv, 3 * i + 1)
                v.tensor_tensor(xob[i % 2][:], xob[i % 2][:], gbb[i % 2][:], ALU.add
                                ).then_inc(sv, 1)
                v.wait_ge(sv, 3 * i + 2)
                v.tensor_scalar(gbb[i % 2][:], gbb[i % 2][:], 0.0, None, ALU.mult
                                ).then_inc(sv, 1)

        @blk.scalar
        def _(sc):
            for i in range(nt):
                sc.wait_ge(sv, 3 * i + 2)
                sc.dma_start(out=xnew[ts(i, 128), :], in_=xob[i % 2][:]
                             ).then_inc(stp[i % 2], 16)
            sc.wait_ge(st0, 16 * ((nt + 1) // 2))
            sc.wait_ge(st1, 16 * (nt // 2))

    return nc


NBIS = 46


class _Sched:
    KINDS = {"vector": ("sv", 1), "tensor": ("sp", 1), "scalar": ("sa", 1),
             "gpsimd": ("sz", 1), "sync_dma": ("sl", 16),
             "scalar_dma": ("sa2", 16), "gpsimd_dma": ("sg", 16)}

    def __init__(self):
        self.steps = []
        self.cnt = {v[0]: 0 for v in self.KINDS.values()}

    def add(self, kind, fn, lag=None):
        semn, inc = self.KINDS[kind]
        waits = {}
        for k, v in self.cnt.items():
            if lag:
                for lk, ln in lag.items():
                    if self.KINDS[lk][0] == k:
                        v = v - ln * self.KINDS[lk][1]
            if v > 0:
                waits[k] = v
        self.steps.append((kind, fn, waits))
        self.cnt[semn] += inc

    def emit_one(self, eng, sems, kinds):
        lw = {}
        for kind, fn, waits in self.steps:
            if kind not in kinds:
                continue
            semn, inc = self.KINDS[kind]
            for k, v in waits.items():
                if lw.get(k, -1) < v:
                    eng.wait_ge(sems[k], v)
                    lw[k] = v
            inst = fn(eng)
            if inst is None:
                raise RuntimeError("step returned no instruction")
            inst.then_inc(sems[semn], inc)


def build_fsel():
    nc = bass.Bass("TRN2", target_bir_lowering=False)
    sco = nc.dram_tensor("sco", [128, 64], F32, kind="ExternalInput").ap()
    lnfull = nc.dram_tensor("lnfull", [T, D], BF16, kind="ExternalInput").ap()
    ln1c = nc.dram_tensor("ln1c", [128, 16], F32, kind="ExternalInput").ap()
    biasc = nc.dram_tensor("biasc", [128, 32], F32, kind="ExternalInput").ap()
    sbase = nc.dram_tensor("sbase", [1, 1], U32, kind="ExternalInput").ap()
    invfs = nc.dram_tensor("invfs", [1, 128], F32, kind="ExternalInput").ap()
    identf = nc.dram_tensor("identf", [128, 128], F32, kind="ExternalInput").ap()
    identb = nc.dram_tensor("identb", [128, 128], BF16, kind="ExternalInput").ap()
    rotb = nc.dram_tensor("rotb", [128, 128], BF16, kind="ExternalInput").ap()
    iota1k = nc.dram_tensor("iota1k", [128, 1024], F32, kind="ExternalInput").ap()
    trif = nc.dram_tensor("trif", [128, 128], F32, kind="ExternalInput").ap()
    tcolf = nc.dram_tensor("tcolf", [128, 1], F32, kind="ExternalInput").ap()
    onescf = nc.dram_tensor("onescf", [128, 1], F32, kind="ExternalInput").ap()
    onesf = nc.dram_tensor("onesf", [1, 128], F32, kind="ExternalInput").ap()

    qTs = nc.dram_tensor("qTs", [16 * 128, 128], BF16, kind="ExternalOutput").ap()
    kkTs = nc.dram_tensor("kkTs", [8 * 128, 128], BF16, kind="ExternalOutput").ap()
    vso = nc.dram_tensor("vso", [128, 1024], BF16, kind="ExternalOutput").ap()
    selo = nc.dram_tensor("selo", [128, D], BF16, kind="ExternalOutput").ap()
    posco = nc.dram_tensor("posco", [2, SSEL], F32, kind="ExternalOutput").ap()
    acolo = nc.dram_tensor("acolo", [128, 64], U32, kind="ExternalOutput").ap()

    fw3b = nc.dram_tensor("fw3b", [D, 4096], BF16, kind="ExternalInput").ap()
    posd = nc.dram_tensor("posd", [SSEL, 1], F32).ap()

    es = ExitStack()
    sb = lambda n, shp, dt: es.enter_context(nc.sbuf_tensor(n, shp, dt))
    ps = lambda n, shp, dt=F32: es.enter_context(nc.psum_tensor(n, shp, dt))
    sem = lambda n: es.enter_context(nc.semaphore(n))
    with es:
        scsb = sb("scsb", [128, 64], F32)
        selw = sb("selw", [128, 64], F32)
        mge = sb("mge", [128, 64], F32)
        mgt = sb("mgt", [128, 64], F32)
        meq = sb("meq", [128, 64], F32)
        Ceq = sb("Ceq", [128, 64], F32)
        Csel = sb("Csel", [128, 64], F32)
        afb = sb("afb", [128, 64], F32)
        acu = sb("acu", [128, 64], U32)
        srow = sb("srow", [1, 16], F32)
        crow1 = sb("crow1", [1, 64], F32)
        crow2 = sb("crow2", [1, 64], F32)
        exbc = sb("exbc", [128, 64], F32)
        midc = sb("midc", [128, 1], F32)
        cntp = sb("cntp", [128, 1], F32)
        needc = sb("needc", [128, 1], F32)
        iot = sb("iot", [128, 1024], F32)
        trs = sb("trs", [128, 128], F32)
        tcs = sb("tcs", [128, 1], F32)
        onc = sb("onc", [128, 1], F32)
        onr = sb("onr", [1, 128], F32)
        idf = sb("idf", [128, 128], F32)
        idb = sb("idb", [128, 128], BF16)
        rob = sb("rob", [128, 128], BF16)
        Pf_ = [sb("Pf0", [128, 1024], F32), sb("Pf1", [128, 1024], F32)]
        stf_ = [sb("stf0", [128, 2], F32), sb("stf1", [128, 2], F32)]
        poscosb = sb("poscosb", [2, 1024], F32)
        sbs = sb("sbs", [1, 1], U32)
        sbf1 = sb("sbf1", [1, 1], F32)
        slotf = sb("slotf", [128, 1], F32)
        slotu = sb("slotu", [128, 1], U32)
        posg = sb("posg", [128, 1], F32)
        myidx = sb("myidx", [128, 1], U32)
        posrow = sb("posrow", [1, 128], F32)
        selrow = sb("selrow", [128, D], BF16)
        sq = sb("sq", [128, D], F32)
        rms = sb("rms", [128, 8], F32)
        hrow = sb("hrow", [128, D], BF16)
        hT = sb("hT", [128, D], BF16)
        fws = sb("fws", [128, 65536], BF16)
        lnc1 = sb("lnc1", [128, 16], F32)
        bcs = sb("bcs", [128, 32], F32)
        qko = sb("qko", [128, 4096], BF16)
        invr = sb("invr", [1, 128], F32)
        angs = sb("angs", [128, 128], F32)
        u1 = sb("u1", [128, 128], F32)
        ri = sb("ri", [128, 128], I32)
        rf = sb("rf", [128, 128], F32)
        sini = sb("sini", [128, 128], F32)
        sinv = sb("sinv", [128, 128], F32)
        cosv = sb("cosv", [128, 128], F32)
        rt1 = sb("rt1", [128, 128], F32)
        rt2 = sb("rt2", [128, 128], F32)
        vstok = sb("vstok", [128, 1024], BF16)
        psF = ps("psF", [128, 512])
        psG = ps("psG", [128, 512])
        psH1 = ps("psH1", [128, 512])
        psH2 = ps("psH2", [128, 512])
        psR = ps("psR", [128, 512])
        ptr = ps("ptr", [128, 1024], BF16)
        SEMS = {n: sem(n) for n in ["sv", "sp", "sa", "sz", "sl", "sa2", "sg"]}
        blk = es.enter_context(nc.Block())

        S = _Sched()

        S.add("gpsimd_dma", lambda g: g.dma_start(out=sbs[:], in_=sbase))
        jobs = [(scsb, sco), (iot, iota1k), (trs, trif), (tcs, tcolf),
                (onc, onescf), (onr, onesf), (idf, identf), (idb, identb),
                (rob, rotb), (lnc1, ln1c), (bcs, biasc), (invr, invfs)]
        for dst, src in jobs:
            S.add("sync_dma", lambda s_, dst=dst, src=src: s_.dma_start(out=dst[:], in_=src))
        BIG = {k: 10 ** 6 for k in _Sched.KINDS}
        for dm in range(16):
            q = "sync_dma" if dm % 2 == 0 else "scalar_dma"
            S.add(q, lambda e, dm=dm: e.dma_start(
                out=fws[:, dm * 4096:(dm + 1) * 4096], in_=fw3b[ts(dm, 128), :]),
                lag=BIG)

        S.add("vector", lambda v: v.tensor_reduce(midc[:], scsb[:], AX.X, ALU.max,
                                                  apply_absolute_value=True))
        S.add("tensor", lambda t: t.transpose(psG[0:1, 0:128], midc[:], idf[:]))
        S.add("vector", lambda v: v.tensor_reduce(srow[:, 4:5], psG[0:1, 0:128],
                                                  AX.X, ALU.max))
        S.add("vector", lambda v: v.tensor_scalar(srow[:, 1:2], srow[:, 4:5], 1.0,
                                                  None, ALU.add))
        S.add("vector", lambda v: v.tensor_scalar(srow[:, 0:1], srow[:, 1:2], -1.0,
                                                  None, ALU.mult))
        for it in range(NBIS):
            S.add("vector", lambda v: v.tensor_tensor(srow[:, 4:5], srow[:, 0:1],
                                                      srow[:, 1:2], ALU.add))
            S.add("vector", lambda v: v.tensor_scalar(srow[:, 2:3], srow[:, 4:5], 0.5,
                                                      None, ALU.mult))
            S.add("tensor", lambda t: t.matmul(psF[:, 0:1], onr[:, :], srow[:, 2:3],
                                               start=True, stop=True))
            S.add("vector", lambda v: v.tensor_copy(midc[:], psF[:, 0:1]))
            S.add("vector", lambda v: v.tensor_scalar(selw[:], scsb[:], midc[:], 0.0,
                                                      ALU.is_ge, ALU.add,
                                                      accum_out=cntp[:]))
            S.add("tensor", lambda t: t.matmul(psG[0:1, 0:1], onc[:], cntp[:],
                                               start=True, stop=True))
            S.add("vector", lambda v: v.tensor_scalar(srow[:, 3:4], psG[0:1, 0:1],
                                                      1023.5, None, ALU.is_ge))
            S.add("vector", lambda v: v.tensor_tensor(srow[:, 5:6], srow[:, 2:3],
                                                      srow[:, 0:1], ALU.subtract))
            S.add("vector", lambda v: v.tensor_tensor(srow[:, 6:7], srow[:, 5:6],
                                                      srow[:, 3:4], ALU.mult))
            S.add("vector", lambda v: v.tensor_tensor(srow[:, 0:1], srow[:, 0:1],
                                                      srow[:, 6:7], ALU.add))
            S.add("vector", lambda v: v.tensor_tensor(srow[:, 7:8], srow[:, 1:2],
                                                      srow[:, 2:3], ALU.subtract))
            S.add("vector", lambda v: v.tensor_tensor(srow[:, 8:9], srow[:, 7:8],
                                                      srow[:, 3:4], ALU.mult))
            S.add("vector", lambda v: v.tensor_tensor(srow[:, 9:10], srow[:, 2:3],
                                                      srow[:, 8:9], ALU.add))
            S.add("vector", lambda v: v.tensor_copy(srow[:, 1:2], srow[:, 9:10]))

        S.add("tensor", lambda t: t.matmul(psF[:, 0:1], onr[:, :], srow[:, 0:1],
                                           start=True, stop=True))
        S.add("vector", lambda v: v.tensor_copy(midc[:], psF[:, 0:1]))
        S.add("vector", lambda v: v.tensor_scalar(mge[:], scsb[:], midc[:], None,
                                                  ALU.is_ge))
        S.add("tensor", lambda t: t.matmul(psF[:, 0:1], onr[:, :], srow[:, 1:2],
                                           start=True, stop=True))
        S.add("vector", lambda v: v.tensor_copy(midc[:], psF[:, 0:1]))
        S.add("vector", lambda v: v.tensor_scalar(mgt[:], scsb[:], midc[:], 0.0,
                                                  ALU.is_ge, ALU.add,
                                                  accum_out=cntp[:]))
        S.add("vector", lambda v: v.tensor_tensor(meq[:], mge[:], mgt[:], ALU.subtract))
        S.add("tensor", lambda t: t.matmul(psG[0:1, 0:1], onc[:], cntp[:],
                                           start=True, stop=True))
        S.add("vector", lambda v: v.tensor_scalar(srow[:, 10:11], psG[0:1, 0:1],
                                                  -1.0, 1024.0, ALU.mult, ALU.add))
        S.add("tensor", lambda t: t.matmul(psF[:, 0:1], onr[:, :], srow[:, 10:11],
                                           start=True, stop=True))
        S.add("vector", lambda v: v.tensor_copy(needc[:], psF[:, 0:1]))

        def cumsum_steps(src, dst):
            S.add("tensor", lambda t: t.matmul(psG[:, 0:64], trs[:], src[:],
                                               start=True, stop=True))
            S.add("tensor", lambda t: t.matmul(psF[0:1, 0:64], onc[:], src[:],
                                               start=True, stop=True))
            S.add("vector", lambda v: v.tensor_copy(crow1[:], psF[0:1, 0:64]))
            for k in [1, 2, 4, 8, 16, 32]:
                S.add("vector", lambda v, k=k: v.tensor_copy(crow2[:, 0:k], crow1[:, 0:k]))
                S.add("vector", lambda v, k=k: v.tensor_tensor(
                    crow2[:, k:64], crow1[:, k:64], crow1[:, 0:64 - k], ALU.add))
                S.add("vector", lambda v: v.tensor_copy(crow1[:], crow2[:]))
            S.add("vector", lambda v: v.tensor_tensor(crow2[:], crow1[:],
                                                      psF[0:1, 0:64], ALU.subtract))
            S.add("tensor", lambda t: t.matmul(psF[:, 0:64], onr[:, :], crow2[:],
                                               start=True, stop=True))
            S.add("vector", lambda v: v.tensor_copy(exbc[:], psF[:, 0:64]))
            S.add("vector", lambda v: v.tensor_tensor(dst[:], psG[:, 0:64], exbc[:],
                                                      ALU.add))

        cumsum_steps(meq, Ceq)
        S.add("vector", lambda v: v.tensor_scalar(Ceq[:], Ceq[:], needc[:], None,
                                                  ALU.is_le))
        S.add("vector", lambda v: v.tensor_tensor(Ceq[:], Ceq[:], meq[:], ALU.mult))
        S.add("vector", lambda v: v.tensor_tensor(selw[:], mgt[:], Ceq[:], ALU.add))
        cumsum_steps(selw, Csel)
        S.add("vector", lambda v: v.tensor_scalar(afb[:], Csel[:], -1.0, None, ALU.add))
        S.add("vector", lambda v: v.tensor_tensor(afb[:], afb[:], selw[:], ALU.mult))
        S.add("vector", lambda v: v.tensor_scalar(Csel[:], selw[:], -1e6, 1e6,
                                                  ALU.mult, ALU.add))
        S.add("vector", lambda v: v.tensor_tensor(afb[:], afb[:], Csel[:], ALU.add))
        S.add("vector", lambda v: v.tensor_copy(acu[:], afb[:]))
        S.add("scalar_dma", lambda sc: sc.dma_start(out=acolo, in_=acu[:]))

        for f in range(64):
            pf = Pf_[f % 2]
            stf = stf_[f % 2]
            LG = {"tensor": 2}
            S.add("vector", lambda v, f=f, stf=stf: v.tensor_scalar(
                stf[:, 0:1], tcs[:], float(f * 128), None, ALU.add), lag=LG)
            S.add("vector", lambda v, f=f, stf=stf: v.tensor_copy(
                stf[:, 1:2], scsb[:, f:f + 1]), lag=LG)
            S.add("vector", lambda v, f=f, pf=pf: v.tensor_scalar(
                pf[:], iot[:], afb[:, f:f + 1], 0.0, ALU.subtract, ALU.is_equal),
                lag=LG)
            S.add("tensor", lambda t, f=f, pf=pf, stf=stf: t.matmul(
                psH1[0:2, 0:512], stf[:], pf[:, 0:512],
                start=(f == 0), stop=(f == 63)))
            S.add("tensor", lambda t, f=f, pf=pf, stf=stf: t.matmul(
                psH2[0:2, 0:512], stf[:], pf[:, 512:1024],
                start=(f == 0), stop=(f == 63)))
        S.add("vector", lambda v: v.tensor_copy(poscosb[:, 0:512], psH1[0:2, 0:512]))
        S.add("vector", lambda v: v.tensor_copy(poscosb[:, 512:1024], psH2[0:2, 0:512]))
        S.add("scalar_dma", lambda sc: sc.dma_start(out=posco, in_=poscosb[:]))
        S.add("scalar_dma", lambda sc: sc.dma_start(
            out=posd.rearrange("a b -> b a"), in_=poscosb[0:1, :]))

        S.add("vector", lambda v: v.tensor_copy(sbf1[:], sbs[:]))
        S.add("tensor", lambda t: t.matmul(psF[:, 0:1], onr[:, :], sbf1[:],
                                           start=True, stop=True))
        S.add("vector", lambda v: v.tensor_copy(slotf[:], psF[:, 0:1]))
        S.add("vector", lambda v: v.tensor_tensor(slotf[:], slotf[:], tcs[:], ALU.add))
        S.add("vector", lambda v: v.tensor_copy(slotu[:], slotf[:]))
        S.add("gpsimd_dma", lambda g: g.indirect_dma_start(
            posg[:], None, posd, bass.IndirectOffsetOnAxis(ap=slotu[:], axis=0)))
        S.add("vector", lambda v: v.tensor_copy(myidx[:], posg[:]))
        S.add("gpsimd_dma", lambda g: g.indirect_dma_start(
            selrow[:], None, lnfull, bass.IndirectOffsetOnAxis(ap=myidx[:], axis=0)))
        S.add("tensor", lambda t: t.transpose(psF[0:1, 0:128], posg[:], idf[:]))
        S.add("vector", lambda v: v.tensor_copy(posrow[:], psF[0:1, 0:128]))
        S.add("scalar_dma", lambda sc: sc.dma_start(out=selo, in_=selrow[:]))

        S.add("vector", lambda v: v.tensor_tensor(sq[:], selrow[:], selrow[:], ALU.mult))
        S.add("vector", lambda v: v.tensor_reduce(rms[:, 0:1], sq[:], AX.X, ALU.add))
        S.add("vector", lambda v: v.tensor_scalar(rms[:, 1:2], rms[:, 0:1],
                                                  1.0 / 2048.0, 1e-6, ALU.mult, ALU.add))
        S.add("vector", lambda v: v.reciprocal(rms[:, 2:3], rms[:, 1:2]))
        S.add("scalar", lambda sc: sc.activation(rms[:, 3:4], rms[:, 2:3], AF.Sqrt))
        S.add("vector", lambda v: v.tensor_scalar(hrow[:], selrow[:], rms[:, 3:4],
                                                  None, ALU.mult))
        for g_ in range(2):
            bank = ptr
            def trg(t, g_=g_):
                for k in range(8):
                    dt = g_ * 8 + k
                    mm = t.transpose(ptr[:, k * 128:(k + 1) * 128],
                                     hrow[:, dt * 128:(dt + 1) * 128], idb[:])
                return mm
            S.add("tensor", trg)
            for k in range(8):
                dt = g_ * 8 + k
                S.add("vector", lambda v, dt=dt, k=k: v.tensor_scalar(
                    hT[:, dt * 128:(dt + 1) * 128], ptr[:, k * 128:(k + 1) * 128],
                    lnc1[:, dt:dt + 1], None, ALU.mult))

        for io in range(32):
            bank = psH1 if io % 2 == 0 else psH2
            def qmm(t, io=io, bank=bank):
                for dm in range(16):
                    mm = t.matmul(bank[:, 0:128],
                                  fws[:, (dm * 32 + io) * 128:(dm * 32 + io + 1) * 128],
                                  hT[:, dm * 128:(dm + 1) * 128],
                                  start=(dm == 0), stop=(dm == 15))
                return mm
            S.add("tensor", qmm, lag=({"vector": 1} if io >= 2 else None))
            S.add("vector", lambda v, io=io, bank=bank: v.tensor_scalar(
                qko[:, io * 128:(io + 1) * 128], bank[:, 0:128],
                bcs[:, io:io + 1], None, ALU.add))

        S.add("tensor", lambda t: t.matmul(psR[:, 0:128], invr[:, :], posrow[:],
                                           start=True, stop=True))
        S.add("vector", lambda v: v.tensor_copy(angs[:], psR[:, 0:128]))
        for nm, off in [("sin", 0.0), ("cos", 0.25)]:
            dstv = sinv if nm == "sin" else cosv
            S.add("vector", lambda v, off=off: v.tensor_scalar(
                u1[:], angs[:], 1.0 / TWOPI, off, ALU.mult, ALU.add))
            S.add("vector", lambda v: v.tensor_copy(ri[:], u1[:]))
            S.add("vector", lambda v: v.tensor_copy(rf[:], ri[:]))
            S.add("vector", lambda v: v.tensor_tensor(sini[:], u1[:], rf[:],
                                                      ALU.subtract))
            S.add("vector", lambda v: v.tensor_scalar(rf[:], sini[:], 0.5, None,
                                                      ALU.is_gt))
            S.add("vector", lambda v: v.tensor_tensor(sini[:], sini[:], rf[:],
                                                      ALU.subtract))
            S.add("vector", lambda v: v.tensor_scalar(rf[:], sini[:], -0.5, None,
                                                      ALU.is_lt))
            S.add("vector", lambda v: v.tensor_tensor(sini[:], sini[:], rf[:],
                                                      ALU.add))
            S.add("vector", lambda v: v.tensor_scalar(sini[:], sini[:], TWOPI,
                                                      None, ALU.mult))
            S.add("scalar", lambda sc, dstv=dstv: sc.activation(dstv[:], sini[:],
                                                                AF.Sin))
        for ti in range(24):
            S.add("tensor", lambda t, ti=ti: t.matmul(
                psR[:, 0:128], rob[:], qko[:, ti * 128:(ti + 1) * 128],
                start=True, stop=True))
            S.add("vector", lambda v, ti=ti: v.tensor_tensor(
                rt1[:], qko[:, ti * 128:(ti + 1) * 128], cosv[:], ALU.mult))
            S.add("vector", lambda v: v.tensor_tensor(rt2[:], psR[:, 0:128], sinv[:],
                                                      ALU.mult))
            S.add("vector", lambda v, ti=ti: v.tensor_tensor(
                qko[:, ti * 128:(ti + 1) * 128], rt1[:], rt2[:], ALU.add))

        def vtr(t):
            for k in range(8):
                mm = t.transpose(ptr[:, k * 128:(k + 1) * 128],
                                 qko[:, (24 + k) * 128:(25 + k) * 128], idb[:])
            return mm
        S.add("tensor", vtr)
        S.add("vector", lambda v: v.tensor_copy(vstok[:], ptr[:]))

        S.add("scalar_dma", lambda sc: sc.dma_start(
            out=qTs.rearrange("(dt p) s -> p dt s", p=128),
            in_=qko[:, 0:2048].rearrange("p (dt s) -> p dt s", s=128)))
        S.add("scalar_dma", lambda sc: sc.dma_start(
            out=kkTs.rearrange("(dt p) s -> p dt s", p=128),
            in_=qko[:, 2048:3072].rearrange("p (dt s) -> p dt s", s=128)))
        S.add("scalar_dma", lambda sc: sc.dma_start(out=vso, in_=vstok[:]))

        @blk.sync
        def _(s_):
            S.emit_one(s_, SEMS, ["sync_dma"])

        @blk.gpsimd
        def _(g):
            S.emit_one(g, SEMS, ["gpsimd", "gpsimd_dma"])

        @blk.vector
        def _(v):
            S.emit_one(v, SEMS, ["vector"])

        @blk.tensor
        def _(t):
            S.emit_one(t, SEMS, ["tensor"])

        @blk.scalar
        def _(sc):
            S.emit_one(sc, SEMS, ["scalar", "scalar_dma"])
            sc.wait_ge(SEMS["sa2"], S.cnt["sa2"])
            sc.wait_ge(SEMS["sg"], S.cnt["sg"])
            sc.wait_ge(SEMS["sv"], S.cnt["sv"])

    return nc

def build_fmain():
    nc = bass.Bass("TRN2", target_bir_lowering=False)
    qTs = nc.dram_tensor("qTs", [16 * 128, 128], BF16, kind="ExternalInput").ap()
    kkT = nc.dram_tensor("kkT", [8 * 128, SSEL], BF16, kind="ExternalInput").ap()
    vfull = nc.dram_tensor("vfull", [SSEL, 1024], BF16, kind="ExternalInput").ap()
    seltok = nc.dram_tensor("seltok", [128, D], BF16, kind="ExternalInput").ap()
    myscol = nc.dram_tensor("myscol", [128, 1], F32, kind="ExternalInput").ap()
    amask = nc.dram_tensor("amask", [128, SSEL], F32, kind="ExternalInput").ap()
    ln2bc = nc.dram_tensor("ln2bc", [128, D], F32, kind="ExternalInput").ap()
    identb = nc.dram_tensor("identb", [128, 128], BF16, kind="ExternalInput").ap()
    gato = nc.dram_tensor("gato", [128, D], BF16, kind="ExternalOutput").ap()

    fwob = nc.dram_tensor("fwob", [D, D], BF16, kind="ExternalInput").ap()
    fwgb = nc.dram_tensor("fwgb", [64, 128, D], BF16, kind="ExternalInput").ap()
    fwub = nc.dram_tensor("fwub", [64, 128, D], BF16, kind="ExternalInput").ap()
    fwdb = nc.dram_tensor("fwdb", [IMLP, D], BF16, kind="ExternalInput").ap()

    es = ExitStack()
    sb = lambda n, shp, dt: es.enter_context(nc.sbuf_tensor(n, shp, dt))
    ps = lambda n, shp, dt=F32: es.enter_context(nc.psum_tensor(n, shp, dt))
    sem = lambda n: es.enter_context(nc.semaphore(n))
    with es:
        qs = sb("qs", [128, 2048], BF16)
        ks = sb("ks", [128, 8192], BF16)
        vss = sb("vss", [128, 8192], BF16)
        Pb = sb("Pb", [128, 1024], BF16)
        Pe = sb("Pe", [128, 1024], F32)
        PT = sb("PT", [128, 1024], BF16)
        ctxT = sb("ctxT", [128, 2048], BF16)
        sel = sb("sel", [128, D], BF16)
        self32 = sb("self32", [128, D], F32)
        x1tok = sb("x1tok", [128, D], F32)
        sqtok = sb("sqtok", [128, D], F32)
        h2k = sb("h2k", [128, D], BF16)
        h2T = sb("h2T", [128, D], BF16)
        gatok = sb("gatok", [128, D], BF16)
        msk = sb("msk", [128, SSEL], F32)
        lnbc = sb("lnbc", [128, D], F32)
        wsl = [sb(f"wsl{i}", [128, 2048], BF16) for i in range(2)]
        wgl = [sb(f"wgl{i}", [128, 2048], BF16) for i in range(2)]
        wul = [sb(f"wul{i}", [128, 2048], BF16) for i in range(2)]
        wdl = [sb(f"wdl{i}", [128, 2048], BF16) for i in range(2)]
        g1 = sb("g1", [128, 128], F32)
        mt = sb("mt", [128, 128], BF16)
        sm2 = sb("sm2", [128, 16], F32)
        rstdc = sb("rstdc", [128, 1], F32)
        gcol = sb("gcol", [128, 1], F32)
        msc = sb("msc", [128, 1], F32)
        idb = sb("idb", [128, 128], BF16)
        psA = ps("psA", [128, 512])
        psB = ps("psB", [128, 512])
        psC = ps("psC", [128, 512])
        psD = ps("psD", [128, 512])
        psE = ps("psE", [128, 512])
        psFb = ps("psFb", [128, 512])
        ptr = ps("ptr", [128, 1024], BF16)
        BANKS = [psC, psD, psE, psFb]
        SEMS = {n: sem(n) for n in ["sv", "sp", "sa", "sz", "sl", "sa2", "sg"]}
        swg = [sem("swg0"), sem("swg1")]
        swu = [sem("swu0"), sem("swu1")]
        swd = [sem("swd0"), sem("swd1")]
        smp = sem("smp")
        smv = sem("smv")
        sma = sem("sma")
        sst = sem("sst")
        blk = es.enter_context(nc.Block())

        S = _Sched()

        jobs = [(idb, identb), (msk, amask), (sel, seltok), (msc, myscol),
                (lnbc, ln2bc)]
        for dst, src in jobs:
            S.add("sync_dma", lambda s_, dst=dst, src=src: s_.dma_start(out=dst[:], in_=src))
        S.add("sync_dma", lambda s_: s_.dma_start(
            out=qs.rearrange("p (dt s) -> p dt s", s=128),
            in_=qTs.rearrange("(dt p) s -> p dt s", p=128)))
        S.add("sync_dma", lambda s_: s_.dma_start(
            out=ks.rearrange("p (dt s) -> p dt s", s=SSEL),
            in_=kkT.rearrange("(dt p) s -> p dt s", p=128)))
        S.add("sync_dma", lambda s_: s_.dma_start(
            out=vss.rearrange("p (st dv) -> p st dv", dv=1024),
            in_=vfull.rearrange("(st p) dv -> p st dv", p=128)))

        for h in range(16):
            kv = h // 2
            S.add("tensor", lambda t, h=h, kv=kv: t.matmul(
                psA[:, 0:512], qs[:, h * 128:(h + 1) * 128],
                ks[:, kv * 1024:kv * 1024 + 512], start=True, stop=True),
                lag=({"vector": 8, "scalar": 1} if h else None))
            S.add("tensor", lambda t, h=h, kv=kv: t.matmul(
                psB[:, 0:512], qs[:, h * 128:(h + 1) * 128],
                ks[:, kv * 1024 + 512:(kv + 1) * 1024], start=True, stop=True),
                lag=({"vector": 7} if h else None))
            S.add("vector", lambda v: v.tensor_tensor(psA[:, 0:512], psA[:, 0:512],
                                                      msk[:, 0:512], ALU.add))
            S.add("vector", lambda v: v.tensor_tensor(psB[:, 0:512], psB[:, 0:512],
                                                      msk[:, 512:1024], ALU.add))
            S.add("vector", lambda v: v.reduce_max(sm2[:, 0:1], psA[:, 0:512], AX.X))
            S.add("vector", lambda v: v.reduce_max(sm2[:, 1:2], psB[:, 0:512], AX.X))
            S.add("vector", lambda v: v.tensor_tensor(sm2[:, 2:3], sm2[:, 0:1],
                                                      sm2[:, 1:2], ALU.max))
            S.add("vector", lambda v: v.tensor_scalar(sm2[:, 3:4], sm2[:, 2:3],
                                                      -ISQ, None, ALU.mult))
            S.add("scalar", lambda sc: sc.activation(Pe[:, 0:512], psA[:, 0:512],
                                                     AF.Exp, bias=sm2[:, 3:4], scale=ISQ))
            S.add("scalar", lambda sc: sc.activation(Pe[:, 512:1024], psB[:, 0:512],
                                                     AF.Exp, bias=sm2[:, 3:4], scale=ISQ))
            S.add("vector", lambda v: v.tensor_reduce(sm2[:, 4:5], Pe[:], AX.X, ALU.add))
            S.add("vector", lambda v: v.reciprocal(sm2[:, 5:6], sm2[:, 4:5]))
            S.add("vector", lambda v: v.tensor_scalar(Pb[:], Pe[:], sm2[:, 5:6],
                                                      None, ALU.mult))
            def trg(t):
                for k in range(8):
                    mm = t.transpose(ptr[:, k * 128:(k + 1) * 128],
                                     Pb[:, k * 128:(k + 1) * 128], idb[:])
                return mm
            S.add("tensor", trg)
            S.add("vector", lambda v: v.tensor_copy(PT[:], ptr[:]))
            def avg(t, kv=kv):
                for st_ in range(8):
                    mm = t.matmul(psC[:, 0:128],
                                  vss[:, st_ * 1024 + kv * 128:st_ * 1024 + (kv + 1) * 128],
                                  PT[:, st_ * 128:(st_ + 1) * 128],
                                  start=(st_ == 0), stop=(st_ == 7))
                return mm
            S.add("tensor", avg)
            S.add("vector", lambda v, h=h: v.tensor_copy(ctxT[:, h * 128:(h + 1) * 128],
                                                         psC[:, 0:128]))

        S.add("vector", lambda v: v.tensor_copy(self32[:], sel[:]))
        for dc in range(16):
            S.add("sync_dma", lambda s_, dc=dc: s_.dma_start(
                out=wsl[dc % 2][:], in_=fwob[ts(dc, 128), :]), lag={"tensor": 1})
            def womm(t, dc=dc):
                for ch in range(4):
                    mm = t.matmul(BANKS[ch][:, 0:512],
                                  ctxT[:, dc * 128:(dc + 1) * 128],
                                  wsl[dc % 2][:, ch * 512:(ch + 1) * 512],
                                  start=(dc == 0), stop=(dc == 15))
                return mm
            S.add("tensor", womm)
        for ch in range(4):
            S.add("vector", lambda v, ch=ch: v.scalar_tensor_tensor(
                out=x1tok[:, ch * 512:(ch + 1) * 512],
                in0=BANKS[ch][:, 0:512], scalar=1.0,
                in1=self32[:, ch * 512:(ch + 1) * 512], op0=ALU.mult, op1=ALU.add))

        S.add("vector", lambda v: v.tensor_tensor(sqtok[:], x1tok[:], x1tok[:], ALU.mult))
        S.add("vector", lambda v: v.tensor_reduce(sm2[:, 6:7], sqtok[:], AX.X, ALU.add))
        S.add("vector", lambda v: v.tensor_scalar(sm2[:, 7:8], sm2[:, 6:7],
                                                  1.0 / 2048.0, 1e-6, ALU.mult, ALU.add))
        S.add("vector", lambda v: v.reciprocal(sm2[:, 8:9], sm2[:, 7:8]))
        S.add("scalar", lambda sc: sc.activation(rstdc[:], sm2[:, 8:9], AF.Sqrt))
        S.add("vector", lambda v: v.tensor_scalar(sqtok[:], x1tok[:], rstdc[:, 0:1],
                                                  None, ALU.mult))
        S.add("vector", lambda v: v.tensor_tensor(h2k[:], sqtok[:], lnbc[:], ALU.mult))
        for g_ in range(2):
            def trg2(t, g_=g_):
                for k in range(8):
                    dt = g_ * 8 + k
                    mm = t.transpose(ptr[:, k * 128:(k + 1) * 128],
                                     h2k[:, dt * 128:(dt + 1) * 128], idb[:])
                return mm
            S.add("tensor", trg2)
            S.add("vector", lambda v, g_=g_: v.tensor_copy(
                h2T[:, g_ * 1024:(g_ + 1) * 1024], ptr[:]))
        S.add("scalar", lambda sc: sc.activation(gcol[:], msc[:], AF.Sigmoid))

        TOT = dict(S.cnt)

        def ld_gu(e, buf, wsrc, I, sems_):
            return e.dma_start(out=buf[I % 2][:], in_=wsrc[I]
                               ).then_inc(sems_[I % 2], 16)

        @blk.sync
        def _(s_):
            S.emit_one(s_, SEMS, ["sync_dma"])
            for I in range(64):
                if I >= 2:
                    s_.wait_ge(smp, 3 * (I - 1))
                ld_gu(s_, wgl, fwgb, I, swg)

        @blk.gpsimd
        def _(g):
            S.emit_one(g, SEMS, ["gpsimd", "gpsimd_dma"])
            for I in range(64):
                if I >= 2:
                    g.wait_ge(smp, 3 * (I - 1))
                g.dma_start(out=wdl[I % 2][:], in_=fwdb[ts(I, 128), :]
                            ).then_inc(swd[I % 2], 16)

        @blk.tensor
        def _(t):
            S.emit_one(t, SEMS, ["tensor"])
            for k in ["sv", "sa"]:
                t.wait_ge(SEMS[k], TOT[k])
            for I in range(64):
                for sw in (swg, swu, swd):
                    t.wait_ge(sw[I % 2], 16 * (I // 2 + 1))
                if I >= 1:
                    t.wait_ge(smv, 2 * I)
                for dm in range(16):
                    mm = t.matmul(psA[:, 0:128], wgl[I % 2][:, dm * 128:(dm + 1) * 128],
                                  h2T[:, dm * 128:(dm + 1) * 128],
                                  start=(dm == 0), stop=(dm == 15))
                mm.then_inc(smp, 1)
                for dm in range(16):
                    mm = t.matmul(psB[:, 0:128], wul[I % 2][:, dm * 128:(dm + 1) * 128],
                                  h2T[:, dm * 128:(dm + 1) * 128],
                                  start=(dm == 0), stop=(dm == 15))
                mm.then_inc(smp, 1)
                t.wait_ge(smv, 2 * I + 2)
                for ch in range(4):
                    mm = t.matmul(BANKS[ch][:, 0:512], mt[:],
                                  wdl[I % 2][:, ch * 512:(ch + 1) * 512],
                                  start=(I == 0), stop=(I == 63))
                mm.then_inc(smp, 1)

        @blk.vector
        def _(v):
            S.emit_one(v, SEMS, ["vector"])
            for k in ["sa", "sp"]:
                v.wait_ge(SEMS[k], TOT[k])
            for I in range(64):
                v.wait_ge(sma, I + 1)
                v.wait_ge(smp, 3 * I + 2)
                v.tensor_tensor(mt[:], psB[:, 0:128], g1[:], ALU.mult).then_inc(smv, 2)
            v.wait_ge(smp, 192)
            for ch in range(4):
                v.scalar_tensor_tensor(
                    out=sqtok[:, ch * 512:(ch + 1) * 512],
                    in0=BANKS[ch][:, 0:512], scalar=1.0,
                    in1=x1tok[:, ch * 512:(ch + 1) * 512], op0=ALU.mult, op1=ALU.add
                    ).then_inc(smv, 1)
            v.wait_ge(smv, 132)
            v.tensor_tensor(sqtok[:], sqtok[:], self32[:], ALU.subtract).then_inc(smv, 1)
            v.wait_ge(smv, 133)
            v.tensor_scalar(sqtok[:], sqtok[:], gcol[:, 0:1], None, ALU.mult
                            ).then_inc(smv, 1)
            v.wait_ge(smv, 134)
            v.tensor_tensor(gatok[:], sqtok[:], self32[:], ALU.add).then_inc(smv, 1)

        @blk.scalar
        def _(sc):
            S.emit_one(sc, SEMS, ["scalar", "scalar_dma"])
            sc.wait_ge(SEMS["sp"], TOT["sp"])
            ld_gu(sc, wul, fwub, 0, swu)
            ld_gu(sc, wul, fwub, 1, swu)
            for I in range(64):
                sc.wait_ge(smp, 3 * I + 1)
                sc.activation(g1[:], psA[:, 0:128], AF.Silu).then_inc(sma, 1)
                if I + 2 < 64:
                    sc.wait_ge(smp, 3 * I + 2)
                    ld_gu(sc, wul, fwub, I + 2, swu)
            sc.wait_ge(smv, 135)
            sc.dma_start(out=gato, in_=gatok[:]).then_inc(sst, 16)
            sc.wait_ge(sst, 16)
            sc.wait_ge(SEMS["sg"], S.cnt["sg"])
            sc.wait_ge(SEMS["sa2"], S.cnt["sa2"])

    return nc


def build_cast():
    nc = bass.Bass("TRN2", target_bir_lowering=False)
    cin = nc.dram_tensor("cin", [1088, 8192], F32, kind="ExternalInput").ap()
    cout = nc.dram_tensor("cout", [1088, 8192], BF16, kind="ExternalOutput").ap()
    es = ExitStack()
    with es:
        scp = es.enter_context(nc.semaphore("scp"))
        blk = es.enter_context(nc.Block())

        @blk.gpsimd
        def _(g):
            g.dma_start(out=cout, in_=cin).then_inc(scp, 16)
            g.wait_ge(scp, 16)
    return nc
